# version 23
# baseline (speedup 1.0000x reference)
import numpy as np
import concourse.bass as bass
import concourse.mybir as mybir
from concourse.bass_utils import run_bass_kernel_spmd

F32 = mybir.dt.float32
BF16 = mybir.dt.bfloat16
F8 = mybir.dt.float8e4
ALU = mybir.AluOpType
ACTF = mybir.ActivationFunctionType
AX = mybir.AxisListType

D = 4096
NH = 32
NFC = 32
SROWS = 1536
NTOK = 1024
OWN0 = 256
NTT = 12
NOT = 8
NG = 16
SCALE = 1.0 / float(np.sqrt(128.0))
EPS = float(np.float32(1e-5))
INV224 = float(np.float32(1.0) / np.float32(224.0))
CLAMP = float(np.float32(2.0) * np.float32(1e-12))
INVD = float(np.float32(1.0) / np.float32(D))

ENGINES = ("t", "v", "s", "g", "y")
DMA_SEMS = ("dgc", "dg0", "dg1", "dyc", "dy0", "dy1", "ds0", "ds1")


class B:
    def __init__(self):
        self.ops = {e: [] for e in ENGINES}
        self.cnt = {"t": 0, "v": 0, "s": 0, "g": 0, "dgc": 0, "dg0": 0, "dg1": 0,
                    "dyc": 0, "dy0": 0, "dy1": 0, "ds0": 0, "ds1": 0}
        self.pending = {e: {} for e in ENGINES}

    def emit(self, eng, fn, waits=(), inc=None):
        w = dict(self.pending[eng])
        self.pending[eng] = {}
        for sem, val in waits:
            if val and val > 0:
                w[sem] = max(w.get(sem, 0), val)
        if eng in ("v", "s", "g") and inc not in DMA_SEMS:
            prev = self.cnt[eng]
            if prev > 0:
                w[eng] = max(w.get(eng, 0), prev)
            inc = eng
        val = None
        if inc is not None:
            amt = 16 if inc in DMA_SEMS else 1
            self.cnt[inc] += amt
            val = self.cnt[inc]
        self.ops[eng].append((w, fn, inc, val))
        return val

    def barrier(self):
        for e in ENGINES:
            for k, v in self.cnt.items():
                if v > 0:
                    self.pending[e][k] = max(self.pending[e].get(k, 0), v)


PHASES = []


def build_nc():
    nc = bass.Bass("TRN2", target_bir_lowering=False)

    dr = {}
    def din(name, shape, dtype=F32):
        dr[name] = nc.dram_tensor(name, list(shape), dtype, kind="ExternalInput").ap()
    din("xs", (SROWS, D))
    din("wqt", (NH, 128, D))
    din("wkt", (128, D))
    din("wvt", (128, D))
    din("wot", (NH, 128, D))
    din("wft", (NG, 128, 8192))
    din("bias_t", (128, 32))
    din("hbd", (128, 128))
    din("ident", (128, 128))
    din("onesr", (1, 128))
    din("ropekc", (128, SROWS))
    din("ropeks", (128, SROWS))
    din("ropeqc", (128, NTOK))
    din("ropeqs", (128, NTOK))
    din("alphat", (66, 1024))
    din("rsmask", (128, 48))
    din("selw", (66, 256))
    y = nc.dram_tensor("y", [NTOK, D], F32, kind="ExternalOutput").ap()

    BASE = 16544
    A0, B0, C0, D0 = BASE, BASE + 98304, BASE + 163840, BASE + 196608
    LIMIT = 229376
    sb = {}
    def at(name, shape, dtype, off):
        t = nc.alloc_sbuf_tensor_at(name, list(shape), dtype, offset=off)
        sb[name] = t
        return t

    at("xnT", (128, NFC * SROWS), BF16, A0)
    at("attnT", (128, NH * NTOK), BF16, A0)
    at("sigT0", (128, 1536), BF16, A0 + 65536)
    at("sigT1", (128, 1536), BF16, A0 + 68608)
    at("cabc0", (128, 512), BF16, A0 + 71680)
    at("cabc1", (128, 512), BF16, A0 + 72704)
    at("alphat_sb", (66, 1024), F32, A0 + 73728)
    at("rsrp", (66, 512), F32, A0 + 93184)
    for _i in range(4):
        at(f"cacolp{_i}", (66, 256), BF16, A0 + 78848 + 512 * _i)
    at("tmpbl", (128, 256), BF16, A0 + 90112)
    at("tmpbl2", (128, 256), BF16, A0 + 90624)
    at("av_sb2", (128, 512), BF16, A0 + 91136)
    at("av_sb3", (128, 512), BF16, A0 + 92160)
    at("av_sb0", (128, 512), BF16, A0 + 95264)
    at("av_sb1", (128, 512), BF16, A0 + 96288)
    at("wo0", (128, 4096), BF16, A0 + 65536)
    at("wo1", (128, 4096), BF16, A0 + 90112)
    at("wf0", (128, 8192), BF16, A0 + 73728)
    at("wf1", (128, 8192), BF16, A0)
    at("wq8", (128, 8192), F8, A0 + 16384)
    at("swnat", (128, 2), F32, A0 + 24576)
    at("sws", (128, 2), F32, A0 + 24640)
    at("recw", (128, 2), F32, A0 + 24704)
    at("wtmp", (128, 1), F32, A0 + 24768)
    at("wrow", (1, 256), F32, A0 + 24832)
    at("recipw_bc", (128, 256), BF16, A0 + 25856)
    at("ub", (128, 512), BF16, A0 + 26880)
    at("ub1", (128, 512), BF16, A0 + 30208)
    at("wstage0", (128, 512), BF16, A0 + 32768)
    at("wstage1", (128, 512), BF16, A0 + 33792)
    at("runmax0", (128, 512), BF16, A0 + 34816)
    at("runmax1", (128, 512), BF16, A0 + 35840)
    at("swnat1", (128, 2), F32, A0 + 27904)
    at("sws1", (128, 2), F32, A0 + 27968)
    at("recw1", (128, 2), F32, A0 + 28032)
    at("wtmp1", (128, 1), F32, A0 + 28096)
    at("wrow1", (1, 256), F32, A0 + 28160)
    at("recipw_bc1", (128, 256), BF16, A0 + 29184)
    at("xin0", (128, 4096), F32, A0)
    at("xin1", (128, 4096), F32, A0 + 16384)
    at("outsb0", (128, 4096), F32, A0 + 32768)
    at("outsb1", (128, 4096), F32, A0 + 49152)
    at("QrT", (128, NH * NTOK), BF16, B0)
    at("finalT", (128, NH * NTOK), BF16, B0)
    at("xt0", (128, 4096), F32, B0)
    at("xt1", (128, 4096), F32, B0 + 16384)
    at("xnat0", (128, 4096), BF16, B0 + 32768)
    at("xnat1", (128, 4096), BF16, B0 + 40960)
    at("stats0", (128, 16), F32, B0 + 49152)
    at("stats1", (128, 16), F32, B0 + 49216)
    at("wv_sb", (128, 4096), BF16, B0 + 49280)
    at("xq8", (128, NFC * NTOK), F8, C0)
    at("rqc", (128, NTOK), BF16, C0)
    at("rqs", (128, NTOK), BF16, C0 + 2048)
    at("wqh0", (128, 4096), BF16, C0 + 4096)
    at("wqh1", (128, 4096), BF16, C0 + 12288)
    at("q_tmp", (128, 1024), BF16, C0 + 20480)
    at("tmp2", (128, 1024), BF16, C0 + 22528)
    at("wk_sb", (128, 4096), BF16, C0 + 24576)
    at("KrT", (128, SROWS), BF16, D0)
    at("Vn", (128, SROWS), BF16, D0 + 3072)
    at("rkc", (128, SROWS), BF16, D0 + 6144)
    at("rks", (128, SROWS), BF16, D0 + 9216)
    at("sxp_bc", (128, 1024), F32, D0 + 6144)
    at("recipx_bc", (128, 1024), F32, D0 + 10240)
    at("xrow_s", (1, 1024), F32, D0 + 6144)
    at("xrow_r", (1, 1024), F32, D0 + 10240)
    at("snrt", (128, 16), F32, D0 + 14336)
    at("xmaxn", (128, 8), F32, D0 + 14400)
    at("xtmp1", (128, 1), F32, D0 + 14432)
    at("ident_bf", (128, 128), BF16, D0 + 14464)
    at("hbd_bf", (128, 128), BF16, D0 + 14720)
    at("ident_f", (128, 128), F32, D0 + 14976)
    at("onesr_f", (1, 128), F32, D0 + 15488)
    at("onesr_bf", (1, 128), BF16, A0 + 81920)
    at("selw_bf", (66, 256), BF16, A0 + 80896)
    at("rsmask_sb", (128, 48), BF16, A0 + 82176)
    at("bias_sb", (128, 32), F32, D0 + 16000)
    assert D0 + 16128 + 32 <= LIMIT

    ps = {}
    ps["a"] = nc.alloc_psum_tensor("ps_a", [128, 1536], F32)
    for nm in ("b", "c", "d", "e", "f"):
        ps[nm] = nc.alloc_psum_tensor("ps_" + nm, [128, 512], F32)

    bld = B()
    bld.phases = []
    ev = {}
    T = lambda fn, waits=(), inc=None: bld.emit("t", fn, waits, inc)
    V = lambda fn, waits=(): bld.emit("v", fn, waits)
    S = lambda fn, waits=(): bld.emit("s", fn, waits)
    G = lambda fn, waits=(), sem="dgc": bld.emit("g", fn, waits, sem)
    GC = lambda fn, waits=(): bld.emit("g", fn, waits, None)
    Y = lambda fn, waits=(), sem="dyc": bld.emit("y", fn, waits, sem)
    SD = lambda fn, waits=(), sem="ds0": bld.emit("s", fn, waits, sem)

    xnT = sb["xnT"].ap().rearrange("p (f t) -> p f t", f=NFC)
    xq8v = sb["xq8"].ap().rearrange("p (f t) -> p f t", f=NFC)

    for dst, src in [("ident_bf", "ident"), ("hbd_bf", "hbd"), ("rkc", "ropekc"),
                     ("rks", "ropeks"), ("rqc", "ropeqc"), ("rqs", "ropeqs"),
                     ("wk_sb", "wkt"), ("wv_sb", "wvt")]:
        G(lambda g, dst=dst, src=src: g.dma_start(out=sb[dst].ap()[:, :], in_=dr[src]))
    cw0 = bld.cnt["dgc"]
    for dst, src in [("ident_f", "ident"), ("onesr_f", "onesr")]:
        Y(lambda yq, dst=dst, src=src: yq.dma_start(out=sb[dst].ap()[:, :], in_=dr[src]))
    cy0 = bld.cnt["dyc"]

    xts = [sb["xt0"], sb["xt1"]]
    xnats = [sb["xnat0"], sb["xnat1"]]
    statss = [sb["stats0"], sb["stats1"]]
    def p0_stats(tt):
        p = tt % 2
        xt, xnat, st = xts[p], xnats[p], statss[p]
        dsem = "dy0" if p == 0 else "dy1"
        ld = Y(lambda yq, xt=xt, tt=tt: yq.dma_start(
            out=xt.ap()[:, :], in_=dr["xs"][128 * tt:128 * (tt + 1), :]),
            waits=[("s", ev.get(("p0norm", tt - 2), 0))], sem=dsem)
        ev[("p0sum", tt)] = V(lambda v, xt=xt, st=st: v.reduce_sum(
            st.ap()[:, 0:1], xt.ap()[:, :], axis=AX.X),
            waits=[(dsem, ld), ("s", ev.get(("p0norm", tt - 2), 0))])
        ev[("p0sq", tt)] = S(lambda s, xt=xt, xnat=xnat, st=st: s.activation(
            xnat.ap()[:, :], xt.ap()[:, :], ACTF.Square, accum_out=st.ap()[:, 1:2]),
            waits=[(dsem, ld), ("t", ev.get(("p0t", tt - 2), 0)),
                   ("v", ev.get(("p0sum", tt), 0))])
        V(lambda v, st=st: v.tensor_scalar(st.ap()[:, 2:3], st.ap()[:, 0:1], INVD, None, op0=ALU.mult))
        V(lambda v, st=st: v.tensor_tensor(st.ap()[:, 3:4], st.ap()[:, 2:3], st.ap()[:, 2:3], op=ALU.mult))
        V(lambda v, st=st: v.tensor_scalar(st.ap()[:, 4:5], st.ap()[:, 1:2], INVD, EPS, op0=ALU.mult, op1=ALU.add),
          waits=[("s", ev[("p0sq", tt)])])
        ev[("p0var", tt)] = V(lambda v, st=st: v.tensor_tensor(
            st.ap()[:, 5:6], st.ap()[:, 4:5], st.ap()[:, 3:4], op=ALU.subtract))
        ev[("p0sqrt", tt)] = S(lambda s, st=st: s.activation(st.ap()[:, 6:7], st.ap()[:, 5:6], ACTF.Sqrt),
                               waits=[("v", ev[("p0var", tt)])])
        V(lambda v, st=st: v.reciprocal(st.ap()[:, 7:8], st.ap()[:, 6:7]),
          waits=[("s", ev[("p0sqrt", tt)])])
        ev[("p0rstd", tt)] = V(lambda v, st=st: v.tensor_scalar(
            st.ap()[:, 8:9], st.ap()[:, 2:3], st.ap()[:, 7:8], -1.0,
            op0=ALU.mult, op1=ALU.mult))
        ev[("p0norm", tt)] = S(lambda s, xt=xt, xnat=xnat, st=st: s.activation(
            xnat.ap()[:, :], xt.ap()[:, :], ACTF.Identity, bias=st.ap()[:, 8:9], scale=st.ap()[:, 7:8]),
            waits=[("v", ev[("p0rstd", tt)])])

    def p0_trans(tt):
        p = tt % 2
        xnat = xnats[p]
        for bq in range(8):
            gi = tt * 8 + bq
            pb = ps["b"] if gi % 2 == 0 else ps["c"]
            for j in range(4):
                fc = 4 * bq + j
                T(lambda t, xnat=xnat, fc=fc, pb=pb, j=j: t.matmul(
                    pb.ap()[:, 128 * j:128 * (j + 1)],
                    xnat.ap()[:, 128 * fc:128 * (fc + 1)],
                    sb["ident_bf"].ap()[:, :], start=True, stop=True),
                    waits=([("s", ev[("p0norm", tt)]), ("dgc", cw0),
                            (("v" if gi % 2 == 0 else "s"), ev.get(("p0ev", gi - 2), 0))] if j == 0 else ()),
                    inc=("t" if j == 3 else None))
            tmm = bld.cnt["t"]
            if bq % 2 == 0:
                ev[("p0ev", gi)] = V(lambda v, tt=tt, bq=bq, pb=pb: v.tensor_copy(
                    xnT[:, 4 * bq:4 * bq + 4, 128 * tt:128 * (tt + 1)],
                    pb.ap().rearrange("p (a b) -> p a b", a=4)[:, :, :]),
                    waits=[("t", tmm)])
            else:
                ev[("p0ev", gi)] = S(lambda s, tt=tt, bq=bq, pb=pb: s.copy(
                    xnT[:, 4 * bq:4 * bq + 4, 128 * tt:128 * (tt + 1)],
                    pb.ap().rearrange("p (a b) -> p a b", a=4)[:, :, :]),
                    waits=[("t", tmm)])
        ev[("p0t", tt)] = bld.cnt["t"]

    for tt in range(NTT + 1):
        if tt < NTT:
            p0_stats(tt)
        if tt >= 1:
            p0_trans(tt - 1)

    bld.barrier()

    bld.phases.append(("P0", dict(bld.cnt)))

    for ts3 in range(3):
        for fc in range(NFC):
            T(lambda t, ts3=ts3, fc=fc: t.matmul(
                ps["a"].ap()[:, 512 * ts3:512 * (ts3 + 1)],
                sb["wk_sb"].ap()[:, 128 * fc:128 * (fc + 1)],
                xnT[:, fc, 512 * ts3:512 * (ts3 + 1)],
                start=(fc == 0), stop=(fc == NFC - 1)),
                inc=("t" if fc == NFC - 1 else None))
        ev[("kt", ts3)] = bld.cnt["t"]
        off = 512 * (ts3 % 2)
        ev[("kev", ts3)] = S(lambda s, ts3=ts3, off=off: s.copy(
            sb["q_tmp"].ap()[:, off:off + 512],
            ps["a"].ap()[:, 512 * ts3:512 * (ts3 + 1)]),
            waits=[("t", ev[("kt", ts3)]), ("v", ev.get(("krope", ts3 - 2), 0))])
        qt = sb["q_tmp"].ap()
        V(lambda v, off=off, ts3=ts3: v.tensor_tensor(
            sb["tmp2"].ap()[0:64, 0:512], qt[64:128, off:off + 512],
            sb["rks"].ap()[64:128, 512 * ts3:512 * (ts3 + 1)], op=ALU.mult),
            waits=[("s", ev[("kev", ts3)])])
        V(lambda v, off=off, ts3=ts3: v.tensor_tensor(
            sb["tmp2"].ap()[64:128, 0:512], qt[0:64, off:off + 512],
            sb["rks"].ap()[0:64, 512 * ts3:512 * (ts3 + 1)], op=ALU.mult))
        V(lambda v, off=off, ts3=ts3: v.tensor_tensor(
            sb["KrT"].ap()[:, 512 * ts3:512 * (ts3 + 1)], qt[:, off:off + 512],
            sb["rkc"].ap()[:, 512 * ts3:512 * (ts3 + 1)], op=ALU.mult))
        ev[("krope", ts3)] = V(lambda v, ts3=ts3: v.tensor_tensor(
            sb["KrT"].ap()[:, 512 * ts3:512 * (ts3 + 1)],
            sb["KrT"].ap()[:, 512 * ts3:512 * (ts3 + 1)],
            sb["tmp2"].ap()[:, 0:512], op=ALU.add))
    for tg in range(3):
        pb = ps["b"] if tg % 2 == 0 else ps["c"]
        for fc in range(NFC):
            for tt4 in range(4):
                tt = 4 * tg + tt4
                T(lambda t, fc=fc, tt=tt, tt4=tt4, pb=pb: t.matmul(
                    pb.ap()[:, 128 * tt4:128 * (tt4 + 1)],
                    xnT[:, fc, 128 * tt:128 * (tt + 1)],
                    sb["wv_sb"].ap()[:, 128 * fc:128 * (fc + 1)],
                    start=(fc == 0 and tt4 == 0), stop=(fc == NFC - 1 and tt4 == 3)),
                    waits=([("v", ev.get(("vev", tg - 2), 0))] if (fc == 0 and tt4 == 0) else ()),
                    inc=("t" if (fc == NFC - 1 and tt4 == 3) else None))
        ev[("vt", tg)] = bld.cnt["t"]
        ev[("vev", tg)] = V(lambda v, tg=tg, pb=pb: v.tensor_copy(
            sb["Vn"].ap()[:, 512 * tg:512 * (tg + 1)], pb.ap()[:, :]),
            waits=[("t", ev[("vt", tg)])])

    bld.barrier()

    bld.phases.append(("KV", dict(bld.cnt)))

    wqhs = [sb["wqh0"], sb["wqh1"]]
    for h in range(NH):
        wq = wqhs[h % 2]
        qsem = "dg0" if h % 2 == 0 else "dg1"
        ld = G(lambda g, h=h, wq=wq: g.dma_start(out=wq.ap()[:, :], in_=dr["wqt"][h]),
               waits=[("t", ev.get(("qt", h - 2, 1), 0))], sem=qsem)
        for ts in range(2):
            slot = (2 * h + ts) % 4
            pq, off = [(ps["a"], 0), (ps["a"], 512), (ps["b"], 0), (ps["c"], 0)][slot]
            for fc in range(NFC):
                T(lambda t, wq=wq, fc=fc, ts=ts, pq=pq, off=off: t.matmul(
                    pq.ap()[:, off:off + 512],
                    wq.ap()[:, 128 * fc:128 * (fc + 1)],
                    xnT[:, fc, OWN0 + 512 * ts:OWN0 + 512 * (ts + 1)],
                    start=(fc == 0), stop=(fc == NFC - 1)),
                    waits=([(qsem, ld), ("s", ev.get(("qev", h - 2, ts), 0))] if fc == 0 else ()),
                    inc=("t" if fc == NFC - 1 else None))
            ev[("qt", h, ts)] = bld.cnt["t"]
            ev[("qev", h, ts)] = S(lambda s, ts=ts, pq=pq, off=off: s.copy(
                sb["q_tmp"].ap()[:, 512 * ts:512 * (ts + 1)], pq.ap()[:, off:off + 512]),
                waits=[("t", ev[("qt", h, ts)]), ("v", ev.get(("qrope", h - 1), 0))])
        qt = sb["q_tmp"].ap()
        dstq = sb["QrT"].ap()[:, NTOK * h:NTOK * (h + 1)]
        V(lambda v, qt=qt: v.tensor_tensor(
            sb["tmp2"].ap()[0:64, :], qt[64:128, :], sb["rqs"].ap()[64:128, :], op=ALU.mult),
            waits=[("s", ev[("qev", h, 1)])])
        V(lambda v, qt=qt: v.tensor_tensor(
            sb["tmp2"].ap()[64:128, :], qt[0:64, :], sb["rqs"].ap()[0:64, :], op=ALU.mult))
        V(lambda v, qt=qt, dstq=dstq: v.tensor_tensor(dstq, qt, sb["rqc"].ap()[:, :], op=ALU.mult))
        ev[("qrope", h)] = V(lambda v, dstq=dstq: v.tensor_tensor(
            dstq, dstq, sb["tmp2"].ap()[:, :], op=ALU.add))

    bld.barrier()

    bld.phases.append(("Q", dict(bld.cnt)))

    for tt in range(NOT):
        for bq in range(8):
            gi = tt * 8 + bq
            pb = ps["b"] if gi % 2 == 0 else ps["c"]
            for j in range(4):
                fc = 4 * bq + j
                T(lambda t, fc=fc, tt=tt, pb=pb, j=j: t.matmul(
                    pb.ap()[:, 128 * j:128 * (j + 1)],
                    xnT[:, fc, OWN0 + 128 * tt:OWN0 + 128 * (tt + 1)],
                    sb["hbd_bf"].ap()[:, :], start=True, stop=True),
                    waits=([("v", ev.get(("xsc", gi - 2), 0))] if j == 0 else ()),
                    inc=("t" if j == 3 else None))
            tmm = bld.cnt["t"]
            if bq == 0:
                V(lambda v, tt=tt, pb=pb: v.reduce_max(
                    sb["xmaxn"].ap()[:, tt:tt + 1],
                    pb.ap().rearrange("p (a b) -> p a b", a=4)[:, :, :],
                    axis=AX.XY, apply_absolute_value=True), waits=[("t", tmm)])
            else:
                V(lambda v, pb=pb: v.reduce_max(
                    sb["xtmp1"].ap()[:, 0:1],
                    pb.ap().rearrange("p (a b) -> p a b", a=4)[:, :, :],
                    axis=AX.XY, apply_absolute_value=True), waits=[("t", tmm)])
                V(lambda v, tt=tt: v.tensor_tensor(
                    sb["xmaxn"].ap()[:, tt:tt + 1], sb["xmaxn"].ap()[:, tt:tt + 1],
                    sb["xtmp1"].ap()[:, 0:1], op=ALU.max))
            ev[("xsc", gi)] = bld.cnt["v"]
        V(lambda v, tt=tt: v.tensor_scalar(
            sb["snrt"].ap()[:, 2 * tt:2 * tt + 1], sb["xmaxn"].ap()[:, tt:tt + 1],
            INV224, CLAMP, op0=ALU.mult, op1=ALU.max))
        ev[("xsn", tt)] = V(lambda v, tt=tt: v.reciprocal(
            sb["snrt"].ap()[:, 2 * tt + 1:2 * tt + 2], sb["snrt"].ap()[:, 2 * tt:2 * tt + 1]))
    for tt in range(NOT):
        T(lambda t, tt=tt: t.matmul(
            ps["a"].ap()[0:1, 128 * tt:128 * (tt + 1)],
            sb["snrt"].ap()[:, 2 * tt:2 * tt + 1],
            sb["ident_f"].ap()[:, :], start=True, stop=True),
            waits=([("v", ev[("xsn", NOT - 1)]), ("dyc", cy0),
                    ("v", ev[("xsc", NOT * 8 - 1)])] if tt == 0 else ()))
        prow = ps["b"] if tt < 4 else ps["c"]
        T(lambda t, tt=tt, prow=prow: t.matmul(
            prow.ap()[0:1, 128 * (tt % 4):128 * (tt % 4) + 128],
            sb["snrt"].ap()[:, 2 * tt + 1:2 * tt + 2],
            sb["ident_f"].ap()[:, :], start=True, stop=True),
            inc=("t" if tt == NOT - 1 else None))
    ev["xrt"] = bld.cnt["t"]
    ev["xrow_s_ev"] = V(lambda v: v.tensor_copy(sb["xrow_s"].ap()[0:1, :], ps["a"].ap()[0:1, 0:1024]),
                        waits=[("t", ev["xrt"])])
    V(lambda v: v.tensor_copy(sb["xrow_r"].ap()[0:1, 0:512], ps["b"].ap()[0:1, 0:512]))
    ev["xrow_r_ev"] = V(lambda v: v.tensor_copy(sb["xrow_r"].ap()[0:1, 512:1024], ps["c"].ap()[0:1, 0:512]))
    for c2 in range(2):
        pd = ps["d"] if c2 == 0 else ps["e"]
        T(lambda t, c2=c2, pd=pd: t.matmul(
            pd.ap()[:, :], sb["onesr_f"].ap()[0:1, :],
            sb["xrow_s"].ap()[0:1, 512 * c2:512 * (c2 + 1)], start=True, stop=True),
            waits=[("v", ev["xrow_s_ev"])])
        pb = ps["b"] if c2 == 0 else ps["c"]
        T(lambda t, c2=c2, pb=pb: t.matmul(
            pb.ap()[:, :], sb["onesr_f"].ap()[0:1, :],
            sb["xrow_r"].ap()[0:1, 512 * c2:512 * (c2 + 1)], start=True, stop=True),
            waits=[("v", ev["xrow_r_ev"])],
            inc=("t" if c2 == 1 else None))
    xbc_t = bld.cnt["t"]
    V(lambda v: v.tensor_copy(sb["sxp_bc"].ap()[:, 0:512], ps["d"].ap()[:, :]), waits=[("t", xbc_t)])
    V(lambda v: v.tensor_copy(sb["sxp_bc"].ap()[:, 512:1024], ps["e"].ap()[:, :]))
    V(lambda v: v.tensor_copy(sb["recipx_bc"].ap()[:, 0:512], ps["b"].ap()[:, :]))
    ev["xbc_ev"] = V(lambda v: v.tensor_copy(sb["recipx_bc"].ap()[:, 512:1024], ps["c"].ap()[:, :]))
    for fc in range(NFC):
        for ts in range(2):
            k = 2 * fc + ts
            pd = ps["d"] if k % 2 == 0 else ps["e"]
            tmm = T(lambda t, fc=fc, ts=ts, pd=pd: t.matmul(
                pd.ap()[:, :], sb["hbd_bf"].ap()[:, :],
                xnT[:, fc, OWN0 + 512 * ts:OWN0 + 512 * (ts + 1)], start=True, stop=True),
                waits=[("v", ev.get(("xq", k - 2), ev["xbc_ev"]))], inc="t")
            V(lambda v, fc=fc, ts=ts, pd=pd: v.tensor_tensor(
                xq8v[:, fc, 512 * ts:512 * (ts + 1)], pd.ap()[:, :],
                sb["recipx_bc"].ap()[:, 512 * ts:512 * (ts + 1)], op=ALU.mult),
                waits=[("t", tmm)])
            ev[("xq", k)] = bld.cnt["v"]

    latew = [("t", bld.cnt["t"]), ("v", bld.cnt["v"])]
    for dst, src in [("onesr_bf", "onesr"), ("rsmask_sb", "rsmask"), ("selw_bf", "selw")]:
        G(lambda g, dst=dst, src=src: g.dma_start(out=sb[dst].ap()[:, :], in_=dr[src]),
          waits=latew)
    for dst, src in [("alphat_sb", "alphat"), ("bias_sb", "bias_t")]:
        Y(lambda yq, dst=dst, src=src: yq.dma_start(out=sb[dst].ap()[:, :], in_=dr[src]),
          waits=latew)

    bld.barrier()

    bld.phases.append(("XQ", dict(bld.cnt)))

    sigTs = [sb["sigT0"], sb["sigT1"]]
    cabcs = [sb["cabc0"], sb["cabc1"]]
    cacolps = [sb[f"cacolp{_i}"] for _i in range(4)]
    av_sbs = [sb["av_sb0"], sb["av_sb1"], sb["av_sb2"], sb["av_sb3"]]

    def att_scores(hs):
        rr, h = hs // 32, hs % 32
        L0 = 256 * rr
        for j in range(6):
            T(lambda t, h=h, j=j, L0=L0, rr=rr: t.matmul(
                ps["a"].ap()[:, 256 * j:256 * (j + 1)],
                sb["KrT"].ap()[:, L0 + 128 * j:L0 + 128 * (j + 1)],
                sb["QrT"].ap()[:, NTOK * h + 256 * rr:NTOK * h + 256 * (rr + 1)],
                start=True, stop=True),
                waits=([("s", ev.get(("sig", hs - 1), 0))] if j == 0 else ()),
                inc=("t" if j == 5 else None))
        ev[("sct", hs)] = bld.cnt["t"]
        sigT = sigTs[hs % 2].ap()
        ev[("sig", hs)] = S(lambda s, sigT=sigT: s.activation(
            sigT[:, :], ps["a"].ap()[:, :], ACTF.Sigmoid),
            waits=[("t", ev[("sct", hs)]), ("t", ev.get(("avt", hs - 2), 0))])

    def att_cons1(hs):
        rr, h = hs // 32, hs % 32
        L0 = 256 * rr
        par = hs % 2
        sigT = sigTs[par].ap()
        b3, m3 = hs // 3, hs % 3
        prs = ps["d"] if b3 % 2 == 0 else ps["f"]
        base = 32 * m3
        for j in range(6):
            T(lambda t, sigT=sigT, j=j, rr=rr, prs=prs, base=base: t.matmul(
                prs.ap()[base:base + 2, 0:256],
                sb["rsmask_sb"].ap()[:, 12 * rr + 2 * j:12 * rr + 2 * j + 2],
                sigT[:, 256 * j:256 * (j + 1)],
                start=(j == 0), stop=(j == 5)),
                waits=([("s", ev[("sig", hs)]),
                        ("v", ev.get(("recb", b3 - 2), 0))] if j == 0 else ()),
                inc=("t" if j == 5 else None))
        ev[("rst", hs)] = bld.cnt["t"]
        pav = ps["b"] if par == 0 else ps["c"]
        for j in range(6):
            ops = []
            if j <= 3:
                ops.append((0, j == 0, False))
            if j >= 2:
                ops.append((1, False, j == 5))
            for (w2, st_, sp_) in ops:
                T(lambda t, j=j, w2=w2, st_=st_, sp_=sp_, sigT=sigT, pav=pav, L0=L0: t.matmul(
                    pav.ap()[:, 256 * w2:256 * (w2 + 1)],
                    sb["Vn"].ap()[:, L0 + 128 * j:L0 + 128 * (j + 1)],
                    sigT[:, 256 * j:256 * (j + 1)],
                    start=st_, stop=sp_),
                    waits=([("s", ev.get(("avec", hs - 2), 0))] if (j == 0 and w2 == 0) else ()),
                    inc=("t" if (w2 == 1 and sp_) else None))
        ev[("avt", hs)] = bld.cnt["t"]
        ev[("avec", hs)] = S(lambda s, pav=pav, hs4=hs % 4: s.copy(
            av_sbs[hs4].ap()[:, :], pav.ap()[:, :]),
            waits=[("t", ev[("avt", hs)]), ("v", ev.get(("bl", hs - 4), 0))])
        if m3 == 2 or hs == 127:
            ro = 256 * (b3 % 2)
            V(lambda v, prs=prs, ro=ro: v.reciprocal(
                sb["rsrp"].ap()[0:66, ro:ro + 256], prs.ap()[0:66, 0:256]),
                waits=[("t", ev[("rst", hs)])])
            ev[("recb", b3)] = bld.cnt["v"]
            for hh in range(3 * b3, min(3 * b3 + 3, 128)):
                bh = 32 * (hh % 3)
                rrh = hh // 32
                ev[("ca", hh)] = V(lambda v, rrh=rrh, h4=hh % 4, ro=ro, bh=bh:
                    v.tensor_tensor(
                        cacolps[h4].ap()[bh:bh + 2, :],
                        sb["rsrp"].ap()[bh:bh + 2, ro:ro + 256],
                        sb["alphat_sb"].ap()[bh:bh + 2, 256 * rrh:256 * (rrh + 1)],
                        op=ALU.mult),
                    waits=[("t", ev.get(("bctt", hh - 4), 0))])

    def att_cons2_pe(hs):
        cac = cacolps[hs % 4].ap()
        bh = 32 * (hs % 3)
        for w2 in range(2):
            T(lambda t, w2=w2, cac=cac, bh=bh: t.matmul(
                ps["e"].ap()[:, 256 * w2:256 * (w2 + 1)],
                sb["selw_bf"].ap()[bh:bh + 2, 128 * w2:128 * (w2 + 1)],
                cac[bh:bh + 2, 0:256],
                start=True, stop=True),
                waits=([("v", ev[("ca", hs)]), ("s", ev.get(("cab", hs - 1), 0))]
                       if w2 == 0 else ()),
                inc=("t" if w2 == 1 else None))
        ev[("bctt", hs)] = bld.cnt["t"]

    def att_cons2(hs):
        rr, h = hs // 32, hs % 32
        par = hs % 2
        cab = cabcs[par].ap()
        ev[("cab", hs)] = S(lambda s, cab=cab: s.copy(cab[:, :], ps["e"].ap()[:, :]),
                            waits=[("t", ev[("bctt", hs)]), ("v", ev.get(("bl", hs - 2), 0))])
        avs = av_sbs[hs % 4].ap()
        dsta = sb["attnT"].ap()[:, NTOK * h + 256 * rr:NTOK * h + 256 * (rr + 1)]
        V(lambda v, avs=avs, cab=cab: v.tensor_tensor(
            sb["tmpbl"].ap()[:, :], avs[:, 0:256], cab[:, 0:256], op=ALU.mult),
            waits=[("s", ev[("cab", hs)])])
        V(lambda v, avs=avs, cab=cab: v.tensor_tensor(
            sb["tmpbl2"].ap()[:, :], avs[:, 256:512], cab[:, 256:512], op=ALU.mult))
        V(lambda v, dsta=dsta: v.tensor_tensor(
            dsta, sb["tmpbl"].ap()[:, :], sb["tmpbl2"].ap()[:, :], op=ALU.add))
        ev[("bl", hs)] = bld.cnt["v"]

    for hs in range(132):
        if 4 <= hs:
            att_cons2_pe(hs - 4)
        if hs < 128:
            att_scores(hs)
        if 4 <= hs:
            att_cons2(hs - 4)
        if 1 <= hs and hs - 1 < 128:
            att_cons1(hs - 1)

    bld.barrier()

    bld.phases.append(("ATT", dict(bld.cnt)))

    wos = [sb["wo0"], sb["wo1"]]
    for fo in range(NH):
        wo = wos[fo % 2]
        osem = "dg0" if fo % 2 == 0 else "dg1"
        ld = G(lambda g, fo=fo, wo=wo: g.dma_start(out=wo.ap()[:, :], in_=dr["wot"][fo]),
               waits=[("t", ev.get(("wot", fo - 2, 1), 0))], sem=osem)
        for ts in range(2):
            slot = (2 * fo + ts) % 4
            pq, off = [(ps["a"], 0), (ps["a"], 512), (ps["b"], 0), (ps["c"], 0)][slot]
            for fc in range(NFC):
                T(lambda t, wo=wo, fc=fc, ts=ts, pq=pq, off=off: t.matmul(
                    pq.ap()[:, off:off + 512],
                    wo.ap()[:, 128 * fc:128 * (fc + 1)],
                    sb["attnT"].ap()[:, NTOK * fc + 512 * ts:NTOK * fc + 512 * (ts + 1)],
                    start=(fc == 0), stop=(fc == NFC - 1)),
                    waits=([(osem, ld), ("s", ev.get(("woev", fo - 2, ts), 0))] if fc == 0 else ()),
                    inc=("t" if fc == NFC - 1 else None))
            ev[("wot", fo, ts)] = bld.cnt["t"]
            ev[("woev", fo, ts)] = S(lambda s, fo=fo, ts=ts, pq=pq, off=off: s.activation(
                sb["finalT"].ap()[:, NTOK * fo + 512 * ts:NTOK * fo + 512 * (ts + 1)],
                pq.ap()[:, off:off + 512], ACTF.Identity, bias=sb["bias_sb"].ap()[:, fo:fo + 1]),
                waits=[("t", ev[("wot", fo, ts)])])
    wfs = [sb["wf0"], sb["wf1"]]
    ev[("wfld", 0)] = G(lambda g: g.dma_start(out=sb["wf0"].ap()[:, :], in_=dr["wft"][0]),
                        waits=[("dg0", bld.cnt["dg0"])], sem="dg0")

    bld.barrier()

    bld.phases.append(("WO", dict(bld.cnt)))

    swnats = [sb["swnat"], sb["swnat1"]]
    swss = [sb["sws"], sb["sws1"]]
    recws = [sb["recw"], sb["recw1"]]
    wtmps = [sb["wtmp"], sb["wtmp1"]]
    wrows = [sb["wrow"], sb["wrow1"]]
    recipws = [sb["recipw_bc"], sb["recipw_bc1"]]
    main_ps = [ps["b"].ap()[:, :], ps["c"].ap()[:, :], ps["f"].ap()[:, :],
               ps["a"].ap()[:, 0:512]]

    def scales_batch(g_, k):
        p = g_ % 2
        wf = wfs[p]
        ldsem = "dg0" if p == 0 else "dg1"
        fo2, bq = k // 8, k % 8
        slot = 1 + k % 2
        for j in range(4):
            fc = 4 * bq + j
            T(lambda t, wf=wf, fc=fc, fo2=fo2, slot=slot, j=j: t.matmul(
                ps["a"].ap()[:, 512 * slot + 128 * j:512 * slot + 128 * (j + 1)],
                wf.ap()[:, 256 * fc + 128 * fo2:256 * fc + 128 * (fo2 + 1)],
                sb["hbd_bf"].ap()[:, :], start=True, stop=True),
                waits=([(ldsem, ev[("wfld", g_)]),
                        ("s", ev[("fstg", g_, k - 2)] if k >= 2 else
                              ev.get(("fstg", g_ - 1, 14 + k), 0))]
                       if j == 0 else ()),
                inc=("t" if j == 3 else None))
        tmm = bld.cnt["t"]
        stg = sb["wstage0"] if k % 2 == 0 else sb["wstage1"]
        rmx = sb["runmax0"] if fo2 == 0 else sb["runmax1"]
        ev[("fstg", g_, k)] = S(lambda s, stg=stg, slot=slot: s.activation(
            stg.ap()[:, :], ps["a"].ap()[:, 512 * slot:512 * (slot + 1)], ACTF.Abs),
            waits=[("t", tmm), ("v", ev.get(("fsc", g_, k - 2), 0))])
        if bq == 0:
            V(lambda v, stg=stg, rmx=rmx: v.tensor_copy(rmx.ap()[:, :], stg.ap()[:, :]),
                waits=[("s", ev[("fstg", g_, k)])])
        else:
            V(lambda v, stg=stg, rmx=rmx: v.tensor_tensor(
                rmx.ap()[:, :], rmx.ap()[:, :], stg.ap()[:, :], op=ALU.max),
                waits=[("s", ev[("fstg", g_, k)])])
        ev[("fsc", g_, k)] = bld.cnt["v"]
        if bq == 7:
            V(lambda v, fo2=fo2, p=p, rmx=rmx: v.reduce_max(
                swnats[p].ap()[:, fo2:fo2 + 1], rmx.ap()[:, :], axis=AX.X))
            V(lambda v, fo2=fo2, p=p: v.tensor_scalar(
                swss[p].ap()[:, fo2:fo2 + 1], swnats[p].ap()[:, fo2:fo2 + 1],
                INV224, CLAMP, op0=ALU.mult, op1=ALU.max))
            ev[("fsn", g_, fo2)] = V(lambda v, fo2=fo2, p=p: v.reciprocal(
                recws[p].ap()[:, fo2:fo2 + 1], swss[p].ap()[:, fo2:fo2 + 1]))
        if k == 15:
            ev[("fsct", g_)] = bld.cnt["t"]

    def emit_rows(g_):
        p = g_ % 2
        for fo2 in range(2):
            T(lambda t, fo2=fo2, p=p: t.matmul(
                ps["d"].ap()[0:1, 128 * fo2:128 * (fo2 + 1)],
                recws[p].ap()[:, fo2:fo2 + 1],
                sb["ident_f"].ap()[:, :], start=True, stop=True),
                waits=[("v", ev[("fsn", g_, fo2)]),
                       ("s", ev.get(("fqS", g_ - 1, NFC - 1), 0)),
                       ("v", ev.get(("fbce", g_ - 1), 0))],
                inc=("t" if fo2 == 1 else None))
        ev[("frt", g_)] = bld.cnt["t"]
        ev[("frow", g_)] = V(lambda v, p=p: v.tensor_copy(
            wrows[p].ap()[0:1, :], ps["d"].ap()[0:1, 0:256]),
            waits=[("t", ev[("frt", g_)])])
        ev[("fbc", g_)] = T(lambda t, p=p: t.matmul(
            ps["d"].ap()[:, 256:512], sb["onesr_f"].ap()[0:1, :],
            wrows[p].ap()[0:1, :], start=True, stop=True),
            waits=[("v", ev[("frow", g_)])], inc="t")
        ev[("fbce", g_)] = V(lambda v, p=p: v.tensor_copy(
            recipws[p].ap()[:, :], ps["d"].ap()[:, 256:512]),
            waits=[("t", ev[("fbc", g_)])])

    ubs = [sb["ub"], sb["ub1"]]
    def emit_epi(g_, ks):
        p = g_ % 2
        for k in ks:
            fo2, ts = k // 2, k % 2
            fo_idx = 2 * g_ + fo2
            ubp = ubs[k % 2]
            stt = V(lambda v, k=k, fo2=fo2, ts=ts, p=p, ubp=ubp: v.scalar_tensor_tensor(
                ubp.ap()[:, :], main_ps[k], swss[p].ap()[:, fo2:fo2 + 1],
                sb["sxp_bc"].ap()[:, 512 * ts:512 * (ts + 1)],
                op0=ALU.mult, op1=ALU.mult),
                waits=[("t", ev[("fmt01", g_)] if k < 2 else ev[("fmt", g_)]),
                       ("g", ev.get(("fepg", g_, k - 2), ev.get(("fepg", g_ - 1, k + 2), 0)))])
            ev[("fep", g_, k)] = stt
            ev[("fepg", g_, k)] = GC(lambda g, fo_idx=fo_idx, ts=ts, ubp=ubp: g.tensor_tensor(
                sb["finalT"].ap()[:, NTOK * fo_idx + 512 * ts:NTOK * fo_idx + 512 * (ts + 1)],
                sb["finalT"].ap()[:, NTOK * fo_idx + 512 * ts:NTOK * fo_idx + 512 * (ts + 1)],
                ubp.ap()[:, :], op=ALU.add),
                waits=[("v", stt)])

    ev[("wfld", 1)] = G(lambda g: g.dma_start(out=sb["wf1"].ap()[:, :], in_=dr["wft"][1]),
                        sem="dg1")
    for k in range(16):
        scales_batch(0, k)
    emit_rows(0)
    def emit_wfscale(g_):
        p = g_ % 2
        wf = wfs[p]
        ldsem = "dg0" if p == 0 else "dg1"
        for fc in range(NFC):
            V(lambda v, wf=wf, fc=fc, p=p: v.tensor_tensor(
                wf.ap()[:, 256 * fc:256 * (fc + 1)], wf.ap()[:, 256 * fc:256 * (fc + 1)],
                recipws[p].ap()[:, :], op=ALU.mult),
                waits=([("v", ev[("fbce", g_)]), ("t", ev[("fsct", g_)]),
                        (ldsem, ev[("wfld", g_)])] if fc == 0 else ()))
            ev[("wfs", g_, fc)] = bld.cnt["v"]

    emit_wfscale(0)
    for g_ in range(NG):
        p = g_ % 2
        wf = wfs[p]
        ldsem = "dg0" if p == 0 else "dg1"
        def qmm(fc):
            pd = ps["d"] if (fc // 2) % 2 == 0 else ps["e"]
            off = 256 * (fc % 2)
            ev[("fqt", g_, fc)] = T(lambda t, wf=wf, fc=fc, pd=pd, off=off: t.matmul(
                pd.ap()[:, off:off + 256], sb["hbd_bf"].ap()[:, :],
                wf.ap()[:, 256 * fc:256 * (fc + 1)], start=True, stop=True),
                waits=[("v", ev[("wfs", g_, fc)]),
                       ("s", ev[("fqS", g_, fc - 3)] if fc >= 3 else
                             ev.get(("fqS", g_ - 1, NFC - 1), 0))]
                      + ([("v", ev[("fbce", g_)])] if fc == 0 else []),
                inc="t")
            if fc % 2 == 1:
                ev[("fqS", g_, fc)] = S(lambda s, fc=fc, pd=pd: s.copy(
                    sb["wq8"].ap()[:, 256 * (fc - 1):256 * (fc + 1)], pd.ap()[:, :]),
                    waits=[("t", ev[("fqt", g_, fc)])])
            else:
                ev[("fqS", g_, fc)] = ev.get(("fqS", g_, fc - 1), 0)
        wq8r = sb["wq8"].ap().rearrange("p (f c) -> p f c", f=NFC)
        def main_mm(fcp, k, gate):
            fo2, ts = k // 2, k % 2
            T(lambda t, fcp=fcp, fo2=fo2, ts=ts, k=k: t.matmul(
                main_ps[k],
                wq8r[:, fcp:fcp + 2, 128 * fo2:128 * (fo2 + 1)],
                xq8v[:, fcp:fcp + 2, 512 * ts:512 * (ts + 1)],
                start=(fcp == 0), stop=(fcp == NFC - 2),
                perf_mode=mybir.MatmulPerfMode.DoubleRow),
                waits=(([("s", ev[("fqS", g_, fcp + 1)])] if gate else [])
                      + ([("v", ev.get(("fep", g_ - 1, k), 0))] if fcp == 0 else [])),
                inc=("t" if fcp == NFC - 2 else None))
        qmm(0)
        qmm(1)
        qmm(2)
        qmm(3)
        for fc in range(0, NFC, 2):
            if fc + 4 < NFC:
                qmm(fc + 4)
                qmm(fc + 5)
            if fc >= 2:
                main_mm(fc - 2, 0, True)
                main_mm(fc - 2, 1, False)
            if g_ + 1 < NG:
                scales_batch(g_ + 1, fc // 2)
            if fc == 4 and g_ >= 1:
                emit_epi(g_ - 1, (2, 3))
        main_mm(NFC - 2, 0, True)
        main_mm(NFC - 2, 1, False)
        ev[("fmt01", g_)] = bld.cnt["t"]
        emit_epi(g_, (0, 1))
        for fc in range(0, NFC, 2):
            main_mm(fc, 2, False)
            main_mm(fc, 3, False)
        ev[("fmt", g_)] = bld.cnt["t"]
        if g_ + 2 < NG:
            ev[("wfld", g_ + 2)] = G(lambda g, g_=g_: g.dma_start(
                out=wfs[g_ % 2].ap()[:, :], in_=dr["wft"][g_ + 2]),
                waits=[("t", ev[("fqt", g_, NFC - 1)]), ("v", ev[("wfs", g_, NFC - 1)])],
                sem=("dg0" if g_ % 2 == 0 else "dg1"))
        if g_ + 1 < NG:
            emit_rows(g_ + 1)
            emit_wfscale(g_ + 1)
        if g_ == NG - 1:
            emit_epi(g_ - 1, (2, 3)) if ("fepg", g_ - 1, 2) not in ev else None
            emit_epi(g_, (2, 3))

    bld.barrier()

    bld.phases.append(("FFN", dict(bld.cnt)))

    xins = [sb["xin0"], sb["xin1"]]
    outs = [sb["outsb0"], sb["outsb1"]]
    for tt in range(NOT):
        p = tt % 2
        xin, osb = xins[p], outs[p]
        dsem = "dy0" if p == 0 else "dy1"
        ssem = "ds0" if p == 0 else "ds1"
        ld = Y(lambda yq, xin=xin, tt=tt: yq.dma_start(
            out=xin.ap()[:, :], in_=dr["xs"][OWN0 + 128 * tt:OWN0 + 128 * (tt + 1), :]),
            waits=[("v", ev.get(("oev", (tt - 2) * 8 + 7), 0))], sem=dsem)
        for bq in range(8):
            gi = tt * 8 + bq
            pb = ps["b"] if gi % 2 == 0 else ps["c"]
            for j in range(4):
                fc = 4 * bq + j
                T(lambda t, fc=fc, tt=tt, pb=pb, j=j: t.matmul(
                    pb.ap()[:, 128 * j:128 * (j + 1)],
                    sb["finalT"].ap()[:, NTOK * fc + 128 * tt:NTOK * fc + 128 * (tt + 1)],
                    sb["ident_bf"].ap()[:, :], start=True, stop=True),
                    waits=([("v", ev.get(("oev", gi - 2), 0))] if j == 0 else ()),
                    inc=("t" if j == 3 else None))
            tmm = bld.cnt["t"]
            V(lambda v, osb=osb, xin=xin, bq=bq, pb=pb: v.tensor_tensor(
                osb.ap()[:, 512 * bq:512 * (bq + 1)], pb.ap()[:, :],
                xin.ap()[:, 512 * bq:512 * (bq + 1)], op=ALU.add),
                waits=[("t", tmm), (dsem, ld), (ssem, ev.get(("ost", tt - 2), 0))])
            ev[("oev", gi)] = bld.cnt["v"]
        ev[("ost", tt)] = SD(lambda s, osb=osb, tt=tt: s.dma_start(
            out=y[128 * tt:128 * (tt + 1), :], in_=osb.ap()[:, :]),
            waits=[("v", ev[("oev", tt * 8 + 7)])], sem=ssem)

    bld.phases.append(("OUT", dict(bld.cnt)))
    global PHASES
    PHASES = bld.phases

    bld.ops["s"].append(({"ds0": bld.cnt["ds0"], "ds1": bld.cnt["ds1"]}, None, None, None))

    import contextlib
    with contextlib.ExitStack() as stack:
        SEMS = {}
        for nm in ("t", "v", "s", "g") + DMA_SEMS:
            SEMS[nm] = stack.enter_context(nc.semaphore("sem_" + nm))
        block = stack.enter_context(nc.Block())

        def run(engname, eng):
            lastw = {}
            for (w, fn, inc, val) in bld.ops[engname]:
                for semn in sorted(w):
                    tgt = w[semn]
                    if tgt > lastw.get(semn, 0):
                        eng.wait_ge(SEMS[semn], tgt)
                        lastw[semn] = tgt
                if fn is None:
                    continue
                inst = fn(eng)
                if inc is not None:
                    inst.then_inc(SEMS[inc], 16 if inc in DMA_SEMS else 1)

        @block.tensor
        def _(eng):
            run("t", eng)

        @block.vector
        def _(eng):
            run("v", eng)

        @block.scalar
        def _(eng):
            run("s", eng)

        @block.gpsimd
        def _(eng):
            run("g", eng)

        @block.sync
        def _(eng):
            run("y", eng)

    return nc


def _host_prep(L0_attn_x, L0_attn_Wq, L0_attn_Wk, L0_attn_Wv, L0_attn_Wo,
               L0_ffn_weight, L0_ffn_bias, L0_ffn_hadamard_matrix):
    f32 = np.float32
    Wq = np.asarray(L0_attn_Wq, f32)
    Wk = np.asarray(L0_attn_Wk, f32)
    Wv = np.asarray(L0_attn_Wv, f32)
    Wo = np.asarray(L0_attn_Wo, f32)
    Wf = np.asarray(L0_ffn_weight, f32)
    H = np.asarray(L0_ffn_hadamard_matrix, f32)
    bias = np.asarray(L0_ffn_bias, f32)

    wqt = np.ascontiguousarray(Wq.reshape(32, 128, 32, 128).transpose(0, 3, 2, 1)).reshape(32, 128, 4096)
    wot = np.ascontiguousarray(Wo.reshape(32, 128, 32, 128).transpose(0, 3, 2, 1)).reshape(32, 128, 4096)
    wkt = np.ascontiguousarray(Wk.reshape(128, 32, 128).transpose(2, 1, 0)).reshape(128, 4096)
    wvt = np.ascontiguousarray(Wv.reshape(128, 32, 128).transpose(2, 1, 0)).reshape(128, 4096)
    wft = np.ascontiguousarray(
        Wf.reshape(16, 256, 32, 128).transpose(0, 3, 2, 1)).reshape(16, 128, 8192)
    bias_t = np.ascontiguousarray(bias.reshape(32, 128).T)
    hbd = np.zeros((128, 128), f32)
    hbd[0:64, 0:64] = H
    hbd[64:128, 64:128] = H

    S = 2048
    inv = 1.0 / 10000.0 ** (np.arange(0, 128, 2, dtype=f32) / 128.0)
    frq = np.outer(np.arange(S, dtype=f32), inv)
    cos = np.zeros((S, 128), f32); sin = np.zeros((S, 128), f32)
    cos[:, 0::2] = np.cos(frq); cos[:, 1::2] = np.cos(frq)
    sin[:, 0::2] = np.sin(frq); sin[:, 1::2] = np.sin(frq)
    cosT = np.ascontiguousarray(cos.T)
    sin_modT = np.ascontiguousarray(sin.T)
    sin_modT[0:64, :] = -sin_modT[0:64, :]
    sin_modT = np.ascontiguousarray(np.concatenate([sin_modT[64:128], sin_modT[0:64]], axis=0))

    alpha_lin = np.linspace(0.0, 1.0, 256, dtype=f32)

    def _mk_selw():
        s = np.zeros((66, 256), f32)
        for bh in (0, 32, 64):
            s[bh + 0, 0:128] = 1.0
            s[bh + 1, 128:256] = 1.0
        return s

    consts = dict(
        bias_t=bias_t, hbd=hbd,
        ident=np.eye(128, dtype=f32),
        onesr=np.ones((1, 128), f32),
        selw=_mk_selw(),
        wqt=wqt, wkt=wkt, wvt=wvt, wot=wot, wft=wft,
    )

    x = np.asarray(L0_attn_x, f32)
    per_core = []
    for core in range(8):
        b, half = core // 2, core % 2
        xs = np.zeros((SROWS, D), f32)
        kc = np.zeros((128, SROWS), f32); ks = np.zeros((128, SROWS), f32)
        if half == 0:
            xs[256:1536] = x[b, 0:1280]
            kc[:, 256:1536] = cosT[:, 0:1280]; ks[:, 256:1536] = sin_modT[:, 0:1280]
        else:
            xs[0:1280] = x[b, 768:2048]
            kc[:, 0:1280] = cosT[:, 768:2048]; ks[:, 0:1280] = sin_modT[:, 768:2048]
        q0 = 1024 * half
        qc = (cosT[:, q0:q0 + 1024] * f32(SCALE)).astype(f32)
        qs = (sin_modT[:, q0:q0 + 1024] * f32(SCALE)).astype(f32)
        alphat = np.zeros((66, 1024), f32)
        for rr in range(4):
            r_abs = rr + 4 * half
            if r_abs == 0:
                om, al = np.zeros(256, f32), np.ones(256, f32)
            else:
                om, al = (1.0 - alpha_lin).astype(f32), alpha_lin
            for bh in (0, 32, 64):
                alphat[bh + 0, 256 * rr:256 * (rr + 1)] = om
                alphat[bh + 1, 256 * rr:256 * (rr + 1)] = al
        rsmask = np.zeros((128, 48), f32)
        for rr in range(4):
            for j in range(6):
                if j <= 3:
                    rsmask[:, rr * 12 + 2 * j] = 1.0
                if j >= 2:
                    dead = half == 1 and rr == 3 and j >= 4
                    rsmask[:, rr * 12 + 2 * j + 1] = 0.0 if dead else 1.0
        m = dict(consts)
        m.update(xs=np.ascontiguousarray(xs), ropekc=np.ascontiguousarray(kc),
                 ropeks=np.ascontiguousarray(ks), ropeqc=np.ascontiguousarray(qc),
                 ropeqs=np.ascontiguousarray(qs), alphat=alphat, rsmask=rsmask)
        per_core.append(m)
    return per_core


def kernel(L0_attn_x, L0_attn_Wq, L0_attn_Wk, L0_attn_Wv, L0_attn_Wo,
           L0_ffn_weight, L0_ffn_bias, L0_ffn_hadamard_matrix):
    in_maps = _host_prep(L0_attn_x, L0_attn_Wq, L0_attn_Wk, L0_attn_Wv,
                         L0_attn_Wo, L0_ffn_weight, L0_ffn_bias,
                         L0_ffn_hadamard_matrix)
    nc = build_nc()
    res = run_bass_kernel_spmd(nc, in_maps, core_ids=list(range(8)))
    out = np.empty((4, 2048, 4096), np.float32)
    for core in range(8):
        b, half = core // 2, core % 2
        out[b, 1024 * half:1024 * (half + 1)] = res.results[core]["y"]
    return out


# version 24
# speedup vs baseline: 1.0271x; 1.0271x over previous
import numpy as np
import concourse.bass as bass
import concourse.mybir as mybir
from concourse.bass_utils import run_bass_kernel_spmd

F32 = mybir.dt.float32
BF16 = mybir.dt.bfloat16
F8 = mybir.dt.float8e4
ALU = mybir.AluOpType
ACTF = mybir.ActivationFunctionType
AX = mybir.AxisListType

D = 4096
NH = 32
NFC = 32
SROWS = 1536
NTOK = 1024
OWN0 = 256
NTT = 12
NOT = 8
NG = 16
SCALE = 1.0 / float(np.sqrt(128.0))
EPS = float(np.float32(1e-5))
INV224 = float(np.float32(1.0) / np.float32(224.0))
CLAMP = float(np.float32(2.0) * np.float32(1e-12))
INVD = float(np.float32(1.0) / np.float32(D))

ENGINES = ("t", "v", "s", "g", "y")
DMA_SEMS = ("dgc", "dg0", "dg1", "dyc", "dy0", "dy1", "ds0", "ds1")


class B:
    def __init__(self):
        self.ops = {e: [] for e in ENGINES}
        self.cnt = {"t": 0, "v": 0, "s": 0, "g": 0, "dgc": 0, "dg0": 0, "dg1": 0,
                    "dyc": 0, "dy0": 0, "dy1": 0, "ds0": 0, "ds1": 0}
        self.pending = {e: {} for e in ENGINES}

    def emit(self, eng, fn, waits=(), inc=None):
        w = dict(self.pending[eng])
        self.pending[eng] = {}
        for sem, val in waits:
            if val and val > 0:
                w[sem] = max(w.get(sem, 0), val)
        if eng in ("v", "s", "g") and inc not in DMA_SEMS:
            prev = self.cnt[eng]
            if prev > 0:
                w[eng] = max(w.get(eng, 0), prev)
            inc = eng
        val = None
        if inc is not None:
            amt = 16 if inc in DMA_SEMS else 1
            self.cnt[inc] += amt
            val = self.cnt[inc]
        self.ops[eng].append((w, fn, inc, val))
        return val

    def barrier(self):
        for e in ENGINES:
            for k, v in self.cnt.items():
                if v > 0:
                    self.pending[e][k] = max(self.pending[e].get(k, 0), v)


PHASES = []


def build_nc():
    nc = bass.Bass("TRN2", target_bir_lowering=False)

    dr = {}
    def din(name, shape, dtype=F32):
        dr[name] = nc.dram_tensor(name, list(shape), dtype, kind="ExternalInput").ap()
    din("xs", (SROWS, D))
    din("wqt", (NH, 128, D))
    din("wkt", (128, D))
    din("wvt", (128, D))
    din("wot", (NH, 128, D))
    din("wft", (NG, 128, 8192))
    din("bias_t", (128, 32))
    din("hbd", (128, 128))
    din("ident", (128, 128))
    din("onesr", (1, 128))
    din("ropekc", (128, SROWS))
    din("ropeks", (128, SROWS))
    din("ropeqc", (128, NTOK))
    din("ropeqs", (128, NTOK))
    din("alphat", (66, 1024))
    din("rsmask", (128, 48))
    din("selw", (66, 256))
    y = nc.dram_tensor("y", [NTOK, D], F32, kind="ExternalOutput").ap()

    BASE = 16544
    A0, B0, C0, D0 = BASE, BASE + 98304, BASE + 163840, BASE + 196608
    LIMIT = 229376
    sb = {}
    def at(name, shape, dtype, off):
        t = nc.alloc_sbuf_tensor_at(name, list(shape), dtype, offset=off)
        sb[name] = t
        return t

    at("xnT", (128, NFC * SROWS), BF16, A0)
    at("attnT", (128, NH * NTOK), BF16, A0)
    at("sigT0", (128, 1536), BF16, A0 + 65536)
    at("sigT1", (128, 1536), BF16, A0 + 68608)
    at("cabc0", (128, 512), BF16, A0 + 71680)
    at("cabc1", (128, 512), BF16, A0 + 72704)
    at("alphat_sb", (66, 1024), F32, A0 + 73728)
    at("rsrp", (66, 512), F32, A0 + 93184)
    for _i in range(4):
        at(f"cacolp{_i}", (66, 256), BF16, A0 + 78848 + 512 * _i)
    at("tmpbl", (128, 256), BF16, A0 + 90112)
    at("tmpbl2", (128, 256), BF16, A0 + 90624)
    at("av_sb2", (128, 512), BF16, A0 + 91136)
    at("av_sb3", (128, 512), BF16, A0 + 92160)
    at("av_sb0", (128, 512), BF16, A0 + 95264)
    at("av_sb1", (128, 512), BF16, A0 + 96288)
    at("wo0", (128, 4096), BF16, A0 + 65536)
    at("wo1", (128, 4096), BF16, A0 + 90112)
    at("wf0", (128, 8192), BF16, A0 + 73728)
    at("wf1", (128, 8192), BF16, A0)
    at("wq8", (128, 8192), F8, A0 + 16384)
    at("swnat", (128, 2), F32, A0 + 24576)
    at("sws", (128, 2), F32, A0 + 24640)
    at("recw", (128, 2), F32, A0 + 24704)
    at("wtmp", (128, 1), F32, A0 + 24768)
    at("wrow", (1, 256), F32, A0 + 24832)
    at("recipw_bc", (128, 256), BF16, A0 + 25856)
    at("ub", (128, 512), BF16, A0 + 26880)
    at("ub1", (128, 512), BF16, A0 + 30208)
    at("wstage0", (128, 512), BF16, A0 + 32768)
    at("wstage1", (128, 512), BF16, A0 + 33792)
    at("runmax0", (128, 512), BF16, A0 + 34816)
    at("runmax1", (128, 512), BF16, A0 + 35840)
    at("swnat1", (128, 2), F32, A0 + 27904)
    at("sws1", (128, 2), F32, A0 + 27968)
    at("recw1", (128, 2), F32, A0 + 28032)
    at("wtmp1", (128, 1), F32, A0 + 28096)
    at("wrow1", (1, 256), F32, A0 + 28160)
    at("recipw_bc1", (128, 256), BF16, A0 + 29184)
    at("xin0", (128, 4096), F32, A0)
    at("xin1", (128, 4096), F32, A0 + 16384)
    at("outsb0", (128, 4096), F32, A0 + 32768)
    at("outsb1", (128, 4096), F32, A0 + 49152)
    at("QrT", (128, NH * NTOK), BF16, B0)
    at("finalT", (128, NH * NTOK), BF16, B0)
    at("xt0", (128, 4096), F32, B0)
    at("xt1", (128, 4096), F32, B0 + 16384)
    at("xnat0", (128, 4096), BF16, B0 + 32768)
    at("xnat1", (128, 4096), BF16, B0 + 40960)
    at("stats0", (128, 16), F32, B0 + 49152)
    at("stats1", (128, 16), F32, B0 + 49216)
    at("wv_sb", (128, 4096), BF16, B0 + 49280)
    at("xq8", (128, NFC * NTOK), F8, C0)
    at("rqc", (128, NTOK), BF16, C0)
    at("rqs", (128, NTOK), BF16, C0 + 2048)
    at("wqh0", (128, 4096), BF16, C0 + 4096)
    at("wqh1", (128, 4096), BF16, C0 + 12288)
    at("q_tmp", (128, 1024), BF16, C0 + 20480)
    at("tmp2", (128, 1024), BF16, C0 + 22528)
    at("wk_sb", (128, 4096), BF16, C0 + 24576)
    at("KrT", (128, SROWS), BF16, D0)
    at("Vn", (128, SROWS), BF16, D0 + 3072)
    at("rkc", (128, SROWS), BF16, D0 + 6144)
    at("rks", (128, SROWS), BF16, D0 + 9216)
    at("sxp_bc", (128, 1024), F32, D0 + 6144)
    at("recipx_bc", (128, 1024), F32, D0 + 10240)
    at("xrow_s", (1, 1024), F32, D0 + 6144)
    at("xrow_r", (1, 1024), F32, D0 + 10240)
    at("snrt", (128, 16), F32, D0 + 14336)
    at("xmaxn", (128, 8), F32, D0 + 14400)
    at("xtmp1", (128, 1), F32, D0 + 14432)
    at("ident_bf", (128, 128), BF16, D0 + 14464)
    at("hbd_bf", (128, 128), BF16, D0 + 14720)
    at("ident_f", (128, 128), F32, D0 + 14976)
    at("onesr_f", (1, 128), F32, D0 + 15488)
    at("onesr_bf", (1, 128), BF16, A0 + 81920)
    at("selw_bf", (66, 256), BF16, A0 + 80896)
    at("rsmask_sb", (128, 48), BF16, A0 + 82176)
    at("bias_sb", (128, 32), F32, D0 + 16000)
    assert D0 + 16128 + 32 <= LIMIT

    ps = {}
    ps["a"] = nc.alloc_psum_tensor("ps_a", [128, 1536], F32)
    for nm in ("b", "c", "d", "e", "f"):
        ps[nm] = nc.alloc_psum_tensor("ps_" + nm, [128, 512], F32)

    bld = B()
    bld.phases = []
    ev = {}
    T = lambda fn, waits=(), inc=None: bld.emit("t", fn, waits, inc)
    V = lambda fn, waits=(): bld.emit("v", fn, waits)
    S = lambda fn, waits=(): bld.emit("s", fn, waits)
    G = lambda fn, waits=(), sem="dgc": bld.emit("g", fn, waits, sem)
    GC = lambda fn, waits=(): bld.emit("g", fn, waits, None)
    Y = lambda fn, waits=(), sem="dyc": bld.emit("y", fn, waits, sem)
    SD = lambda fn, waits=(), sem="ds0": bld.emit("s", fn, waits, sem)

    xnT = sb["xnT"].ap().rearrange("p (f t) -> p f t", f=NFC)
    xq8v = sb["xq8"].ap().rearrange("p (f t) -> p f t", f=NFC)

    for dst, src in [("ident_bf", "ident"), ("hbd_bf", "hbd"), ("rkc", "ropekc"),
                     ("rks", "ropeks"), ("rqc", "ropeqc"), ("rqs", "ropeqs"),
                     ("wk_sb", "wkt"), ("wv_sb", "wvt")]:
        G(lambda g, dst=dst, src=src: g.dma_start(out=sb[dst].ap()[:, :], in_=dr[src]))
    cw0 = bld.cnt["dgc"]
    for dst, src in [("ident_f", "ident"), ("onesr_f", "onesr")]:
        Y(lambda yq, dst=dst, src=src: yq.dma_start(out=sb[dst].ap()[:, :], in_=dr[src]))
    cy0 = bld.cnt["dyc"]

    xts = [sb["xt0"], sb["xt1"]]
    xnats = [sb["xnat0"], sb["xnat1"]]
    statss = [sb["stats0"], sb["stats1"]]
    def p0_stats(tt):
        p = tt % 2
        xt, xnat, st = xts[p], xnats[p], statss[p]
        dsem = "dy0" if p == 0 else "dy1"
        ld = Y(lambda yq, xt=xt, tt=tt: yq.dma_start(
            out=xt.ap()[:, :], in_=dr["xs"][128 * tt:128 * (tt + 1), :]),
            waits=[("s", ev.get(("p0norm", tt - 2), 0))], sem=dsem)
        ev[("p0sum", tt)] = V(lambda v, xt=xt, st=st: v.reduce_sum(
            st.ap()[:, 0:1], xt.ap()[:, :], axis=AX.X),
            waits=[(dsem, ld), ("s", ev.get(("p0norm", tt - 2), 0))])
        ev[("p0sq", tt)] = S(lambda s, xt=xt, xnat=xnat, st=st: s.activation(
            xnat.ap()[:, :], xt.ap()[:, :], ACTF.Square, accum_out=st.ap()[:, 1:2]),
            waits=[(dsem, ld), ("t", ev.get(("p0t", tt - 2), 0)),
                   ("v", ev.get(("p0sum", tt), 0))])
        V(lambda v, st=st: v.tensor_scalar(st.ap()[:, 2:3], st.ap()[:, 0:1], INVD, None, op0=ALU.mult))
        V(lambda v, st=st: v.tensor_tensor(st.ap()[:, 3:4], st.ap()[:, 2:3], st.ap()[:, 2:3], op=ALU.mult))
        V(lambda v, st=st: v.tensor_scalar(st.ap()[:, 4:5], st.ap()[:, 1:2], INVD, EPS, op0=ALU.mult, op1=ALU.add),
          waits=[("s", ev[("p0sq", tt)])])
        ev[("p0var", tt)] = V(lambda v, st=st: v.tensor_tensor(
            st.ap()[:, 5:6], st.ap()[:, 4:5], st.ap()[:, 3:4], op=ALU.subtract))
        ev[("p0sqrt", tt)] = S(lambda s, st=st: s.activation(st.ap()[:, 6:7], st.ap()[:, 5:6], ACTF.Sqrt),
                               waits=[("v", ev[("p0var", tt)])])
        V(lambda v, st=st: v.reciprocal(st.ap()[:, 7:8], st.ap()[:, 6:7]),
          waits=[("s", ev[("p0sqrt", tt)])])
        ev[("p0rstd", tt)] = V(lambda v, st=st: v.tensor_scalar(
            st.ap()[:, 8:9], st.ap()[:, 2:3], st.ap()[:, 7:8], -1.0,
            op0=ALU.mult, op1=ALU.mult))
        ev[("p0norm", tt)] = S(lambda s, xt=xt, xnat=xnat, st=st: s.activation(
            xnat.ap()[:, :], xt.ap()[:, :], ACTF.Identity, bias=st.ap()[:, 8:9], scale=st.ap()[:, 7:8]),
            waits=[("v", ev[("p0rstd", tt)])])

    def p0_trans(tt):
        p = tt % 2
        xnat = xnats[p]
        for bq in range(8):
            gi = tt * 8 + bq
            pb = ps["b"] if gi % 2 == 0 else ps["c"]
            for j in range(4):
                fc = 4 * bq + j
                T(lambda t, xnat=xnat, fc=fc, pb=pb, j=j: t.matmul(
                    pb.ap()[:, 128 * j:128 * (j + 1)],
                    xnat.ap()[:, 128 * fc:128 * (fc + 1)],
                    sb["ident_bf"].ap()[:, :], start=True, stop=True),
                    waits=([("s", ev[("p0norm", tt)]), ("dgc", cw0),
                            (("v" if gi % 2 == 0 else "s"), ev.get(("p0ev", gi - 2), 0))] if j == 0 else ()),
                    inc=("t" if j == 3 else None))
            tmm = bld.cnt["t"]
            if bq % 2 == 0:
                ev[("p0ev", gi)] = V(lambda v, tt=tt, bq=bq, pb=pb: v.tensor_copy(
                    xnT[:, 4 * bq:4 * bq + 4, 128 * tt:128 * (tt + 1)],
                    pb.ap().rearrange("p (a b) -> p a b", a=4)[:, :, :]),
                    waits=[("t", tmm)])
            else:
                ev[("p0ev", gi)] = S(lambda s, tt=tt, bq=bq, pb=pb: s.copy(
                    xnT[:, 4 * bq:4 * bq + 4, 128 * tt:128 * (tt + 1)],
                    pb.ap().rearrange("p (a b) -> p a b", a=4)[:, :, :]),
                    waits=[("t", tmm)])
        ev[("p0t", tt)] = bld.cnt["t"]

    for tt in range(NTT + 1):
        if tt < NTT:
            p0_stats(tt)
        if tt >= 1:
            p0_trans(tt - 1)

    bld.barrier()

    bld.phases.append(("P0", dict(bld.cnt)))

    for ts3 in range(3):
        for fc in range(NFC):
            T(lambda t, ts3=ts3, fc=fc: t.matmul(
                ps["a"].ap()[:, 512 * ts3:512 * (ts3 + 1)],
                sb["wk_sb"].ap()[:, 128 * fc:128 * (fc + 1)],
                xnT[:, fc, 512 * ts3:512 * (ts3 + 1)],
                start=(fc == 0), stop=(fc == NFC - 1)),
                inc=("t" if fc == NFC - 1 else None))
        ev[("kt", ts3)] = bld.cnt["t"]
        off = 512 * (ts3 % 2)
        ev[("kev", ts3)] = S(lambda s, ts3=ts3, off=off: s.copy(
            sb["q_tmp"].ap()[:, off:off + 512],
            ps["a"].ap()[:, 512 * ts3:512 * (ts3 + 1)]),
            waits=[("t", ev[("kt", ts3)]), ("v", ev.get(("krope", ts3 - 2), 0))])
        qt = sb["q_tmp"].ap()
        V(lambda v, off=off, ts3=ts3: v.tensor_tensor(
            sb["tmp2"].ap()[0:64, 0:512], qt[64:128, off:off + 512],
            sb["rks"].ap()[64:128, 512 * ts3:512 * (ts3 + 1)], op=ALU.mult),
            waits=[("s", ev[("kev", ts3)])])
        V(lambda v, off=off, ts3=ts3: v.tensor_tensor(
            sb["tmp2"].ap()[64:128, 0:512], qt[0:64, off:off + 512],
            sb["rks"].ap()[0:64, 512 * ts3:512 * (ts3 + 1)], op=ALU.mult))
        V(lambda v, off=off, ts3=ts3: v.tensor_tensor(
            sb["KrT"].ap()[:, 512 * ts3:512 * (ts3 + 1)], qt[:, off:off + 512],
            sb["rkc"].ap()[:, 512 * ts3:512 * (ts3 + 1)], op=ALU.mult))
        ev[("krope", ts3)] = V(lambda v, ts3=ts3: v.tensor_tensor(
            sb["KrT"].ap()[:, 512 * ts3:512 * (ts3 + 1)],
            sb["KrT"].ap()[:, 512 * ts3:512 * (ts3 + 1)],
            sb["tmp2"].ap()[:, 0:512], op=ALU.add))
    for tg in range(3):
        pb = ps["b"] if tg % 2 == 0 else ps["c"]
        for fc in range(NFC):
            for tt4 in range(4):
                tt = 4 * tg + tt4
                T(lambda t, fc=fc, tt=tt, tt4=tt4, pb=pb: t.matmul(
                    pb.ap()[:, 128 * tt4:128 * (tt4 + 1)],
                    xnT[:, fc, 128 * tt:128 * (tt + 1)],
                    sb["wv_sb"].ap()[:, 128 * fc:128 * (fc + 1)],
                    start=(fc == 0 and tt4 == 0), stop=(fc == NFC - 1 and tt4 == 3)),
                    waits=([("v", ev.get(("vev", tg - 2), 0))] if (fc == 0 and tt4 == 0) else ()),
                    inc=("t" if (fc == NFC - 1 and tt4 == 3) else None))
        ev[("vt", tg)] = bld.cnt["t"]
        ev[("vev", tg)] = V(lambda v, tg=tg, pb=pb: v.tensor_copy(
            sb["Vn"].ap()[:, 512 * tg:512 * (tg + 1)], pb.ap()[:, :]),
            waits=[("t", ev[("vt", tg)])])

    bld.barrier()

    bld.phases.append(("KV", dict(bld.cnt)))

    wqhs = [sb["wqh0"], sb["wqh1"]]
    for h in range(NH):
        wq = wqhs[h % 2]
        qsem = "dg0" if h % 2 == 0 else "dg1"
        ld = G(lambda g, h=h, wq=wq: g.dma_start(out=wq.ap()[:, :], in_=dr["wqt"][h]),
               waits=[("t", ev.get(("qt", h - 2, 1), 0))], sem=qsem)
        for ts in range(2):
            slot = (2 * h + ts) % 4
            pq, off = [(ps["a"], 0), (ps["a"], 512), (ps["b"], 0), (ps["c"], 0)][slot]
            for fc in range(NFC):
                T(lambda t, wq=wq, fc=fc, ts=ts, pq=pq, off=off: t.matmul(
                    pq.ap()[:, off:off + 512],
                    wq.ap()[:, 128 * fc:128 * (fc + 1)],
                    xnT[:, fc, OWN0 + 512 * ts:OWN0 + 512 * (ts + 1)],
                    start=(fc == 0), stop=(fc == NFC - 1)),
                    waits=([(qsem, ld), ("s", ev.get(("qev", h - 2, ts), 0))] if fc == 0 else ()),
                    inc=("t" if fc == NFC - 1 else None))
            ev[("qt", h, ts)] = bld.cnt["t"]
            ev[("qev", h, ts)] = S(lambda s, ts=ts, pq=pq, off=off: s.copy(
                sb["q_tmp"].ap()[:, 512 * ts:512 * (ts + 1)], pq.ap()[:, off:off + 512]),
                waits=[("t", ev[("qt", h, ts)]), ("v", ev.get(("qrope", h - 1), 0))])
        qt = sb["q_tmp"].ap()
        dstq = sb["QrT"].ap()[:, NTOK * h:NTOK * (h + 1)]
        V(lambda v, qt=qt: v.tensor_tensor(
            sb["tmp2"].ap()[0:64, :], qt[64:128, :], sb["rqs"].ap()[64:128, :], op=ALU.mult),
            waits=[("s", ev[("qev", h, 1)])])
        V(lambda v, qt=qt: v.tensor_tensor(
            sb["tmp2"].ap()[64:128, :], qt[0:64, :], sb["rqs"].ap()[0:64, :], op=ALU.mult))
        V(lambda v, qt=qt, dstq=dstq: v.tensor_tensor(dstq, qt, sb["rqc"].ap()[:, :], op=ALU.mult))
        ev[("qrope", h)] = V(lambda v, dstq=dstq: v.tensor_tensor(
            dstq, dstq, sb["tmp2"].ap()[:, :], op=ALU.add))

    bld.barrier()

    bld.phases.append(("Q", dict(bld.cnt)))

    for tt in range(NOT):
        for bq in range(8):
            gi = tt * 8 + bq
            pb = ps["b"] if gi % 2 == 0 else ps["c"]
            for j in range(4):
                fc = 4 * bq + j
                T(lambda t, fc=fc, tt=tt, pb=pb, j=j: t.matmul(
                    pb.ap()[:, 128 * j:128 * (j + 1)],
                    xnT[:, fc, OWN0 + 128 * tt:OWN0 + 128 * (tt + 1)],
                    sb["hbd_bf"].ap()[:, :], start=True, stop=True),
                    waits=([("v", ev.get(("xsc", gi - 2), 0))] if j == 0 else ()),
                    inc=("t" if j == 3 else None))
            tmm = bld.cnt["t"]
            if bq == 0:
                V(lambda v, tt=tt, pb=pb: v.reduce_max(
                    sb["xmaxn"].ap()[:, tt:tt + 1],
                    pb.ap().rearrange("p (a b) -> p a b", a=4)[:, :, :],
                    axis=AX.XY, apply_absolute_value=True), waits=[("t", tmm)])
            else:
                V(lambda v, pb=pb: v.reduce_max(
                    sb["xtmp1"].ap()[:, 0:1],
                    pb.ap().rearrange("p (a b) -> p a b", a=4)[:, :, :],
                    axis=AX.XY, apply_absolute_value=True), waits=[("t", tmm)])
                V(lambda v, tt=tt: v.tensor_tensor(
                    sb["xmaxn"].ap()[:, tt:tt + 1], sb["xmaxn"].ap()[:, tt:tt + 1],
                    sb["xtmp1"].ap()[:, 0:1], op=ALU.max))
            ev[("xsc", gi)] = bld.cnt["v"]
        V(lambda v, tt=tt: v.tensor_scalar(
            sb["snrt"].ap()[:, 2 * tt:2 * tt + 1], sb["xmaxn"].ap()[:, tt:tt + 1],
            INV224, CLAMP, op0=ALU.mult, op1=ALU.max))
        ev[("xsn", tt)] = V(lambda v, tt=tt: v.reciprocal(
            sb["snrt"].ap()[:, 2 * tt + 1:2 * tt + 2], sb["snrt"].ap()[:, 2 * tt:2 * tt + 1]))
    for tt in range(NOT):
        T(lambda t, tt=tt: t.matmul(
            ps["a"].ap()[0:1, 128 * tt:128 * (tt + 1)],
            sb["snrt"].ap()[:, 2 * tt:2 * tt + 1],
            sb["ident_f"].ap()[:, :], start=True, stop=True),
            waits=([("v", ev[("xsn", NOT - 1)]), ("dyc", cy0),
                    ("v", ev[("xsc", NOT * 8 - 1)])] if tt == 0 else ()))
        prow = ps["b"] if tt < 4 else ps["c"]
        T(lambda t, tt=tt, prow=prow: t.matmul(
            prow.ap()[0:1, 128 * (tt % 4):128 * (tt % 4) + 128],
            sb["snrt"].ap()[:, 2 * tt + 1:2 * tt + 2],
            sb["ident_f"].ap()[:, :], start=True, stop=True),
            inc=("t" if tt == NOT - 1 else None))
    ev["xrt"] = bld.cnt["t"]
    ev["xrow_s_ev"] = V(lambda v: v.tensor_copy(sb["xrow_s"].ap()[0:1, :], ps["a"].ap()[0:1, 0:1024]),
                        waits=[("t", ev["xrt"])])
    V(lambda v: v.tensor_copy(sb["xrow_r"].ap()[0:1, 0:512], ps["b"].ap()[0:1, 0:512]))
    ev["xrow_r_ev"] = V(lambda v: v.tensor_copy(sb["xrow_r"].ap()[0:1, 512:1024], ps["c"].ap()[0:1, 0:512]))
    for c2 in range(2):
        pd = ps["d"] if c2 == 0 else ps["e"]
        T(lambda t, c2=c2, pd=pd: t.matmul(
            pd.ap()[:, :], sb["onesr_f"].ap()[0:1, :],
            sb["xrow_s"].ap()[0:1, 512 * c2:512 * (c2 + 1)], start=True, stop=True),
            waits=[("v", ev["xrow_s_ev"])])
        pb = ps["b"] if c2 == 0 else ps["c"]
        T(lambda t, c2=c2, pb=pb: t.matmul(
            pb.ap()[:, :], sb["onesr_f"].ap()[0:1, :],
            sb["xrow_r"].ap()[0:1, 512 * c2:512 * (c2 + 1)], start=True, stop=True),
            waits=[("v", ev["xrow_r_ev"])],
            inc=("t" if c2 == 1 else None))
    xbc_t = bld.cnt["t"]
    V(lambda v: v.tensor_copy(sb["sxp_bc"].ap()[:, 0:512], ps["d"].ap()[:, :]), waits=[("t", xbc_t)])
    V(lambda v: v.tensor_copy(sb["sxp_bc"].ap()[:, 512:1024], ps["e"].ap()[:, :]))
    V(lambda v: v.tensor_copy(sb["recipx_bc"].ap()[:, 0:512], ps["b"].ap()[:, :]))
    ev["xbc_ev"] = V(lambda v: v.tensor_copy(sb["recipx_bc"].ap()[:, 512:1024], ps["c"].ap()[:, :]))
    for fc in range(NFC):
        for ts in range(2):
            k = 2 * fc + ts
            pd = ps["d"] if k % 2 == 0 else ps["e"]
            tmm = T(lambda t, fc=fc, ts=ts, pd=pd: t.matmul(
                pd.ap()[:, :], sb["hbd_bf"].ap()[:, :],
                xnT[:, fc, OWN0 + 512 * ts:OWN0 + 512 * (ts + 1)], start=True, stop=True),
                waits=[("v", ev.get(("xq", k - 2), ev["xbc_ev"]))], inc="t")
            V(lambda v, fc=fc, ts=ts, pd=pd: v.tensor_tensor(
                xq8v[:, fc, 512 * ts:512 * (ts + 1)], pd.ap()[:, :],
                sb["recipx_bc"].ap()[:, 512 * ts:512 * (ts + 1)], op=ALU.mult),
                waits=[("t", tmm)])
            ev[("xq", k)] = bld.cnt["v"]

    latew = [("t", bld.cnt["t"]), ("v", bld.cnt["v"])]
    for dst, src in [("onesr_bf", "onesr"), ("rsmask_sb", "rsmask"), ("selw_bf", "selw")]:
        G(lambda g, dst=dst, src=src: g.dma_start(out=sb[dst].ap()[:, :], in_=dr[src]),
          waits=latew)
    for dst, src in [("alphat_sb", "alphat"), ("bias_sb", "bias_t")]:
        Y(lambda yq, dst=dst, src=src: yq.dma_start(out=sb[dst].ap()[:, :], in_=dr[src]),
          waits=latew)

    bld.barrier()

    bld.phases.append(("XQ", dict(bld.cnt)))

    sigTs = [sb["sigT0"], sb["sigT1"]]
    cabcs = [sb["cabc0"], sb["cabc1"]]
    cacolps = [sb[f"cacolp{_i}"] for _i in range(4)]
    av_sbs = [sb["av_sb0"], sb["av_sb1"], sb["av_sb2"], sb["av_sb3"]]

    def att_scores(hs):
        rr, h = hs // 32, hs % 32
        L0 = 256 * rr
        for j in range(6):
            T(lambda t, h=h, j=j, L0=L0, rr=rr: t.matmul(
                ps["a"].ap()[:, 256 * j:256 * (j + 1)],
                sb["KrT"].ap()[:, L0 + 128 * j:L0 + 128 * (j + 1)],
                sb["QrT"].ap()[:, NTOK * h + 256 * rr:NTOK * h + 256 * (rr + 1)],
                start=True, stop=True),
                waits=([("s", ev.get(("sig", hs - 1), 0))] if j == 0 else ()),
                inc=("t" if j == 5 else None))
        ev[("sct", hs)] = bld.cnt["t"]
        sigT = sigTs[hs % 2].ap()
        ev[("sig", hs)] = S(lambda s, sigT=sigT: s.activation(
            sigT[:, :], ps["a"].ap()[:, :], ACTF.Sigmoid),
            waits=[("t", ev[("sct", hs)]), ("t", ev.get(("avt", hs - 2), 0))])

    def att_cons1(hs):
        rr, h = hs // 32, hs % 32
        L0 = 256 * rr
        par = hs % 2
        sigT = sigTs[par].ap()
        b3, m3 = hs // 3, hs % 3
        prs = ps["d"] if b3 % 2 == 0 else ps["f"]
        base = 32 * m3
        for j in range(6):
            T(lambda t, sigT=sigT, j=j, rr=rr, prs=prs, base=base: t.matmul(
                prs.ap()[base:base + 2, 0:256],
                sb["rsmask_sb"].ap()[:, 12 * rr + 2 * j:12 * rr + 2 * j + 2],
                sigT[:, 256 * j:256 * (j + 1)],
                start=(j == 0), stop=(j == 5)),
                waits=([("s", ev[("sig", hs)]),
                        ("v", ev.get(("recb", b3 - 2), 0))] if j == 0 else ()),
                inc=("t" if j == 5 else None))
        ev[("rst", hs)] = bld.cnt["t"]
        pav = ps["b"] if par == 0 else ps["c"]
        for j in range(6):
            ops = []
            if j <= 3:
                ops.append((0, j == 0, False))
            if j >= 2:
                ops.append((1, False, j == 5))
            for (w2, st_, sp_) in ops:
                T(lambda t, j=j, w2=w2, st_=st_, sp_=sp_, sigT=sigT, pav=pav, L0=L0: t.matmul(
                    pav.ap()[:, 256 * w2:256 * (w2 + 1)],
                    sb["Vn"].ap()[:, L0 + 128 * j:L0 + 128 * (j + 1)],
                    sigT[:, 256 * j:256 * (j + 1)],
                    start=st_, stop=sp_),
                    waits=([("s", ev.get(("avec", hs - 2), 0))] if (j == 0 and w2 == 0) else ()),
                    inc=("t" if (w2 == 1 and sp_) else None))
        ev[("avt", hs)] = bld.cnt["t"]
        ev[("avec", hs)] = S(lambda s, pav=pav, hs4=hs % 4: s.copy(
            av_sbs[hs4].ap()[:, :], pav.ap()[:, :]),
            waits=[("t", ev[("avt", hs)]), ("v", ev.get(("bl", hs - 4), 0))])
        if m3 == 2 or hs == 127:
            ro = 256 * (b3 % 2)
            V(lambda v, prs=prs, ro=ro: v.reciprocal(
                sb["rsrp"].ap()[0:66, ro:ro + 256], prs.ap()[0:66, 0:256]),
                waits=[("t", ev[("rst", hs)])])
            ev[("recb", b3)] = bld.cnt["v"]
            for hh in range(3 * b3, min(3 * b3 + 3, 128)):
                bh = 32 * (hh % 3)
                rrh = hh // 32
                ev[("ca", hh)] = V(lambda v, rrh=rrh, h4=hh % 4, ro=ro, bh=bh:
                    v.tensor_tensor(
                        cacolps[h4].ap()[bh:bh + 2, :],
                        sb["rsrp"].ap()[bh:bh + 2, ro:ro + 256],
                        sb["alphat_sb"].ap()[bh:bh + 2, 256 * rrh:256 * (rrh + 1)],
                        op=ALU.mult),
                    waits=[("t", ev.get(("bctt", hh - 4), 0))])

    def att_cons2_pe(hs):
        cac = cacolps[hs % 4].ap()
        bh = 32 * (hs % 3)
        for w2 in range(2):
            T(lambda t, w2=w2, cac=cac, bh=bh: t.matmul(
                ps["e"].ap()[:, 256 * w2:256 * (w2 + 1)],
                sb["selw_bf"].ap()[bh:bh + 2, 128 * w2:128 * (w2 + 1)],
                cac[bh:bh + 2, 0:256],
                start=True, stop=True),
                waits=([("v", ev[("ca", hs)]), ("s", ev.get(("cab", hs - 1), 0))]
                       if w2 == 0 else ()),
                inc=("t" if w2 == 1 else None))
        ev[("bctt", hs)] = bld.cnt["t"]

    def att_cons2(hs):
        rr, h = hs // 32, hs % 32
        par = hs % 2
        cab = cabcs[par].ap()
        ev[("cab", hs)] = S(lambda s, cab=cab: s.copy(cab[:, :], ps["e"].ap()[:, :]),
                            waits=[("t", ev[("bctt", hs)]), ("v", ev.get(("bl", hs - 2), 0))])
        avs = av_sbs[hs % 4].ap()
        dsta = sb["attnT"].ap()[:, NTOK * h + 256 * rr:NTOK * h + 256 * (rr + 1)]
        V(lambda v, avs=avs, cab=cab: v.tensor_tensor(
            sb["tmpbl"].ap()[:, :], avs[:, 0:256], cab[:, 0:256], op=ALU.mult),
            waits=[("s", ev[("cab", hs)])])
        V(lambda v, avs=avs, cab=cab: v.tensor_tensor(
            sb["tmpbl2"].ap()[:, :], avs[:, 256:512], cab[:, 256:512], op=ALU.mult))
        V(lambda v, dsta=dsta: v.tensor_tensor(
            dsta, sb["tmpbl"].ap()[:, :], sb["tmpbl2"].ap()[:, :], op=ALU.add))
        ev[("bl", hs)] = bld.cnt["v"]

    for hs in range(132):
        if hs < 128:
            att_scores(hs)
        if 4 <= hs:
            att_cons2_pe(hs - 4)
            att_cons2(hs - 4)
        if 1 <= hs and hs - 1 < 128:
            att_cons1(hs - 1)

    bld.barrier()

    bld.phases.append(("ATT", dict(bld.cnt)))

    wos = [sb["wo0"], sb["wo1"]]
    for fo in range(NH):
        wo = wos[fo % 2]
        osem = "dg0" if fo % 2 == 0 else "dg1"
        ld = G(lambda g, fo=fo, wo=wo: g.dma_start(out=wo.ap()[:, :], in_=dr["wot"][fo]),
               waits=[("t", ev.get(("wot", fo - 2, 1), 0))], sem=osem)
        for ts in range(2):
            slot = (2 * fo + ts) % 4
            pq, off = [(ps["a"], 0), (ps["a"], 512), (ps["b"], 0), (ps["c"], 0)][slot]
            for fc in range(NFC):
                T(lambda t, wo=wo, fc=fc, ts=ts, pq=pq, off=off: t.matmul(
                    pq.ap()[:, off:off + 512],
                    wo.ap()[:, 128 * fc:128 * (fc + 1)],
                    sb["attnT"].ap()[:, NTOK * fc + 512 * ts:NTOK * fc + 512 * (ts + 1)],
                    start=(fc == 0), stop=(fc == NFC - 1)),
                    waits=([(osem, ld), ("s", ev.get(("woev", fo - 2, ts), 0))] if fc == 0 else ()),
                    inc=("t" if fc == NFC - 1 else None))
            ev[("wot", fo, ts)] = bld.cnt["t"]
            ev[("woev", fo, ts)] = S(lambda s, fo=fo, ts=ts, pq=pq, off=off: s.activation(
                sb["finalT"].ap()[:, NTOK * fo + 512 * ts:NTOK * fo + 512 * (ts + 1)],
                pq.ap()[:, off:off + 512], ACTF.Identity, bias=sb["bias_sb"].ap()[:, fo:fo + 1]),
                waits=[("t", ev[("wot", fo, ts)])])
    wfs = [sb["wf0"], sb["wf1"]]
    ev[("wfld", 0)] = G(lambda g: g.dma_start(out=sb["wf0"].ap()[:, :], in_=dr["wft"][0]),
                        waits=[("dg0", bld.cnt["dg0"])], sem="dg0")

    bld.barrier()

    bld.phases.append(("WO", dict(bld.cnt)))

    swnats = [sb["swnat"], sb["swnat1"]]
    swss = [sb["sws"], sb["sws1"]]
    recws = [sb["recw"], sb["recw1"]]
    wtmps = [sb["wtmp"], sb["wtmp1"]]
    wrows = [sb["wrow"], sb["wrow1"]]
    recipws = [sb["recipw_bc"], sb["recipw_bc1"]]
    main_ps = [ps["b"].ap()[:, :], ps["c"].ap()[:, :], ps["f"].ap()[:, :],
               ps["a"].ap()[:, 0:512]]

    def scales_batch(g_, k):
        p = g_ % 2
        wf = wfs[p]
        ldsem = "dg0" if p == 0 else "dg1"
        fo2, bq = k // 8, k % 8
        slot = 1 + k % 2
        for j in range(4):
            fc = 4 * bq + j
            T(lambda t, wf=wf, fc=fc, fo2=fo2, slot=slot, j=j: t.matmul(
                ps["a"].ap()[:, 512 * slot + 128 * j:512 * slot + 128 * (j + 1)],
                wf.ap()[:, 256 * fc + 128 * fo2:256 * fc + 128 * (fo2 + 1)],
                sb["hbd_bf"].ap()[:, :], start=True, stop=True),
                waits=([(ldsem, ev[("wfld", g_)]),
                        ("s", ev[("fstg", g_, k - 2)] if k >= 2 else
                              ev.get(("fstg", g_ - 1, 14 + k), 0))]
                       if j == 0 else ()),
                inc=("t" if j == 3 else None))
        tmm = bld.cnt["t"]
        stg = sb["wstage0"] if k % 2 == 0 else sb["wstage1"]
        rmx = sb["runmax0"] if fo2 == 0 else sb["runmax1"]
        ev[("fstg", g_, k)] = S(lambda s, stg=stg, slot=slot: s.activation(
            stg.ap()[:, :], ps["a"].ap()[:, 512 * slot:512 * (slot + 1)], ACTF.Abs),
            waits=[("t", tmm), ("v", ev.get(("fsc", g_, k - 2), 0))])
        if bq == 0:
            V(lambda v, stg=stg, rmx=rmx: v.tensor_copy(rmx.ap()[:, :], stg.ap()[:, :]),
                waits=[("s", ev[("fstg", g_, k)])])
        else:
            V(lambda v, stg=stg, rmx=rmx: v.tensor_tensor(
                rmx.ap()[:, :], rmx.ap()[:, :], stg.ap()[:, :], op=ALU.max),
                waits=[("s", ev[("fstg", g_, k)])])
        ev[("fsc", g_, k)] = bld.cnt["v"]
        if bq == 7:
            V(lambda v, fo2=fo2, p=p, rmx=rmx: v.reduce_max(
                swnats[p].ap()[:, fo2:fo2 + 1], rmx.ap()[:, :], axis=AX.X))
            V(lambda v, fo2=fo2, p=p: v.tensor_scalar(
                swss[p].ap()[:, fo2:fo2 + 1], swnats[p].ap()[:, fo2:fo2 + 1],
                INV224, CLAMP, op0=ALU.mult, op1=ALU.max))
            ev[("fsn", g_, fo2)] = V(lambda v, fo2=fo2, p=p: v.reciprocal(
                recws[p].ap()[:, fo2:fo2 + 1], swss[p].ap()[:, fo2:fo2 + 1]))
        if k == 15:
            ev[("fsct", g_)] = bld.cnt["t"]

    def emit_rows(g_):
        p = g_ % 2
        for fo2 in range(2):
            T(lambda t, fo2=fo2, p=p: t.matmul(
                ps["d"].ap()[0:1, 128 * fo2:128 * (fo2 + 1)],
                recws[p].ap()[:, fo2:fo2 + 1],
                sb["ident_f"].ap()[:, :], start=True, stop=True),
                waits=[("v", ev[("fsn", g_, fo2)]),
                       ("s", ev.get(("fqS", g_ - 1, NFC - 1), 0)),
                       ("v", ev.get(("fbce", g_ - 1), 0))],
                inc=("t" if fo2 == 1 else None))
        ev[("frt", g_)] = bld.cnt["t"]
        ev[("frow", g_)] = V(lambda v, p=p: v.tensor_copy(
            wrows[p].ap()[0:1, :], ps["d"].ap()[0:1, 0:256]),
            waits=[("t", ev[("frt", g_)])])
        ev[("fbc", g_)] = T(lambda t, p=p: t.matmul(
            ps["d"].ap()[:, 256:512], sb["onesr_f"].ap()[0:1, :],
            wrows[p].ap()[0:1, :], start=True, stop=True),
            waits=[("v", ev[("frow", g_)])], inc="t")
        ev[("fbce", g_)] = V(lambda v, p=p: v.tensor_copy(
            recipws[p].ap()[:, :], ps["d"].ap()[:, 256:512]),
            waits=[("t", ev[("fbc", g_)])])

    ubs = [sb["ub"], sb["ub1"]]
    def emit_epi(g_, ks):
        p = g_ % 2
        for k in ks:
            fo2, ts = k // 2, k % 2
            fo_idx = 2 * g_ + fo2
            ubp = ubs[k % 2]
            stt = V(lambda v, k=k, fo2=fo2, ts=ts, p=p, ubp=ubp: v.scalar_tensor_tensor(
                ubp.ap()[:, :], main_ps[k], swss[p].ap()[:, fo2:fo2 + 1],
                sb["sxp_bc"].ap()[:, 512 * ts:512 * (ts + 1)],
                op0=ALU.mult, op1=ALU.mult),
                waits=[("t", ev[("fmt01", g_)] if k < 2 else ev[("fmt", g_)]),
                       ("g", ev.get(("fepg", g_, k - 2), ev.get(("fepg", g_ - 1, k + 2), 0)))])
            ev[("fep", g_, k)] = stt
            ev[("fepg", g_, k)] = GC(lambda g, fo_idx=fo_idx, ts=ts, ubp=ubp: g.tensor_tensor(
                sb["finalT"].ap()[:, NTOK * fo_idx + 512 * ts:NTOK * fo_idx + 512 * (ts + 1)],
                sb["finalT"].ap()[:, NTOK * fo_idx + 512 * ts:NTOK * fo_idx + 512 * (ts + 1)],
                ubp.ap()[:, :], op=ALU.add),
                waits=[("v", stt)])

    ev[("wfld", 1)] = G(lambda g: g.dma_start(out=sb["wf1"].ap()[:, :], in_=dr["wft"][1]),
                        sem="dg1")
    for k in range(16):
        scales_batch(0, k)
    emit_rows(0)
    def emit_wfscale(g_):
        p = g_ % 2
        wf = wfs[p]
        ldsem = "dg0" if p == 0 else "dg1"
        for fc in range(NFC):
            V(lambda v, wf=wf, fc=fc, p=p: v.tensor_tensor(
                wf.ap()[:, 256 * fc:256 * (fc + 1)], wf.ap()[:, 256 * fc:256 * (fc + 1)],
                recipws[p].ap()[:, :], op=ALU.mult),
                waits=([("v", ev[("fbce", g_)]), ("t", ev[("fsct", g_)]),
                        (ldsem, ev[("wfld", g_)])] if fc == 0 else ()))
            ev[("wfs", g_, fc)] = bld.cnt["v"]

    emit_wfscale(0)
    for g_ in range(NG):
        p = g_ % 2
        wf = wfs[p]
        ldsem = "dg0" if p == 0 else "dg1"
        def qmm(fc):
            pd = ps["d"] if (fc // 2) % 2 == 0 else ps["e"]
            off = 256 * (fc % 2)
            ev[("fqt", g_, fc)] = T(lambda t, wf=wf, fc=fc, pd=pd, off=off: t.matmul(
                pd.ap()[:, off:off + 256], sb["hbd_bf"].ap()[:, :],
                wf.ap()[:, 256 * fc:256 * (fc + 1)], start=True, stop=True),
                waits=[("v", ev[("wfs", g_, fc)]),
                       ("s", ev[("fqS", g_, fc - 3)] if fc >= 3 else
                             ev.get(("fqS", g_ - 1, NFC - 1), 0))]
                      + ([("v", ev[("fbce", g_)])] if fc == 0 else []),
                inc="t")
            if fc % 2 == 1:
                ev[("fqS", g_, fc)] = S(lambda s, fc=fc, pd=pd: s.copy(
                    sb["wq8"].ap()[:, 256 * (fc - 1):256 * (fc + 1)], pd.ap()[:, :]),
                    waits=[("t", ev[("fqt", g_, fc)])])
            else:
                ev[("fqS", g_, fc)] = ev.get(("fqS", g_, fc - 1), 0)
        wq8r = sb["wq8"].ap().rearrange("p (f c) -> p f c", f=NFC)
        def main_mm(fcp, k, gate):
            fo2, ts = k // 2, k % 2
            T(lambda t, fcp=fcp, fo2=fo2, ts=ts, k=k: t.matmul(
                main_ps[k],
                wq8r[:, fcp:fcp + 2, 128 * fo2:128 * (fo2 + 1)],
                xq8v[:, fcp:fcp + 2, 512 * ts:512 * (ts + 1)],
                start=(fcp == 0), stop=(fcp == NFC - 2),
                perf_mode=mybir.MatmulPerfMode.DoubleRow),
                waits=(([("s", ev[("fqS", g_, fcp + 1)])] if gate else [])
                      + ([("v", ev.get(("fep", g_ - 1, k), 0))] if fcp == 0 else [])),
                inc=("t" if fcp == NFC - 2 else None))
        qmm(0)
        qmm(1)
        qmm(2)
        qmm(3)
        for fc in range(0, NFC, 2):
            if fc + 4 < NFC:
                qmm(fc + 4)
                qmm(fc + 5)
            if fc >= 2:
                main_mm(fc - 2, 0, True)
                main_mm(fc - 2, 1, False)
            if g_ + 1 < NG:
                scales_batch(g_ + 1, fc // 2)
            if fc == 4 and g_ >= 1:
                emit_epi(g_ - 1, (2, 3))
        main_mm(NFC - 2, 0, True)
        main_mm(NFC - 2, 1, False)
        ev[("fmt01", g_)] = bld.cnt["t"]
        emit_epi(g_, (0, 1))
        for fc in range(0, NFC, 2):
            main_mm(fc, 2, False)
            main_mm(fc, 3, False)
        ev[("fmt", g_)] = bld.cnt["t"]
        if g_ + 2 < NG:
            ev[("wfld", g_ + 2)] = G(lambda g, g_=g_: g.dma_start(
                out=wfs[g_ % 2].ap()[:, :], in_=dr["wft"][g_ + 2]),
                waits=[("t", ev[("fqt", g_, NFC - 1)]), ("v", ev[("wfs", g_, NFC - 1)])],
                sem=("dg0" if g_ % 2 == 0 else "dg1"))
        if g_ + 1 < NG:
            emit_rows(g_ + 1)
            emit_wfscale(g_ + 1)
        if g_ == NG - 1:
            emit_epi(g_ - 1, (2, 3)) if ("fepg", g_ - 1, 2) not in ev else None
            emit_epi(g_, (2, 3))

    bld.barrier()

    bld.phases.append(("FFN", dict(bld.cnt)))

    xins = [sb["xin0"], sb["xin1"]]
    outs = [sb["outsb0"], sb["outsb1"]]
    for tt in range(NOT):
        p = tt % 2
        xin, osb = xins[p], outs[p]
        dsem = "dy0" if p == 0 else "dy1"
        ssem = "ds0" if p == 0 else "ds1"
        ld = Y(lambda yq, xin=xin, tt=tt: yq.dma_start(
            out=xin.ap()[:, :], in_=dr["xs"][OWN0 + 128 * tt:OWN0 + 128 * (tt + 1), :]),
            waits=[("v", ev.get(("oev", (tt - 2) * 8 + 7), 0))], sem=dsem)
        for bq in range(8):
            gi = tt * 8 + bq
            pb = ps["b"] if gi % 2 == 0 else ps["c"]
            for j in range(4):
                fc = 4 * bq + j
                T(lambda t, fc=fc, tt=tt, pb=pb, j=j: t.matmul(
                    pb.ap()[:, 128 * j:128 * (j + 1)],
                    sb["finalT"].ap()[:, NTOK * fc + 128 * tt:NTOK * fc + 128 * (tt + 1)],
                    sb["ident_bf"].ap()[:, :], start=True, stop=True),
                    waits=([("v", ev.get(("oev", gi - 2), 0))] if j == 0 else ()),
                    inc=("t" if j == 3 else None))
            tmm = bld.cnt["t"]
            V(lambda v, osb=osb, xin=xin, bq=bq, pb=pb: v.tensor_tensor(
                osb.ap()[:, 512 * bq:512 * (bq + 1)], pb.ap()[:, :],
                xin.ap()[:, 512 * bq:512 * (bq + 1)], op=ALU.add),
                waits=[("t", tmm), (dsem, ld), (ssem, ev.get(("ost", tt - 2), 0))])
            ev[("oev", gi)] = bld.cnt["v"]
        ev[("ost", tt)] = SD(lambda s, osb=osb, tt=tt: s.dma_start(
            out=y[128 * tt:128 * (tt + 1), :], in_=osb.ap()[:, :]),
            waits=[("v", ev[("oev", tt * 8 + 7)])], sem=ssem)

    bld.phases.append(("OUT", dict(bld.cnt)))
    global PHASES
    PHASES = bld.phases

    bld.ops["s"].append(({"ds0": bld.cnt["ds0"], "ds1": bld.cnt["ds1"]}, None, None, None))

    import contextlib
    with contextlib.ExitStack() as stack:
        SEMS = {}
        for nm in ("t", "v", "s", "g") + DMA_SEMS:
            SEMS[nm] = stack.enter_context(nc.semaphore("sem_" + nm))
        block = stack.enter_context(nc.Block())

        def run(engname, eng):
            lastw = {}
            for (w, fn, inc, val) in bld.ops[engname]:
                for semn in sorted(w):
                    tgt = w[semn]
                    if tgt > lastw.get(semn, 0):
                        eng.wait_ge(SEMS[semn], tgt)
                        lastw[semn] = tgt
                if fn is None:
                    continue
                inst = fn(eng)
                if inc is not None:
                    inst.then_inc(SEMS[inc], 16 if inc in DMA_SEMS else 1)

        @block.tensor
        def _(eng):
            run("t", eng)

        @block.vector
        def _(eng):
            run("v", eng)

        @block.scalar
        def _(eng):
            run("s", eng)

        @block.gpsimd
        def _(eng):
            run("g", eng)

        @block.sync
        def _(eng):
            run("y", eng)

    return nc


def _host_prep(L0_attn_x, L0_attn_Wq, L0_attn_Wk, L0_attn_Wv, L0_attn_Wo,
               L0_ffn_weight, L0_ffn_bias, L0_ffn_hadamard_matrix):
    f32 = np.float32
    Wq = np.asarray(L0_attn_Wq, f32)
    Wk = np.asarray(L0_attn_Wk, f32)
    Wv = np.asarray(L0_attn_Wv, f32)
    Wo = np.asarray(L0_attn_Wo, f32)
    Wf = np.asarray(L0_ffn_weight, f32)
    H = np.asarray(L0_ffn_hadamard_matrix, f32)
    bias = np.asarray(L0_ffn_bias, f32)

    wqt = np.ascontiguousarray(Wq.reshape(32, 128, 32, 128).transpose(0, 3, 2, 1)).reshape(32, 128, 4096)
    wot = np.ascontiguousarray(Wo.reshape(32, 128, 32, 128).transpose(0, 3, 2, 1)).reshape(32, 128, 4096)
    wkt = np.ascontiguousarray(Wk.reshape(128, 32, 128).transpose(2, 1, 0)).reshape(128, 4096)
    wvt = np.ascontiguousarray(Wv.reshape(128, 32, 128).transpose(2, 1, 0)).reshape(128, 4096)
    wft = np.ascontiguousarray(
        Wf.reshape(16, 256, 32, 128).transpose(0, 3, 2, 1)).reshape(16, 128, 8192)
    bias_t = np.ascontiguousarray(bias.reshape(32, 128).T)
    hbd = np.zeros((128, 128), f32)
    hbd[0:64, 0:64] = H
    hbd[64:128, 64:128] = H

    S = 2048
    inv = 1.0 / 10000.0 ** (np.arange(0, 128, 2, dtype=f32) / 128.0)
    frq = np.outer(np.arange(S, dtype=f32), inv)
    cos = np.zeros((S, 128), f32); sin = np.zeros((S, 128), f32)
    cos[:, 0::2] = np.cos(frq); cos[:, 1::2] = np.cos(frq)
    sin[:, 0::2] = np.sin(frq); sin[:, 1::2] = np.sin(frq)
    cosT = np.ascontiguousarray(cos.T)
    sin_modT = np.ascontiguousarray(sin.T)
    sin_modT[0:64, :] = -sin_modT[0:64, :]
    sin_modT = np.ascontiguousarray(np.concatenate([sin_modT[64:128], sin_modT[0:64]], axis=0))

    alpha_lin = np.linspace(0.0, 1.0, 256, dtype=f32)

    def _mk_selw():
        s = np.zeros((66, 256), f32)
        for bh in (0, 32, 64):
            s[bh + 0, 0:128] = 1.0
            s[bh + 1, 128:256] = 1.0
        return s

    consts = dict(
        bias_t=bias_t, hbd=hbd,
        ident=np.eye(128, dtype=f32),
        onesr=np.ones((1, 128), f32),
        selw=_mk_selw(),
        wqt=wqt, wkt=wkt, wvt=wvt, wot=wot, wft=wft,
    )

    x = np.asarray(L0_attn_x, f32)
    per_core = []
    for core in range(8):
        b, half = core // 2, core % 2
        xs = np.zeros((SROWS, D), f32)
        kc = np.zeros((128, SROWS), f32); ks = np.zeros((128, SROWS), f32)
        if half == 0:
            xs[256:1536] = x[b, 0:1280]
            kc[:, 256:1536] = cosT[:, 0:1280]; ks[:, 256:1536] = sin_modT[:, 0:1280]
        else:
            xs[0:1280] = x[b, 768:2048]
            kc[:, 0:1280] = cosT[:, 768:2048]; ks[:, 0:1280] = sin_modT[:, 768:2048]
        q0 = 1024 * half
        qc = (cosT[:, q0:q0 + 1024] * f32(SCALE)).astype(f32)
        qs = (sin_modT[:, q0:q0 + 1024] * f32(SCALE)).astype(f32)
        alphat = np.zeros((66, 1024), f32)
        for rr in range(4):
            r_abs = rr + 4 * half
            if r_abs == 0:
                om, al = np.zeros(256, f32), np.ones(256, f32)
            else:
                om, al = (1.0 - alpha_lin).astype(f32), alpha_lin
            for bh in (0, 32, 64):
                alphat[bh + 0, 256 * rr:256 * (rr + 1)] = om
                alphat[bh + 1, 256 * rr:256 * (rr + 1)] = al
        rsmask = np.zeros((128, 48), f32)
        for rr in range(4):
            for j in range(6):
                if j <= 3:
                    rsmask[:, rr * 12 + 2 * j] = 1.0
                if j >= 2:
                    dead = half == 1 and rr == 3 and j >= 4
                    rsmask[:, rr * 12 + 2 * j + 1] = 0.0 if dead else 1.0
        m = dict(consts)
        m.update(xs=np.ascontiguousarray(xs), ropekc=np.ascontiguousarray(kc),
                 ropeks=np.ascontiguousarray(ks), ropeqc=np.ascontiguousarray(qc),
                 ropeqs=np.ascontiguousarray(qs), alphat=alphat, rsmask=rsmask)
        per_core.append(m)
    return per_core


def kernel(L0_attn_x, L0_attn_Wq, L0_attn_Wk, L0_attn_Wv, L0_attn_Wo,
           L0_ffn_weight, L0_ffn_bias, L0_ffn_hadamard_matrix):
    in_maps = _host_prep(L0_attn_x, L0_attn_Wq, L0_attn_Wk, L0_attn_Wv,
                         L0_attn_Wo, L0_ffn_weight, L0_ffn_bias,
                         L0_ffn_hadamard_matrix)
    nc = build_nc()
    res = run_bass_kernel_spmd(nc, in_maps, core_ids=list(range(8)))
    out = np.empty((4, 2048, 4096), np.float32)
    for core in range(8):
        b, half = core // 2, core % 2
        out[b, 1024 * half:1024 * (half + 1)] = res.results[core]["y"]
    return out


# version 25 (final)
# speedup vs baseline: 1.0724x; 1.0441x over previous
import numpy as np
import concourse.bass as bass
import concourse.mybir as mybir
from concourse.bass_utils import run_bass_kernel_spmd

F32 = mybir.dt.float32
BF16 = mybir.dt.bfloat16
F8 = mybir.dt.float8e4
ALU = mybir.AluOpType
ACTF = mybir.ActivationFunctionType
AX = mybir.AxisListType

D = 4096
NH = 32
NFC = 32
SROWS = 1536
NTOK = 1024
OWN0 = 256
NTT = 12
NOT = 8
NG = 16
SCALE = 1.0 / float(np.sqrt(128.0))
EPS = float(np.float32(1e-5))
INV224 = float(np.float32(1.0) / np.float32(224.0))
CLAMP = float(np.float32(2.0) * np.float32(1e-12))
INVD = float(np.float32(1.0) / np.float32(D))

ENGINES = ("t", "v", "s", "g", "y")
DMA_SEMS = ("dgc", "dg0", "dg1", "dyc", "dy0", "dy1", "ds0", "ds1")


class B:
    def __init__(self):
        self.ops = {e: [] for e in ENGINES}
        self.cnt = {"t": 0, "v": 0, "s": 0, "g": 0, "dgc": 0, "dg0": 0, "dg1": 0,
                    "dyc": 0, "dy0": 0, "dy1": 0, "ds0": 0, "ds1": 0}
        self.pending = {e: {} for e in ENGINES}

    def emit(self, eng, fn, waits=(), inc=None):
        w = dict(self.pending[eng])
        self.pending[eng] = {}
        for sem, val in waits:
            if val and val > 0:
                w[sem] = max(w.get(sem, 0), val)
        if eng in ("v", "s", "g") and inc not in DMA_SEMS:
            prev = self.cnt[eng]
            if prev > 0:
                w[eng] = max(w.get(eng, 0), prev)
            inc = eng
        val = None
        if inc is not None:
            amt = 16 if inc in DMA_SEMS else 1
            self.cnt[inc] += amt
            val = self.cnt[inc]
        self.ops[eng].append((w, fn, inc, val))
        return val

    def barrier(self):
        for e in ENGINES:
            for k, v in self.cnt.items():
                if v > 0:
                    self.pending[e][k] = max(self.pending[e].get(k, 0), v)


PHASES = []


def build_nc():
    nc = bass.Bass("TRN2", target_bir_lowering=False)

    dr = {}
    def din(name, shape, dtype=F32):
        dr[name] = nc.dram_tensor(name, list(shape), dtype, kind="ExternalInput").ap()
    din("xs", (SROWS, D))
    din("wqt", (NH, 128, D))
    din("wkt", (128, D))
    din("wvt", (128, D))
    din("wot", (NH, 128, D))
    din("wft", (NG, 128, 8192))
    din("bias_t", (128, 32))
    din("hbd", (128, 128))
    din("ident", (128, 128))
    din("onesr", (1, 128))
    din("ropekc", (128, SROWS))
    din("ropeks", (128, SROWS))
    din("ropeqc", (128, NTOK))
    din("ropeqs", (128, NTOK))
    din("alphat", (66, 1024))
    din("rsmask", (128, 48))
    din("selw", (66, 256))
    y = nc.dram_tensor("y", [NTOK, D], F32, kind="ExternalOutput").ap()

    BASE = 16544
    A0, B0, C0, D0 = BASE, BASE + 98304, BASE + 163840, BASE + 196608
    LIMIT = 229376
    sb = {}
    def at(name, shape, dtype, off):
        t = nc.alloc_sbuf_tensor_at(name, list(shape), dtype, offset=off)
        sb[name] = t
        return t

    at("xnT", (128, NFC * SROWS), BF16, A0)
    at("attnT", (128, NH * NTOK), BF16, A0)
    at("sigT0", (128, 1536), BF16, A0 + 65536)
    at("sigT1", (128, 1536), BF16, A0 + 68608)
    at("cabc0", (128, 512), BF16, A0 + 71680)
    at("cabc1", (128, 512), BF16, A0 + 72704)
    at("alphat_sb", (66, 1024), F32, A0 + 73728)
    at("rsrp", (66, 512), F32, A0 + 93184)
    for _i in range(4):
        at(f"cacolp{_i}", (66, 256), BF16, A0 + 78848 + 512 * _i)
    at("tmpbl", (128, 256), BF16, A0 + 90112)
    at("tmpbl2", (128, 256), BF16, A0 + 90624)
    at("av_sb2", (128, 512), BF16, A0 + 91136)
    at("av_sb3", (128, 512), BF16, A0 + 92160)
    at("av_sb0", (128, 512), BF16, A0 + 95264)
    at("av_sb1", (128, 512), BF16, A0 + 96288)
    at("wo0", (128, 4096), BF16, A0 + 65536)
    at("wo1", (128, 4096), BF16, A0 + 90112)
    at("wf0", (128, 8192), BF16, A0 + 73728)
    at("wf1", (128, 8192), BF16, A0)
    at("wq8", (128, 8192), F8, A0 + 16384)
    at("swnat", (128, 2), F32, A0 + 24576)
    at("sws", (128, 2), F32, A0 + 24640)
    at("recw", (128, 2), F32, A0 + 24704)
    at("wtmp", (128, 1), F32, A0 + 24768)
    at("wrow", (1, 256), F32, A0 + 24832)
    at("recipw_bc", (128, 256), BF16, A0 + 25856)
    at("ub", (128, 512), BF16, A0 + 26880)
    at("ub1", (128, 512), BF16, A0 + 30208)
    at("wstage0", (128, 512), BF16, A0 + 32768)
    at("wstage1", (128, 512), BF16, A0 + 33792)
    at("runmax0", (128, 512), BF16, A0 + 34816)
    at("runmax1", (128, 512), BF16, A0 + 35840)
    at("swnat1", (128, 2), F32, A0 + 27904)
    at("sws1", (128, 2), F32, A0 + 27968)
    at("recw1", (128, 2), F32, A0 + 28032)
    at("wtmp1", (128, 1), F32, A0 + 28096)
    at("wrow1", (1, 256), F32, A0 + 28160)
    at("recipw_bc1", (128, 256), BF16, A0 + 29184)
    at("xin0", (128, 4096), F32, A0)
    at("xin1", (128, 4096), F32, A0 + 16384)
    at("outsb0", (128, 4096), F32, A0 + 32768)
    at("outsb1", (128, 4096), F32, A0 + 49152)
    at("QrT", (128, NH * NTOK), BF16, B0)
    at("finalT", (128, NH * NTOK), BF16, B0)
    at("xt0", (128, 4096), F32, B0)
    at("xt1", (128, 4096), F32, B0 + 16384)
    at("xnat0", (128, 4096), BF16, B0 + 32768)
    at("xnat1", (128, 4096), BF16, B0 + 40960)
    at("stats0", (128, 16), F32, B0 + 49152)
    at("stats1", (128, 16), F32, B0 + 49216)
    at("wv_sb", (128, 4096), BF16, B0 + 49280)
    at("xq8", (128, NFC * NTOK), F8, C0)
    at("rqc", (128, NTOK), BF16, C0)
    at("rqs", (128, NTOK), BF16, C0 + 2048)
    at("wqh0", (128, 4096), BF16, C0 + 4096)
    at("wqh1", (128, 4096), BF16, C0 + 12288)
    at("q_tmp", (128, 1024), BF16, C0 + 20480)
    at("tmp2", (128, 1024), BF16, C0 + 22528)
    at("wk_sb", (128, 4096), BF16, C0 + 24576)
    at("KrT", (128, SROWS), BF16, D0)
    at("Vn", (128, SROWS), BF16, D0 + 3072)
    at("rkc", (128, SROWS), BF16, D0 + 6144)
    at("rks", (128, SROWS), BF16, D0 + 9216)
    at("sxp_bc", (128, 1024), F32, D0 + 6144)
    at("recipx_bc", (128, 1024), F32, D0 + 10240)
    at("xrow_s", (1, 1024), F32, D0 + 6144)
    at("xrow_r", (1, 1024), F32, D0 + 10240)
    at("snrt", (128, 16), F32, D0 + 14336)
    at("xmaxn", (128, 8), F32, D0 + 14400)
    at("xtmp1", (128, 1), F32, D0 + 14432)
    at("ident_bf", (128, 128), BF16, D0 + 14464)
    at("hbd_bf", (128, 128), BF16, D0 + 14720)
    at("ident_f", (128, 128), F32, D0 + 14976)
    at("onesr_f", (1, 128), F32, D0 + 15488)
    at("onesr_bf", (1, 128), BF16, A0 + 81920)
    at("selw_bf", (66, 256), BF16, A0 + 80896)
    at("rsmask_sb", (128, 48), BF16, A0 + 82176)
    at("bias_sb", (128, 32), F32, D0 + 16000)
    assert D0 + 16128 + 32 <= LIMIT

    ps = {}
    ps["a"] = nc.alloc_psum_tensor("ps_a", [128, 1536], F32)
    for nm in ("b", "c", "d", "e", "f"):
        ps[nm] = nc.alloc_psum_tensor("ps_" + nm, [128, 512], F32)

    bld = B()
    bld.phases = []
    ev = {}
    T = lambda fn, waits=(), inc=None: bld.emit("t", fn, waits, inc)
    V = lambda fn, waits=(): bld.emit("v", fn, waits)
    S = lambda fn, waits=(): bld.emit("s", fn, waits)
    G = lambda fn, waits=(), sem="dgc": bld.emit("g", fn, waits, sem)
    GC = lambda fn, waits=(): bld.emit("g", fn, waits, None)
    Y = lambda fn, waits=(), sem="dyc": bld.emit("y", fn, waits, sem)
    SD = lambda fn, waits=(), sem="ds0": bld.emit("s", fn, waits, sem)

    xnT = sb["xnT"].ap().rearrange("p (f t) -> p f t", f=NFC)
    xq8v = sb["xq8"].ap().rearrange("p (f t) -> p f t", f=NFC)

    for dst, src in [("ident_bf", "ident"), ("hbd_bf", "hbd"), ("rkc", "ropekc"),
                     ("rks", "ropeks"), ("rqc", "ropeqc"), ("rqs", "ropeqs"),
                     ("wk_sb", "wkt"), ("wv_sb", "wvt")]:
        G(lambda g, dst=dst, src=src: g.dma_start(out=sb[dst].ap()[:, :], in_=dr[src]))
    cw0 = bld.cnt["dgc"]
    for dst, src in [("ident_f", "ident"), ("onesr_f", "onesr")]:
        Y(lambda yq, dst=dst, src=src: yq.dma_start(out=sb[dst].ap()[:, :], in_=dr[src]))
    cy0 = bld.cnt["dyc"]

    xts = [sb["xt0"], sb["xt1"]]
    xnats = [sb["xnat0"], sb["xnat1"]]
    statss = [sb["stats0"], sb["stats1"]]
    def p0_stats(tt):
        p = tt % 2
        xt, xnat, st = xts[p], xnats[p], statss[p]
        dsem = "dy0" if p == 0 else "dy1"
        ld = Y(lambda yq, xt=xt, tt=tt: yq.dma_start(
            out=xt.ap()[:, :], in_=dr["xs"][128 * tt:128 * (tt + 1), :]),
            waits=[("s", ev.get(("p0norm", tt - 2), 0))], sem=dsem)
        ev[("p0sum", tt)] = V(lambda v, xt=xt, st=st: v.reduce_sum(
            st.ap()[:, 0:1], xt.ap()[:, :], axis=AX.X),
            waits=[(dsem, ld), ("s", ev.get(("p0norm", tt - 2), 0))])
        ev[("p0sq", tt)] = S(lambda s, xt=xt, xnat=xnat, st=st: s.activation(
            xnat.ap()[:, :], xt.ap()[:, :], ACTF.Square, accum_out=st.ap()[:, 1:2]),
            waits=[(dsem, ld), ("t", ev.get(("p0t", tt - 2), 0)),
                   ("v", ev.get(("p0sum", tt), 0))])
        V(lambda v, st=st: v.tensor_scalar(st.ap()[:, 2:3], st.ap()[:, 0:1], INVD, None, op0=ALU.mult))
        V(lambda v, st=st: v.tensor_tensor(st.ap()[:, 3:4], st.ap()[:, 2:3], st.ap()[:, 2:3], op=ALU.mult))
        V(lambda v, st=st: v.tensor_scalar(st.ap()[:, 4:5], st.ap()[:, 1:2], INVD, EPS, op0=ALU.mult, op1=ALU.add),
          waits=[("s", ev[("p0sq", tt)])])
        ev[("p0var", tt)] = V(lambda v, st=st: v.tensor_tensor(
            st.ap()[:, 5:6], st.ap()[:, 4:5], st.ap()[:, 3:4], op=ALU.subtract))
        ev[("p0sqrt", tt)] = S(lambda s, st=st: s.activation(st.ap()[:, 6:7], st.ap()[:, 5:6], ACTF.Sqrt),
                               waits=[("v", ev[("p0var", tt)])])
        V(lambda v, st=st: v.reciprocal(st.ap()[:, 7:8], st.ap()[:, 6:7]),
          waits=[("s", ev[("p0sqrt", tt)])])
        ev[("p0rstd", tt)] = V(lambda v, st=st: v.tensor_scalar(
            st.ap()[:, 8:9], st.ap()[:, 2:3], st.ap()[:, 7:8], -1.0,
            op0=ALU.mult, op1=ALU.mult))
        ev[("p0norm", tt)] = S(lambda s, xt=xt, xnat=xnat, st=st: s.activation(
            xnat.ap()[:, :], xt.ap()[:, :], ACTF.Identity, bias=st.ap()[:, 8:9], scale=st.ap()[:, 7:8]),
            waits=[("v", ev[("p0rstd", tt)])])

    def p0_trans(tt):
        p = tt % 2
        xnat = xnats[p]
        for bq in range(8):
            gi = tt * 8 + bq
            pb = ps["b"] if gi % 2 == 0 else ps["c"]
            for j in range(4):
                fc = 4 * bq + j
                T(lambda t, xnat=xnat, fc=fc, pb=pb, j=j: t.matmul(
                    pb.ap()[:, 128 * j:128 * (j + 1)],
                    xnat.ap()[:, 128 * fc:128 * (fc + 1)],
                    sb["ident_bf"].ap()[:, :], start=True, stop=True),
                    waits=([("s", ev[("p0norm", tt)]), ("dgc", cw0),
                            (("v" if gi % 2 == 0 else "s"), ev.get(("p0ev", gi - 2), 0))] if j == 0 else ()),
                    inc=("t" if j == 3 else None))
            tmm = bld.cnt["t"]
            if bq % 2 == 0:
                ev[("p0ev", gi)] = V(lambda v, tt=tt, bq=bq, pb=pb: v.tensor_copy(
                    xnT[:, 4 * bq:4 * bq + 4, 128 * tt:128 * (tt + 1)],
                    pb.ap().rearrange("p (a b) -> p a b", a=4)[:, :, :]),
                    waits=[("t", tmm)])
            else:
                ev[("p0ev", gi)] = S(lambda s, tt=tt, bq=bq, pb=pb: s.copy(
                    xnT[:, 4 * bq:4 * bq + 4, 128 * tt:128 * (tt + 1)],
                    pb.ap().rearrange("p (a b) -> p a b", a=4)[:, :, :]),
                    waits=[("t", tmm)])
        ev[("p0t", tt)] = bld.cnt["t"]

    for tt in range(NTT + 1):
        if tt < NTT:
            p0_stats(tt)
        if tt >= 1:
            p0_trans(tt - 1)

    bld.barrier()

    bld.phases.append(("P0", dict(bld.cnt)))

    for ts3 in range(3):
        for fc in range(NFC):
            T(lambda t, ts3=ts3, fc=fc: t.matmul(
                ps["a"].ap()[:, 512 * ts3:512 * (ts3 + 1)],
                sb["wk_sb"].ap()[:, 128 * fc:128 * (fc + 1)],
                xnT[:, fc, 512 * ts3:512 * (ts3 + 1)],
                start=(fc == 0), stop=(fc == NFC - 1)),
                inc=("t" if fc == NFC - 1 else None))
        ev[("kt", ts3)] = bld.cnt["t"]
        off = 512 * (ts3 % 2)
        ev[("kev", ts3)] = S(lambda s, ts3=ts3, off=off: s.copy(
            sb["q_tmp"].ap()[:, off:off + 512],
            ps["a"].ap()[:, 512 * ts3:512 * (ts3 + 1)]),
            waits=[("t", ev[("kt", ts3)]), ("v", ev.get(("krope", ts3 - 2), 0))])
        qt = sb["q_tmp"].ap()
        V(lambda v, off=off, ts3=ts3: v.tensor_tensor(
            sb["tmp2"].ap()[0:64, 0:512], qt[64:128, off:off + 512],
            sb["rks"].ap()[64:128, 512 * ts3:512 * (ts3 + 1)], op=ALU.mult),
            waits=[("s", ev[("kev", ts3)])])
        V(lambda v, off=off, ts3=ts3: v.tensor_tensor(
            sb["tmp2"].ap()[64:128, 0:512], qt[0:64, off:off + 512],
            sb["rks"].ap()[0:64, 512 * ts3:512 * (ts3 + 1)], op=ALU.mult))
        V(lambda v, off=off, ts3=ts3: v.tensor_tensor(
            sb["KrT"].ap()[:, 512 * ts3:512 * (ts3 + 1)], qt[:, off:off + 512],
            sb["rkc"].ap()[:, 512 * ts3:512 * (ts3 + 1)], op=ALU.mult))
        ev[("krope", ts3)] = V(lambda v, ts3=ts3: v.tensor_tensor(
            sb["KrT"].ap()[:, 512 * ts3:512 * (ts3 + 1)],
            sb["KrT"].ap()[:, 512 * ts3:512 * (ts3 + 1)],
            sb["tmp2"].ap()[:, 0:512], op=ALU.add))
    for tg in range(3):
        pb = ps["b"] if tg % 2 == 0 else ps["c"]
        for fc in range(NFC):
            for tt4 in range(4):
                tt = 4 * tg + tt4
                T(lambda t, fc=fc, tt=tt, tt4=tt4, pb=pb: t.matmul(
                    pb.ap()[:, 128 * tt4:128 * (tt4 + 1)],
                    xnT[:, fc, 128 * tt:128 * (tt + 1)],
                    sb["wv_sb"].ap()[:, 128 * fc:128 * (fc + 1)],
                    start=(fc == 0 and tt4 == 0), stop=(fc == NFC - 1 and tt4 == 3)),
                    waits=([("v", ev.get(("vev", tg - 2), 0))] if (fc == 0 and tt4 == 0) else ()),
                    inc=("t" if (fc == NFC - 1 and tt4 == 3) else None))
        ev[("vt", tg)] = bld.cnt["t"]
        ev[("vev", tg)] = V(lambda v, tg=tg, pb=pb: v.tensor_copy(
            sb["Vn"].ap()[:, 512 * tg:512 * (tg + 1)], pb.ap()[:, :]),
            waits=[("t", ev[("vt", tg)])])

    bld.barrier()

    bld.phases.append(("KV", dict(bld.cnt)))

    wqhs = [sb["wqh0"], sb["wqh1"]]
    for h in range(NH):
        wq = wqhs[h % 2]
        qsem = "dg0" if h % 2 == 0 else "dg1"
        ld = G(lambda g, h=h, wq=wq: g.dma_start(out=wq.ap()[:, :], in_=dr["wqt"][h]),
               waits=[("t", ev.get(("qt", h - 2, 1), 0))], sem=qsem)
        for ts in range(2):
            slot = (2 * h + ts) % 4
            pq, off = [(ps["a"], 0), (ps["a"], 512), (ps["b"], 0), (ps["c"], 0)][slot]
            for fc in range(NFC):
                T(lambda t, wq=wq, fc=fc, ts=ts, pq=pq, off=off: t.matmul(
                    pq.ap()[:, off:off + 512],
                    wq.ap()[:, 128 * fc:128 * (fc + 1)],
                    xnT[:, fc, OWN0 + 512 * ts:OWN0 + 512 * (ts + 1)],
                    start=(fc == 0), stop=(fc == NFC - 1)),
                    waits=([(qsem, ld), ("s", ev.get(("qev", h - 2, ts), 0))] if fc == 0 else ()),
                    inc=("t" if fc == NFC - 1 else None))
            ev[("qt", h, ts)] = bld.cnt["t"]
            ev[("qev", h, ts)] = S(lambda s, ts=ts, pq=pq, off=off: s.copy(
                sb["q_tmp"].ap()[:, 512 * ts:512 * (ts + 1)], pq.ap()[:, off:off + 512]),
                waits=[("t", ev[("qt", h, ts)]), ("v", ev.get(("qrope", h - 1), 0))])
        qt = sb["q_tmp"].ap()
        dstq = sb["QrT"].ap()[:, NTOK * h:NTOK * (h + 1)]
        V(lambda v, qt=qt: v.tensor_tensor(
            sb["tmp2"].ap()[0:64, :], qt[64:128, :], sb["rqs"].ap()[64:128, :], op=ALU.mult),
            waits=[("s", ev[("qev", h, 1)])])
        V(lambda v, qt=qt: v.tensor_tensor(
            sb["tmp2"].ap()[64:128, :], qt[0:64, :], sb["rqs"].ap()[0:64, :], op=ALU.mult))
        V(lambda v, qt=qt, dstq=dstq: v.tensor_tensor(dstq, qt, sb["rqc"].ap()[:, :], op=ALU.mult))
        ev[("qrope", h)] = V(lambda v, dstq=dstq: v.tensor_tensor(
            dstq, dstq, sb["tmp2"].ap()[:, :], op=ALU.add))

    bld.barrier()

    bld.phases.append(("Q", dict(bld.cnt)))

    for tt in range(NOT):
        for bq in range(8):
            gi = tt * 8 + bq
            pb = ps["b"] if gi % 2 == 0 else ps["c"]
            for j in range(4):
                fc = 4 * bq + j
                T(lambda t, fc=fc, tt=tt, pb=pb, j=j: t.matmul(
                    pb.ap()[:, 128 * j:128 * (j + 1)],
                    xnT[:, fc, OWN0 + 128 * tt:OWN0 + 128 * (tt + 1)],
                    sb["hbd_bf"].ap()[:, :], start=True, stop=True),
                    waits=([("v", ev.get(("xsc", gi - 2), 0))] if j == 0 else ()),
                    inc=("t" if j == 3 else None))
            tmm = bld.cnt["t"]
            if bq == 0:
                V(lambda v, tt=tt, pb=pb: v.reduce_max(
                    sb["xmaxn"].ap()[:, tt:tt + 1],
                    pb.ap().rearrange("p (a b) -> p a b", a=4)[:, :, :],
                    axis=AX.XY, apply_absolute_value=True), waits=[("t", tmm)])
            else:
                V(lambda v, pb=pb: v.reduce_max(
                    sb["xtmp1"].ap()[:, 0:1],
                    pb.ap().rearrange("p (a b) -> p a b", a=4)[:, :, :],
                    axis=AX.XY, apply_absolute_value=True), waits=[("t", tmm)])
                V(lambda v, tt=tt: v.tensor_tensor(
                    sb["xmaxn"].ap()[:, tt:tt + 1], sb["xmaxn"].ap()[:, tt:tt + 1],
                    sb["xtmp1"].ap()[:, 0:1], op=ALU.max))
            ev[("xsc", gi)] = bld.cnt["v"]
        V(lambda v, tt=tt: v.tensor_scalar(
            sb["snrt"].ap()[:, 2 * tt:2 * tt + 1], sb["xmaxn"].ap()[:, tt:tt + 1],
            INV224, CLAMP, op0=ALU.mult, op1=ALU.max))
        ev[("xsn", tt)] = V(lambda v, tt=tt: v.reciprocal(
            sb["snrt"].ap()[:, 2 * tt + 1:2 * tt + 2], sb["snrt"].ap()[:, 2 * tt:2 * tt + 1]))
    for tt in range(NOT):
        T(lambda t, tt=tt: t.matmul(
            ps["a"].ap()[0:1, 128 * tt:128 * (tt + 1)],
            sb["snrt"].ap()[:, 2 * tt:2 * tt + 1],
            sb["ident_f"].ap()[:, :], start=True, stop=True),
            waits=([("v", ev[("xsn", NOT - 1)]), ("dyc", cy0),
                    ("v", ev[("xsc", NOT * 8 - 1)])] if tt == 0 else ()))
        prow = ps["b"] if tt < 4 else ps["c"]
        T(lambda t, tt=tt, prow=prow: t.matmul(
            prow.ap()[0:1, 128 * (tt % 4):128 * (tt % 4) + 128],
            sb["snrt"].ap()[:, 2 * tt + 1:2 * tt + 2],
            sb["ident_f"].ap()[:, :], start=True, stop=True),
            inc=("t" if tt == NOT - 1 else None))
    ev["xrt"] = bld.cnt["t"]
    ev["xrow_s_ev"] = V(lambda v: v.tensor_copy(sb["xrow_s"].ap()[0:1, :], ps["a"].ap()[0:1, 0:1024]),
                        waits=[("t", ev["xrt"])])
    V(lambda v: v.tensor_copy(sb["xrow_r"].ap()[0:1, 0:512], ps["b"].ap()[0:1, 0:512]))
    ev["xrow_r_ev"] = V(lambda v: v.tensor_copy(sb["xrow_r"].ap()[0:1, 512:1024], ps["c"].ap()[0:1, 0:512]))
    for c2 in range(2):
        pd = ps["d"] if c2 == 0 else ps["e"]
        T(lambda t, c2=c2, pd=pd: t.matmul(
            pd.ap()[:, :], sb["onesr_f"].ap()[0:1, :],
            sb["xrow_s"].ap()[0:1, 512 * c2:512 * (c2 + 1)], start=True, stop=True),
            waits=[("v", ev["xrow_s_ev"])])
        pb = ps["b"] if c2 == 0 else ps["c"]
        T(lambda t, c2=c2, pb=pb: t.matmul(
            pb.ap()[:, :], sb["onesr_f"].ap()[0:1, :],
            sb["xrow_r"].ap()[0:1, 512 * c2:512 * (c2 + 1)], start=True, stop=True),
            waits=[("v", ev["xrow_r_ev"])],
            inc=("t" if c2 == 1 else None))
    xbc_t = bld.cnt["t"]
    V(lambda v: v.tensor_copy(sb["sxp_bc"].ap()[:, 0:512], ps["d"].ap()[:, :]), waits=[("t", xbc_t)])
    V(lambda v: v.tensor_copy(sb["sxp_bc"].ap()[:, 512:1024], ps["e"].ap()[:, :]))
    V(lambda v: v.tensor_copy(sb["recipx_bc"].ap()[:, 0:512], ps["b"].ap()[:, :]))
    ev["xbc_ev"] = V(lambda v: v.tensor_copy(sb["recipx_bc"].ap()[:, 512:1024], ps["c"].ap()[:, :]))
    for fc in range(NFC):
        for ts in range(2):
            k = 2 * fc + ts
            pd = ps["d"] if k % 2 == 0 else ps["e"]
            tmm = T(lambda t, fc=fc, ts=ts, pd=pd: t.matmul(
                pd.ap()[:, :], sb["hbd_bf"].ap()[:, :],
                xnT[:, fc, OWN0 + 512 * ts:OWN0 + 512 * (ts + 1)], start=True, stop=True),
                waits=[("v", ev.get(("xq", k - 2), ev["xbc_ev"]))], inc="t")
            V(lambda v, fc=fc, ts=ts, pd=pd: v.tensor_tensor(
                xq8v[:, fc, 512 * ts:512 * (ts + 1)], pd.ap()[:, :],
                sb["recipx_bc"].ap()[:, 512 * ts:512 * (ts + 1)], op=ALU.mult),
                waits=[("t", tmm)])
            ev[("xq", k)] = bld.cnt["v"]

    latew = [("t", bld.cnt["t"]), ("v", bld.cnt["v"])]
    for dst, src in [("onesr_bf", "onesr"), ("rsmask_sb", "rsmask"), ("selw_bf", "selw")]:
        G(lambda g, dst=dst, src=src: g.dma_start(out=sb[dst].ap()[:, :], in_=dr[src]),
          waits=latew)
    for dst, src in [("alphat_sb", "alphat"), ("bias_sb", "bias_t")]:
        Y(lambda yq, dst=dst, src=src: yq.dma_start(out=sb[dst].ap()[:, :], in_=dr[src]),
          waits=latew)

    bld.barrier()

    bld.phases.append(("XQ", dict(bld.cnt)))

    sigTs = [sb["sigT0"], sb["sigT1"]]
    cabcs = [sb["cabc0"], sb["cabc1"]]
    cacolps = [sb[f"cacolp{_i}"] for _i in range(4)]
    av_sbs = [sb["av_sb0"], sb["av_sb1"], sb["av_sb2"], sb["av_sb3"]]

    def att_scores(hs):
        rr, h = hs // 32, hs % 32
        L0 = 256 * rr
        for j in range(6):
            T(lambda t, h=h, j=j, L0=L0, rr=rr: t.matmul(
                ps["a"].ap()[:, 256 * j:256 * (j + 1)],
                sb["KrT"].ap()[:, L0 + 128 * j:L0 + 128 * (j + 1)],
                sb["QrT"].ap()[:, NTOK * h + 256 * rr:NTOK * h + 256 * (rr + 1)],
                start=True, stop=True),
                waits=([("s", ev.get(("sig", hs - 1), 0))] if j == 0 else ()),
                inc=("t" if j == 5 else None))
        ev[("sct", hs)] = bld.cnt["t"]
        sigT = sigTs[hs % 2].ap()
        ev[("sig", hs)] = S(lambda s, sigT=sigT: s.activation(
            sigT[:, :], ps["a"].ap()[:, :], ACTF.Sigmoid),
            waits=[("t", ev[("sct", hs)]), ("t", ev.get(("avt", hs - 2), 0))])

    def att_cons1(hs):
        rr, h = hs // 32, hs % 32
        L0 = 256 * rr
        par = hs % 2
        sigT = sigTs[par].ap()
        b3, m3 = hs // 3, hs % 3
        prs = ps["d"] if b3 % 2 == 0 else ps["f"]
        base = 32 * m3
        for j in range(6):
            T(lambda t, sigT=sigT, j=j, rr=rr, prs=prs, base=base: t.matmul(
                prs.ap()[base:base + 2, 0:256],
                sb["rsmask_sb"].ap()[:, 12 * rr + 2 * j:12 * rr + 2 * j + 2],
                sigT[:, 256 * j:256 * (j + 1)],
                start=(j == 0), stop=(j == 5)),
                waits=([("s", ev[("sig", hs)]),
                        ("v", ev.get(("recb", b3 - 2), 0))] if j == 0 else ()),
                inc=("t" if j == 5 else None))
        ev[("rst", hs)] = bld.cnt["t"]
        pav = ps["b"] if par == 0 else ps["c"]
        for j in range(6):
            ops = []
            if j <= 3:
                ops.append((0, j == 0, False))
            if j >= 2:
                ops.append((1, False, j == 5))
            for (w2, st_, sp_) in ops:
                T(lambda t, j=j, w2=w2, st_=st_, sp_=sp_, sigT=sigT, pav=pav, L0=L0: t.matmul(
                    pav.ap()[:, 256 * w2:256 * (w2 + 1)],
                    sb["Vn"].ap()[:, L0 + 128 * j:L0 + 128 * (j + 1)],
                    sigT[:, 256 * j:256 * (j + 1)],
                    start=st_, stop=sp_),
                    waits=([("s", ev.get(("avec", hs - 2), 0))] if (j == 0 and w2 == 0) else ()),
                    inc=("t" if (w2 == 1 and sp_) else None))
        ev[("avt", hs)] = bld.cnt["t"]
        ev[("avec", hs)] = S(lambda s, pav=pav, hs4=hs % 4: s.copy(
            av_sbs[hs4].ap()[:, :], pav.ap()[:, :]),
            waits=[("t", ev[("avt", hs)]), ("v", ev.get(("bl", hs - 4), 0))])
        if m3 == 2 or hs == 127:
            ro = 256 * (b3 % 2)
            V(lambda v, prs=prs, ro=ro: v.reciprocal(
                sb["rsrp"].ap()[0:66, ro:ro + 256], prs.ap()[0:66, 0:256]),
                waits=[("t", ev[("rst", hs)])])
            ev[("recb", b3)] = bld.cnt["v"]
            for hh in range(3 * b3, min(3 * b3 + 3, 128)):
                bh = 32 * (hh % 3)
                rrh = hh // 32
                ev[("ca", hh)] = V(lambda v, rrh=rrh, h4=hh % 4, ro=ro, bh=bh:
                    v.tensor_tensor(
                        cacolps[h4].ap()[bh:bh + 2, :],
                        sb["rsrp"].ap()[bh:bh + 2, ro:ro + 256],
                        sb["alphat_sb"].ap()[bh:bh + 2, 256 * rrh:256 * (rrh + 1)],
                        op=ALU.mult),
                    waits=[("t", ev.get(("bctt", hh - 4), 0))])

    def att_cons2_pe(hs):
        cac = cacolps[hs % 4].ap()
        bh = 32 * (hs % 3)
        for w2 in range(2):
            T(lambda t, w2=w2, cac=cac, bh=bh: t.matmul(
                ps["e"].ap()[:, 256 * w2:256 * (w2 + 1)],
                sb["selw_bf"].ap()[bh:bh + 2, 128 * w2:128 * (w2 + 1)],
                cac[bh:bh + 2, 0:256],
                start=True, stop=True),
                waits=([("v", ev[("ca", hs)]), ("s", ev.get(("cab", hs - 1), 0))]
                       if w2 == 0 else ()),
                inc=("t" if w2 == 1 else None))
        ev[("bctt", hs)] = bld.cnt["t"]

    def att_cons2(hs):
        rr, h = hs // 32, hs % 32
        par = hs % 2
        cab = cabcs[par].ap()
        ev[("cab", hs)] = S(lambda s, cab=cab: s.copy(cab[:, :], ps["e"].ap()[:, :]),
                            waits=[("t", ev[("bctt", hs)]), ("v", ev.get(("bl", hs - 2), 0))])
        avs = av_sbs[hs % 4].ap()
        dsta = sb["attnT"].ap()[:, NTOK * h + 256 * rr:NTOK * h + 256 * (rr + 1)]
        V(lambda v, avs=avs, cab=cab: v.tensor_tensor(
            sb["tmpbl"].ap()[:, :], avs[:, 0:256], cab[:, 0:256], op=ALU.mult),
            waits=[("s", ev[("cab", hs)])])
        V(lambda v, avs=avs, cab=cab: v.tensor_tensor(
            sb["tmpbl2"].ap()[:, :], avs[:, 256:512], cab[:, 256:512], op=ALU.mult))
        V(lambda v, dsta=dsta: v.tensor_tensor(
            dsta, sb["tmpbl"].ap()[:, :], sb["tmpbl2"].ap()[:, :], op=ALU.add))
        ev[("bl", hs)] = bld.cnt["v"]

    for hs in range(132):
        if hs < 128:
            att_scores(hs)
        if 1 <= hs and hs - 1 < 128:
            att_cons1(hs - 1)
        if 4 <= hs:
            att_cons2_pe(hs - 4)
            att_cons2(hs - 4)

    bld.barrier()

    bld.phases.append(("ATT", dict(bld.cnt)))

    wos = [sb["wo0"], sb["wo1"]]
    for fo in range(NH):
        wo = wos[fo % 2]
        osem = "dg0" if fo % 2 == 0 else "dg1"
        ld = G(lambda g, fo=fo, wo=wo: g.dma_start(out=wo.ap()[:, :], in_=dr["wot"][fo]),
               waits=[("t", ev.get(("wot", fo - 2, 1), 0))], sem=osem)
        for ts in range(2):
            slot = (2 * fo + ts) % 4
            pq, off = [(ps["a"], 0), (ps["a"], 512), (ps["b"], 0), (ps["c"], 0)][slot]
            for fc in range(NFC):
                T(lambda t, wo=wo, fc=fc, ts=ts, pq=pq, off=off: t.matmul(
                    pq.ap()[:, off:off + 512],
                    wo.ap()[:, 128 * fc:128 * (fc + 1)],
                    sb["attnT"].ap()[:, NTOK * fc + 512 * ts:NTOK * fc + 512 * (ts + 1)],
                    start=(fc == 0), stop=(fc == NFC - 1)),
                    waits=([(osem, ld), ("s", ev.get(("woev", fo - 2, ts), 0))] if fc == 0 else ()),
                    inc=("t" if fc == NFC - 1 else None))
            ev[("wot", fo, ts)] = bld.cnt["t"]
            ev[("woev", fo, ts)] = S(lambda s, fo=fo, ts=ts, pq=pq, off=off: s.activation(
                sb["finalT"].ap()[:, NTOK * fo + 512 * ts:NTOK * fo + 512 * (ts + 1)],
                pq.ap()[:, off:off + 512], ACTF.Identity, bias=sb["bias_sb"].ap()[:, fo:fo + 1]),
                waits=[("t", ev[("wot", fo, ts)])])
    wfs = [sb["wf0"], sb["wf1"]]
    ev[("wfld", 0)] = G(lambda g: g.dma_start(out=sb["wf0"].ap()[:, :], in_=dr["wft"][0]),
                        waits=[("dg0", bld.cnt["dg0"])], sem="dg0")

    bld.barrier()

    bld.phases.append(("WO", dict(bld.cnt)))

    swnats = [sb["swnat"], sb["swnat1"]]
    swss = [sb["sws"], sb["sws1"]]
    recws = [sb["recw"], sb["recw1"]]
    wtmps = [sb["wtmp"], sb["wtmp1"]]
    wrows = [sb["wrow"], sb["wrow1"]]
    recipws = [sb["recipw_bc"], sb["recipw_bc1"]]
    main_ps = [ps["b"].ap()[:, :], ps["c"].ap()[:, :], ps["f"].ap()[:, :],
               ps["a"].ap()[:, 0:512]]

    def scales_batch(g_, k):
        p = g_ % 2
        wf = wfs[p]
        ldsem = "dg0" if p == 0 else "dg1"
        fo2, bq = k // 8, k % 8
        slot = 1 + k % 2
        for j in range(4):
            fc = 4 * bq + j
            T(lambda t, wf=wf, fc=fc, fo2=fo2, slot=slot, j=j: t.matmul(
                ps["a"].ap()[:, 512 * slot + 128 * j:512 * slot + 128 * (j + 1)],
                wf.ap()[:, 256 * fc + 128 * fo2:256 * fc + 128 * (fo2 + 1)],
                sb["hbd_bf"].ap()[:, :], start=True, stop=True),
                waits=([(ldsem, ev[("wfld", g_)]),
                        ("s", ev[("fstg", g_, k - 2)] if k >= 2 else
                              ev.get(("fstg", g_ - 1, 14 + k), 0))]
                       if j == 0 else ()),
                inc=("t" if j == 3 else None))
        tmm = bld.cnt["t"]
        stg = sb["wstage0"] if k % 2 == 0 else sb["wstage1"]
        rmx = sb["runmax0"] if fo2 == 0 else sb["runmax1"]
        ev[("fstg", g_, k)] = S(lambda s, stg=stg, slot=slot: s.activation(
            stg.ap()[:, :], ps["a"].ap()[:, 512 * slot:512 * (slot + 1)], ACTF.Abs),
            waits=[("t", tmm), ("v", ev.get(("fsc", g_, k - 2), 0))])
        if bq == 0:
            V(lambda v, stg=stg, rmx=rmx: v.tensor_copy(rmx.ap()[:, :], stg.ap()[:, :]),
                waits=[("s", ev[("fstg", g_, k)])])
        else:
            V(lambda v, stg=stg, rmx=rmx: v.tensor_tensor(
                rmx.ap()[:, :], rmx.ap()[:, :], stg.ap()[:, :], op=ALU.max),
                waits=[("s", ev[("fstg", g_, k)])])
        ev[("fsc", g_, k)] = bld.cnt["v"]
        if bq == 7:
            V(lambda v, fo2=fo2, p=p, rmx=rmx: v.reduce_max(
                swnats[p].ap()[:, fo2:fo2 + 1], rmx.ap()[:, :], axis=AX.X))
            V(lambda v, fo2=fo2, p=p: v.tensor_scalar(
                swss[p].ap()[:, fo2:fo2 + 1], swnats[p].ap()[:, fo2:fo2 + 1],
                INV224, CLAMP, op0=ALU.mult, op1=ALU.max))
            ev[("fsn", g_, fo2)] = V(lambda v, fo2=fo2, p=p: v.reciprocal(
                recws[p].ap()[:, fo2:fo2 + 1], swss[p].ap()[:, fo2:fo2 + 1]))
        if k == 15:
            ev[("fsct", g_)] = bld.cnt["t"]

    def emit_rows(g_):
        p = g_ % 2
        for fo2 in range(2):
            T(lambda t, fo2=fo2, p=p: t.matmul(
                ps["d"].ap()[0:1, 128 * fo2:128 * (fo2 + 1)],
                recws[p].ap()[:, fo2:fo2 + 1],
                sb["ident_f"].ap()[:, :], start=True, stop=True),
                waits=[("v", ev[("fsn", g_, fo2)]),
                       ("s", ev.get(("fqS", g_ - 1, NFC - 1), 0)),
                       ("v", ev.get(("fbce", g_ - 1), 0))],
                inc=("t" if fo2 == 1 else None))
        ev[("frt", g_)] = bld.cnt["t"]
        ev[("frow", g_)] = V(lambda v, p=p: v.tensor_copy(
            wrows[p].ap()[0:1, :], ps["d"].ap()[0:1, 0:256]),
            waits=[("t", ev[("frt", g_)])])
        ev[("fbc", g_)] = T(lambda t, p=p: t.matmul(
            ps["d"].ap()[:, 256:512], sb["onesr_f"].ap()[0:1, :],
            wrows[p].ap()[0:1, :], start=True, stop=True),
            waits=[("v", ev[("frow", g_)])], inc="t")
        ev[("fbce", g_)] = V(lambda v, p=p: v.tensor_copy(
            recipws[p].ap()[:, :], ps["d"].ap()[:, 256:512]),
            waits=[("t", ev[("fbc", g_)])])

    ubs = [sb["ub"], sb["ub1"]]
    def emit_epi(g_, ks):
        p = g_ % 2
        for k in ks:
            fo2, ts = k // 2, k % 2
            fo_idx = 2 * g_ + fo2
            ubp = ubs[k % 2]
            stt = V(lambda v, k=k, fo2=fo2, ts=ts, p=p, ubp=ubp: v.scalar_tensor_tensor(
                ubp.ap()[:, :], main_ps[k], swss[p].ap()[:, fo2:fo2 + 1],
                sb["sxp_bc"].ap()[:, 512 * ts:512 * (ts + 1)],
                op0=ALU.mult, op1=ALU.mult),
                waits=[("t", ev[("fmt01", g_)] if k < 2 else ev[("fmt", g_)]),
                       ("g", ev.get(("fepg", g_, k - 2), ev.get(("fepg", g_ - 1, k + 2), 0)))])
            ev[("fep", g_, k)] = stt
            ev[("fepg", g_, k)] = GC(lambda g, fo_idx=fo_idx, ts=ts, ubp=ubp: g.tensor_tensor(
                sb["finalT"].ap()[:, NTOK * fo_idx + 512 * ts:NTOK * fo_idx + 512 * (ts + 1)],
                sb["finalT"].ap()[:, NTOK * fo_idx + 512 * ts:NTOK * fo_idx + 512 * (ts + 1)],
                ubp.ap()[:, :], op=ALU.add),
                waits=[("v", stt)])

    ev[("wfld", 1)] = G(lambda g: g.dma_start(out=sb["wf1"].ap()[:, :], in_=dr["wft"][1]),
                        sem="dg1")
    for k in range(16):
        scales_batch(0, k)
    emit_rows(0)
    def emit_wfscale(g_):
        p = g_ % 2
        wf = wfs[p]
        ldsem = "dg0" if p == 0 else "dg1"
        for fc in range(NFC):
            V(lambda v, wf=wf, fc=fc, p=p: v.tensor_tensor(
                wf.ap()[:, 256 * fc:256 * (fc + 1)], wf.ap()[:, 256 * fc:256 * (fc + 1)],
                recipws[p].ap()[:, :], op=ALU.mult),
                waits=([("v", ev[("fbce", g_)]), ("t", ev[("fsct", g_)]),
                        (ldsem, ev[("wfld", g_)])] if fc == 0 else ()))
            ev[("wfs", g_, fc)] = bld.cnt["v"]

    emit_wfscale(0)
    for g_ in range(NG):
        p = g_ % 2
        wf = wfs[p]
        ldsem = "dg0" if p == 0 else "dg1"
        def qmm(fc):
            pd = ps["d"] if (fc // 2) % 2 == 0 else ps["e"]
            off = 256 * (fc % 2)
            ev[("fqt", g_, fc)] = T(lambda t, wf=wf, fc=fc, pd=pd, off=off: t.matmul(
                pd.ap()[:, off:off + 256], sb["hbd_bf"].ap()[:, :],
                wf.ap()[:, 256 * fc:256 * (fc + 1)], start=True, stop=True),
                waits=[("v", ev[("wfs", g_, fc)]),
                       ("s", ev[("fqS", g_, fc - 3)] if fc >= 3 else
                             ev.get(("fqS", g_ - 1, NFC - 1), 0))]
                      + ([("v", ev[("fbce", g_)])] if fc == 0 else []),
                inc="t")
            if fc % 2 == 1:
                ev[("fqS", g_, fc)] = S(lambda s, fc=fc, pd=pd: s.copy(
                    sb["wq8"].ap()[:, 256 * (fc - 1):256 * (fc + 1)], pd.ap()[:, :]),
                    waits=[("t", ev[("fqt", g_, fc)])])
            else:
                ev[("fqS", g_, fc)] = ev.get(("fqS", g_, fc - 1), 0)
        wq8r = sb["wq8"].ap().rearrange("p (f c) -> p f c", f=NFC)
        def main_mm(fcp, k, gate):
            fo2, ts = k // 2, k % 2
            T(lambda t, fcp=fcp, fo2=fo2, ts=ts, k=k: t.matmul(
                main_ps[k],
                wq8r[:, fcp:fcp + 2, 128 * fo2:128 * (fo2 + 1)],
                xq8v[:, fcp:fcp + 2, 512 * ts:512 * (ts + 1)],
                start=(fcp == 0), stop=(fcp == NFC - 2),
                perf_mode=mybir.MatmulPerfMode.DoubleRow),
                waits=(([("s", ev[("fqS", g_, fcp + 1)])] if gate else [])
                      + ([("v", ev.get(("fep", g_ - 1, k), 0))] if fcp == 0 else [])),
                inc=("t" if fcp == NFC - 2 else None))
        qmm(0)
        qmm(1)
        qmm(2)
        qmm(3)
        for fc in range(0, NFC, 2):
            if fc + 4 < NFC:
                qmm(fc + 4)
                qmm(fc + 5)
            if fc >= 2:
                main_mm(fc - 2, 0, True)
                main_mm(fc - 2, 1, False)
            if g_ + 1 < NG:
                scales_batch(g_ + 1, fc // 2)
            if fc == 4 and g_ >= 1:
                emit_epi(g_ - 1, (2, 3))
        main_mm(NFC - 2, 0, True)
        main_mm(NFC - 2, 1, False)
        ev[("fmt01", g_)] = bld.cnt["t"]
        emit_epi(g_, (0, 1))
        for fc in range(0, NFC, 2):
            main_mm(fc, 2, False)
            main_mm(fc, 3, False)
        ev[("fmt", g_)] = bld.cnt["t"]
        if g_ + 2 < NG:
            ev[("wfld", g_ + 2)] = G(lambda g, g_=g_: g.dma_start(
                out=wfs[g_ % 2].ap()[:, :], in_=dr["wft"][g_ + 2]),
                waits=[("t", ev[("fqt", g_, NFC - 1)]), ("v", ev[("wfs", g_, NFC - 1)])],
                sem=("dg0" if g_ % 2 == 0 else "dg1"))
        if g_ + 1 < NG:
            emit_rows(g_ + 1)
            emit_wfscale(g_ + 1)
        if g_ == NG - 1:
            emit_epi(g_ - 1, (2, 3)) if ("fepg", g_ - 1, 2) not in ev else None
            emit_epi(g_, (2, 3))

    bld.barrier()

    bld.phases.append(("FFN", dict(bld.cnt)))

    xins = [sb["xin0"], sb["xin1"]]
    outs = [sb["outsb0"], sb["outsb1"]]
    for tt in range(NOT):
        p = tt % 2
        xin, osb = xins[p], outs[p]
        dsem = "dy0" if p == 0 else "dy1"
        ssem = "ds0" if p == 0 else "ds1"
        ld = Y(lambda yq, xin=xin, tt=tt: yq.dma_start(
            out=xin.ap()[:, :], in_=dr["xs"][OWN0 + 128 * tt:OWN0 + 128 * (tt + 1), :]),
            waits=[("v", ev.get(("oev", (tt - 2) * 8 + 7), 0))], sem=dsem)
        for bq in range(8):
            gi = tt * 8 + bq
            pb = ps["b"] if gi % 2 == 0 else ps["c"]
            for j in range(4):
                fc = 4 * bq + j
                T(lambda t, fc=fc, tt=tt, pb=pb, j=j: t.matmul(
                    pb.ap()[:, 128 * j:128 * (j + 1)],
                    sb["finalT"].ap()[:, NTOK * fc + 128 * tt:NTOK * fc + 128 * (tt + 1)],
                    sb["ident_bf"].ap()[:, :], start=True, stop=True),
                    waits=([("v", ev.get(("oev", gi - 2), 0))] if j == 0 else ()),
                    inc=("t" if j == 3 else None))
            tmm = bld.cnt["t"]
            V(lambda v, osb=osb, xin=xin, bq=bq, pb=pb: v.tensor_tensor(
                osb.ap()[:, 512 * bq:512 * (bq + 1)], pb.ap()[:, :],
                xin.ap()[:, 512 * bq:512 * (bq + 1)], op=ALU.add),
                waits=[("t", tmm), (dsem, ld), (ssem, ev.get(("ost", tt - 2), 0))])
            ev[("oev", gi)] = bld.cnt["v"]
        ev[("ost", tt)] = SD(lambda s, osb=osb, tt=tt: s.dma_start(
            out=y[128 * tt:128 * (tt + 1), :], in_=osb.ap()[:, :]),
            waits=[("v", ev[("oev", tt * 8 + 7)])], sem=ssem)

    bld.phases.append(("OUT", dict(bld.cnt)))
    global PHASES
    PHASES = bld.phases

    bld.ops["s"].append(({"ds0": bld.cnt["ds0"], "ds1": bld.cnt["ds1"]}, None, None, None))

    import contextlib
    with contextlib.ExitStack() as stack:
        SEMS = {}
        for nm in ("t", "v", "s", "g") + DMA_SEMS:
            SEMS[nm] = stack.enter_context(nc.semaphore("sem_" + nm))
        block = stack.enter_context(nc.Block())

        def run(engname, eng):
            lastw = {}
            for (w, fn, inc, val) in bld.ops[engname]:
                for semn in sorted(w):
                    tgt = w[semn]
                    if tgt > lastw.get(semn, 0):
                        eng.wait_ge(SEMS[semn], tgt)
                        lastw[semn] = tgt
                if fn is None:
                    continue
                inst = fn(eng)
                if inc is not None:
                    inst.then_inc(SEMS[inc], 16 if inc in DMA_SEMS else 1)

        @block.tensor
        def _(eng):
            run("t", eng)

        @block.vector
        def _(eng):
            run("v", eng)

        @block.scalar
        def _(eng):
            run("s", eng)

        @block.gpsimd
        def _(eng):
            run("g", eng)

        @block.sync
        def _(eng):
            run("y", eng)

    return nc


def _host_prep(L0_attn_x, L0_attn_Wq, L0_attn_Wk, L0_attn_Wv, L0_attn_Wo,
               L0_ffn_weight, L0_ffn_bias, L0_ffn_hadamard_matrix):
    f32 = np.float32
    Wq = np.asarray(L0_attn_Wq, f32)
    Wk = np.asarray(L0_attn_Wk, f32)
    Wv = np.asarray(L0_attn_Wv, f32)
    Wo = np.asarray(L0_attn_Wo, f32)
    Wf = np.asarray(L0_ffn_weight, f32)
    H = np.asarray(L0_ffn_hadamard_matrix, f32)
    bias = np.asarray(L0_ffn_bias, f32)

    wqt = np.ascontiguousarray(Wq.reshape(32, 128, 32, 128).transpose(0, 3, 2, 1)).reshape(32, 128, 4096)
    wot = np.ascontiguousarray(Wo.reshape(32, 128, 32, 128).transpose(0, 3, 2, 1)).reshape(32, 128, 4096)
    wkt = np.ascontiguousarray(Wk.reshape(128, 32, 128).transpose(2, 1, 0)).reshape(128, 4096)
    wvt = np.ascontiguousarray(Wv.reshape(128, 32, 128).transpose(2, 1, 0)).reshape(128, 4096)
    wft = np.ascontiguousarray(
        Wf.reshape(16, 256, 32, 128).transpose(0, 3, 2, 1)).reshape(16, 128, 8192)
    bias_t = np.ascontiguousarray(bias.reshape(32, 128).T)
    hbd = np.zeros((128, 128), f32)
    hbd[0:64, 0:64] = H
    hbd[64:128, 64:128] = H

    S = 2048
    inv = 1.0 / 10000.0 ** (np.arange(0, 128, 2, dtype=f32) / 128.0)
    frq = np.outer(np.arange(S, dtype=f32), inv)
    cos = np.zeros((S, 128), f32); sin = np.zeros((S, 128), f32)
    cos[:, 0::2] = np.cos(frq); cos[:, 1::2] = np.cos(frq)
    sin[:, 0::2] = np.sin(frq); sin[:, 1::2] = np.sin(frq)
    cosT = np.ascontiguousarray(cos.T)
    sin_modT = np.ascontiguousarray(sin.T)
    sin_modT[0:64, :] = -sin_modT[0:64, :]
    sin_modT = np.ascontiguousarray(np.concatenate([sin_modT[64:128], sin_modT[0:64]], axis=0))

    alpha_lin = np.linspace(0.0, 1.0, 256, dtype=f32)

    def _mk_selw():
        s = np.zeros((66, 256), f32)
        for bh in (0, 32, 64):
            s[bh + 0, 0:128] = 1.0
            s[bh + 1, 128:256] = 1.0
        return s

    consts = dict(
        bias_t=bias_t, hbd=hbd,
        ident=np.eye(128, dtype=f32),
        onesr=np.ones((1, 128), f32),
        selw=_mk_selw(),
        wqt=wqt, wkt=wkt, wvt=wvt, wot=wot, wft=wft,
    )

    x = np.asarray(L0_attn_x, f32)
    per_core = []
    for core in range(8):
        b, half = core // 2, core % 2
        xs = np.zeros((SROWS, D), f32)
        kc = np.zeros((128, SROWS), f32); ks = np.zeros((128, SROWS), f32)
        if half == 0:
            xs[256:1536] = x[b, 0:1280]
            kc[:, 256:1536] = cosT[:, 0:1280]; ks[:, 256:1536] = sin_modT[:, 0:1280]
        else:
            xs[0:1280] = x[b, 768:2048]
            kc[:, 0:1280] = cosT[:, 768:2048]; ks[:, 0:1280] = sin_modT[:, 768:2048]
        q0 = 1024 * half
        qc = (cosT[:, q0:q0 + 1024] * f32(SCALE)).astype(f32)
        qs = (sin_modT[:, q0:q0 + 1024] * f32(SCALE)).astype(f32)
        alphat = np.zeros((66, 1024), f32)
        for rr in range(4):
            r_abs = rr + 4 * half
            if r_abs == 0:
                om, al = np.zeros(256, f32), np.ones(256, f32)
            else:
                om, al = (1.0 - alpha_lin).astype(f32), alpha_lin
            for bh in (0, 32, 64):
                alphat[bh + 0, 256 * rr:256 * (rr + 1)] = om
                alphat[bh + 1, 256 * rr:256 * (rr + 1)] = al
        rsmask = np.zeros((128, 48), f32)
        for rr in range(4):
            for j in range(6):
                if j <= 3:
                    rsmask[:, rr * 12 + 2 * j] = 1.0
                if j >= 2:
                    dead = half == 1 and rr == 3 and j >= 4
                    rsmask[:, rr * 12 + 2 * j + 1] = 0.0 if dead else 1.0
        m = dict(consts)
        m.update(xs=np.ascontiguousarray(xs), ropekc=np.ascontiguousarray(kc),
                 ropeks=np.ascontiguousarray(ks), ropeqc=np.ascontiguousarray(qc),
                 ropeqs=np.ascontiguousarray(qs), alphat=alphat, rsmask=rsmask)
        per_core.append(m)
    return per_core


def kernel(L0_attn_x, L0_attn_Wq, L0_attn_Wk, L0_attn_Wv, L0_attn_Wo,
           L0_ffn_weight, L0_ffn_bias, L0_ffn_hadamard_matrix):
    in_maps = _host_prep(L0_attn_x, L0_attn_Wq, L0_attn_Wk, L0_attn_Wv,
                         L0_attn_Wo, L0_ffn_weight, L0_ffn_bias,
                         L0_ffn_hadamard_matrix)
    nc = build_nc()
    res = run_bass_kernel_spmd(nc, in_maps, core_ids=list(range(8)))
    out = np.empty((4, 2048, 4096), np.float32)
    for core in range(8):
        b, half = core // 2, core % 2
        out[b, 1024 * half:1024 * (half + 1)] = res.results[core]["y"]
    return out


# version 26
# speedup vs baseline: 1.0787x; 1.0059x over previous
import numpy as np
import concourse.bass as bass
import concourse.mybir as mybir
from concourse.bass_utils import run_bass_kernel_spmd

F32 = mybir.dt.float32
BF16 = mybir.dt.bfloat16
F8 = mybir.dt.float8e4
ALU = mybir.AluOpType
ACTF = mybir.ActivationFunctionType
AX = mybir.AxisListType

D = 4096
NH = 32
NFC = 32
SROWS = 1536
NTOK = 1024
OWN0 = 256
NTT = 12
NOT = 8
NG = 16
SCALE = 1.0 / float(np.sqrt(128.0))
EPS = float(np.float32(1e-5))
INV224 = float(np.float32(1.0) / np.float32(224.0))
CLAMP = float(np.float32(2.0) * np.float32(1e-12))
INVD = float(np.float32(1.0) / np.float32(D))

ENGINES = ("t", "v", "s", "g", "y")
DMA_SEMS = ("dgc", "dg0", "dg1", "dyc", "dy0", "dy1", "ds0", "ds1")


class B:
    def __init__(self):
        self.ops = {e: [] for e in ENGINES}
        self.cnt = {"t": 0, "v": 0, "s": 0, "g": 0, "dgc": 0, "dg0": 0, "dg1": 0,
                    "dyc": 0, "dy0": 0, "dy1": 0, "ds0": 0, "ds1": 0}
        self.pending = {e: {} for e in ENGINES}

    def emit(self, eng, fn, waits=(), inc=None):
        w = dict(self.pending[eng])
        self.pending[eng] = {}
        for sem, val in waits:
            if val and val > 0:
                w[sem] = max(w.get(sem, 0), val)
        if eng in ("v", "s", "g") and inc not in DMA_SEMS:
            prev = self.cnt[eng]
            if prev > 0:
                w[eng] = max(w.get(eng, 0), prev)
            inc = eng
        val = None
        if inc is not None:
            amt = 16 if inc in DMA_SEMS else 1
            self.cnt[inc] += amt
            val = self.cnt[inc]
        self.ops[eng].append((w, fn, inc, val))
        return val

    def barrier(self):
        for e in ENGINES:
            for k, v in self.cnt.items():
                if v > 0:
                    self.pending[e][k] = max(self.pending[e].get(k, 0), v)


PHASES = []


def build_nc():
    nc = bass.Bass("TRN2", target_bir_lowering=False)

    dr = {}
    def din(name, shape, dtype=F32):
        dr[name] = nc.dram_tensor(name, list(shape), dtype, kind="ExternalInput").ap()
    din("xs", (SROWS, D))
    din("wqt", (NH, 128, D))
    din("wkt", (128, D))
    din("wvt", (128, D))
    din("wot", (NH, 128, D))
    din("wft", (NG, 128, 8192))
    din("bias_t", (128, 32))
    din("hbd", (128, 128))
    din("ident", (128, 128))
    din("onesr", (1, 128))
    din("ropekc", (128, SROWS))
    din("ropeks", (128, SROWS))
    din("ropeqc", (128, NTOK))
    din("ropeqs", (128, NTOK))
    din("alphat", (66, 1024))
    din("rsmask", (128, 48))
    din("selw", (66, 256))
    y = nc.dram_tensor("y", [NTOK, D], F32, kind="ExternalOutput").ap()

    BASE = 16544
    A0, B0, C0, D0 = BASE, BASE + 98304, BASE + 163840, BASE + 196608
    LIMIT = 229376
    sb = {}
    def at(name, shape, dtype, off):
        t = nc.alloc_sbuf_tensor_at(name, list(shape), dtype, offset=off)
        sb[name] = t
        return t

    at("xnT", (128, NFC * SROWS), BF16, A0)
    at("attnT", (128, NH * NTOK), BF16, A0)
    at("sigT0", (128, 1536), BF16, A0 + 65536)
    at("sigT1", (128, 1536), BF16, A0 + 68608)
    at("cabc0", (128, 512), BF16, A0 + 71680)
    at("cabc1", (128, 512), BF16, A0 + 72704)
    at("alphat_sb", (66, 1024), F32, A0 + 73728)
    at("rsrp", (66, 512), F32, A0 + 93184)
    for _i in range(4):
        at(f"cacolp{_i}", (66, 256), BF16, A0 + 78848 + 512 * _i)
    at("tmpbl", (128, 256), BF16, A0 + 90112)
    at("tmpbl2", (128, 256), BF16, A0 + 90624)
    at("av_sb2", (128, 512), BF16, A0 + 91136)
    at("av_sb3", (128, 512), BF16, A0 + 92160)
    at("av_sb0", (128, 512), BF16, A0 + 95264)
    at("av_sb1", (128, 512), BF16, A0 + 96288)
    at("wo0", (128, 4096), BF16, A0 + 65536)
    at("wo1", (128, 4096), BF16, A0 + 90112)
    at("wf0", (128, 8192), BF16, A0 + 73728)
    at("wf1", (128, 8192), BF16, A0)
    at("wq8", (128, 8192), F8, A0 + 16384)
    at("swnat", (128, 2), F32, A0 + 24576)
    at("sws", (128, 2), F32, A0 + 24640)
    at("recw", (128, 2), F32, A0 + 24704)
    at("wtmp", (128, 1), F32, A0 + 24768)
    at("wrow", (1, 256), F32, A0 + 24832)
    at("recipw_bc", (128, 256), BF16, A0 + 25856)
    at("ub", (128, 512), BF16, A0 + 26880)
    at("ub1", (128, 512), BF16, A0 + 30208)
    at("wstage0", (128, 512), BF16, A0 + 32768)
    at("wstage1", (128, 512), BF16, A0 + 33792)
    at("runmax0", (128, 512), BF16, A0 + 34816)
    at("runmax1", (128, 512), BF16, A0 + 35840)
    at("swnat1", (128, 2), F32, A0 + 27904)
    at("sws1", (128, 2), F32, A0 + 27968)
    at("recw1", (128, 2), F32, A0 + 28032)
    at("wtmp1", (128, 1), F32, A0 + 28096)
    at("wrow1", (1, 256), F32, A0 + 28160)
    at("recipw_bc1", (128, 256), BF16, A0 + 29184)
    at("xin0", (128, 4096), F32, A0)
    at("xin1", (128, 4096), F32, A0 + 16384)
    at("outsb0", (128, 4096), F32, A0 + 32768)
    at("outsb1", (128, 4096), F32, A0 + 49152)
    at("QrT", (128, NH * NTOK), BF16, B0)
    at("finalT", (128, NH * NTOK), BF16, B0)
    at("xt0", (128, 4096), F32, B0)
    at("xt1", (128, 4096), F32, B0 + 16384)
    at("xnat0", (128, 4096), BF16, B0 + 32768)
    at("xnat1", (128, 4096), BF16, B0 + 40960)
    at("stats0", (128, 16), F32, B0 + 49152)
    at("stats1", (128, 16), F32, B0 + 49216)
    at("wv_sb", (128, 4096), BF16, B0 + 49280)
    at("xq8", (128, NFC * NTOK), F8, C0)
    at("rqc", (128, NTOK), BF16, C0)
    at("rqs", (128, NTOK), BF16, C0 + 2048)
    at("wqh0", (128, 4096), BF16, C0 + 4096)
    at("wqh1", (128, 4096), BF16, C0 + 12288)
    at("q_tmp", (128, 1024), BF16, C0 + 20480)
    at("tmp2", (128, 1024), BF16, C0 + 22528)
    at("wk_sb", (128, 4096), BF16, C0 + 24576)
    at("KrT", (128, SROWS), BF16, D0)
    at("Vn", (128, SROWS), BF16, D0 + 3072)
    at("rkc", (128, SROWS), BF16, D0 + 6144)
    at("rks", (128, SROWS), BF16, D0 + 9216)
    at("sxp_bc", (128, 1024), F32, D0 + 6144)
    at("recipx_bc", (128, 1024), F32, D0 + 10240)
    at("xrow_s", (1, 1024), F32, D0 + 6144)
    at("xrow_r", (1, 1024), F32, D0 + 10240)
    at("snrt", (128, 16), F32, D0 + 14336)
    at("xmaxn", (128, 8), F32, D0 + 14400)
    at("xtmp1", (128, 1), F32, D0 + 14432)
    at("ident_bf", (128, 128), BF16, D0 + 14464)
    at("hbd_bf", (128, 128), BF16, D0 + 14720)
    at("ident_f", (128, 128), F32, D0 + 14976)
    at("onesr_f", (1, 128), F32, D0 + 15488)
    at("onesr_bf", (1, 128), BF16, A0 + 81920)
    at("selw_bf", (66, 256), BF16, A0 + 80896)
    at("rsmask_sb", (128, 48), BF16, A0 + 82176)
    at("bias_sb", (128, 32), F32, D0 + 16000)
    assert D0 + 16128 + 32 <= LIMIT

    ps = {}
    ps["a"] = nc.alloc_psum_tensor("ps_a", [128, 1536], F32)
    for nm in ("b", "c", "d", "e", "f"):
        ps[nm] = nc.alloc_psum_tensor("ps_" + nm, [128, 512], F32)

    bld = B()
    bld.phases = []
    ev = {}
    T = lambda fn, waits=(), inc=None: bld.emit("t", fn, waits, inc)
    V = lambda fn, waits=(): bld.emit("v", fn, waits)
    S = lambda fn, waits=(): bld.emit("s", fn, waits)
    G = lambda fn, waits=(), sem="dgc": bld.emit("g", fn, waits, sem)
    GC = lambda fn, waits=(): bld.emit("g", fn, waits, None)
    Y = lambda fn, waits=(), sem="dyc": bld.emit("y", fn, waits, sem)
    SD = lambda fn, waits=(), sem="ds0": bld.emit("s", fn, waits, sem)

    xnT = sb["xnT"].ap().rearrange("p (f t) -> p f t", f=NFC)
    xq8v = sb["xq8"].ap().rearrange("p (f t) -> p f t", f=NFC)

    for dst, src in [("ident_bf", "ident"), ("hbd_bf", "hbd"), ("rkc", "ropekc"),
                     ("rks", "ropeks"), ("rqc", "ropeqc"), ("rqs", "ropeqs"),
                     ("wk_sb", "wkt"), ("wv_sb", "wvt")]:
        G(lambda g, dst=dst, src=src: g.dma_start(out=sb[dst].ap()[:, :], in_=dr[src]))
    cw0 = bld.cnt["dgc"]
    for dst, src in [("ident_f", "ident"), ("onesr_f", "onesr")]:
        Y(lambda yq, dst=dst, src=src: yq.dma_start(out=sb[dst].ap()[:, :], in_=dr[src]))
    cy0 = bld.cnt["dyc"]

    xts = [sb["xt0"], sb["xt1"]]
    xnats = [sb["xnat0"], sb["xnat1"]]
    statss = [sb["stats0"], sb["stats1"]]
    def p0_stats(tt):
        p = tt % 2
        xt, xnat, st = xts[p], xnats[p], statss[p]
        dsem = "dy0" if p == 0 else "dy1"
        ld = Y(lambda yq, xt=xt, tt=tt: yq.dma_start(
            out=xt.ap()[:, :], in_=dr["xs"][128 * tt:128 * (tt + 1), :]),
            waits=[("s", ev.get(("p0norm", tt - 2), 0))], sem=dsem)
        ev[("p0sum", tt)] = V(lambda v, xt=xt, st=st: v.reduce_sum(
            st.ap()[:, 0:1], xt.ap()[:, :], axis=AX.X),
            waits=[(dsem, ld), ("s", ev.get(("p0norm", tt - 2), 0))])
        ev[("p0sq", tt)] = S(lambda s, xt=xt, xnat=xnat, st=st: s.activation(
            xnat.ap()[:, :], xt.ap()[:, :], ACTF.Square, accum_out=st.ap()[:, 1:2]),
            waits=[(dsem, ld), ("t", ev.get(("p0t", tt - 2), 0)),
                   ("v", ev.get(("p0sum", tt), 0))])
        V(lambda v, st=st: v.tensor_scalar(st.ap()[:, 2:3], st.ap()[:, 0:1], INVD, None, op0=ALU.mult))
        V(lambda v, st=st: v.tensor_tensor(st.ap()[:, 3:4], st.ap()[:, 2:3], st.ap()[:, 2:3], op=ALU.mult))
        V(lambda v, st=st: v.tensor_scalar(st.ap()[:, 4:5], st.ap()[:, 1:2], INVD, EPS, op0=ALU.mult, op1=ALU.add),
          waits=[("s", ev[("p0sq", tt)])])
        ev[("p0var", tt)] = V(lambda v, st=st: v.tensor_tensor(
            st.ap()[:, 5:6], st.ap()[:, 4:5], st.ap()[:, 3:4], op=ALU.subtract))
        ev[("p0sqrt", tt)] = S(lambda s, st=st: s.activation(st.ap()[:, 6:7], st.ap()[:, 5:6], ACTF.Sqrt),
                               waits=[("v", ev[("p0var", tt)])])
        V(lambda v, st=st: v.reciprocal(st.ap()[:, 7:8], st.ap()[:, 6:7]),
          waits=[("s", ev[("p0sqrt", tt)])])
        ev[("p0rstd", tt)] = V(lambda v, st=st: v.tensor_scalar(
            st.ap()[:, 8:9], st.ap()[:, 2:3], st.ap()[:, 7:8], -1.0,
            op0=ALU.mult, op1=ALU.mult))
        ev[("p0norm", tt)] = S(lambda s, xt=xt, xnat=xnat, st=st: s.activation(
            xnat.ap()[:, :], xt.ap()[:, :], ACTF.Identity, bias=st.ap()[:, 8:9], scale=st.ap()[:, 7:8]),
            waits=[("v", ev[("p0rstd", tt)])])

    def p0_trans(tt):
        p = tt % 2
        xnat = xnats[p]
        for bq in range(8):
            gi = tt * 8 + bq
            pb = ps["b"] if gi % 2 == 0 else ps["c"]
            for j in range(4):
                fc = 4 * bq + j
                T(lambda t, xnat=xnat, fc=fc, pb=pb, j=j: t.matmul(
                    pb.ap()[:, 128 * j:128 * (j + 1)],
                    xnat.ap()[:, 128 * fc:128 * (fc + 1)],
                    sb["ident_bf"].ap()[:, :], start=True, stop=True),
                    waits=([("s", ev[("p0norm", tt)]), ("dgc", cw0),
                            (("v" if gi % 2 == 0 else "s"), ev.get(("p0ev", gi - 2), 0))] if j == 0 else ()),
                    inc=("t" if j == 3 else None))
            tmm = bld.cnt["t"]
            if bq % 2 == 0:
                ev[("p0ev", gi)] = V(lambda v, tt=tt, bq=bq, pb=pb: v.tensor_copy(
                    xnT[:, 4 * bq:4 * bq + 4, 128 * tt:128 * (tt + 1)],
                    pb.ap().rearrange("p (a b) -> p a b", a=4)[:, :, :]),
                    waits=[("t", tmm)])
            else:
                ev[("p0ev", gi)] = S(lambda s, tt=tt, bq=bq, pb=pb: s.copy(
                    xnT[:, 4 * bq:4 * bq + 4, 128 * tt:128 * (tt + 1)],
                    pb.ap().rearrange("p (a b) -> p a b", a=4)[:, :, :]),
                    waits=[("t", tmm)])
        ev[("p0t", tt)] = bld.cnt["t"]

    for tt in range(NTT + 1):
        if tt < NTT:
            p0_stats(tt)
        if tt >= 1:
            p0_trans(tt - 1)

    bld.barrier()

    bld.phases.append(("P0", dict(bld.cnt)))

    for ts3 in range(3):
        for fc in range(NFC):
            T(lambda t, ts3=ts3, fc=fc: t.matmul(
                ps["a"].ap()[:, 512 * ts3:512 * (ts3 + 1)],
                sb["wk_sb"].ap()[:, 128 * fc:128 * (fc + 1)],
                xnT[:, fc, 512 * ts3:512 * (ts3 + 1)],
                start=(fc == 0), stop=(fc == NFC - 1)),
                inc=("t" if fc == NFC - 1 else None))
        ev[("kt", ts3)] = bld.cnt["t"]
        off = 512 * (ts3 % 2)
        ev[("kev", ts3)] = S(lambda s, ts3=ts3, off=off: s.copy(
            sb["q_tmp"].ap()[:, off:off + 512],
            ps["a"].ap()[:, 512 * ts3:512 * (ts3 + 1)]),
            waits=[("t", ev[("kt", ts3)]), ("v", ev.get(("krope", ts3 - 2), 0))])
        qt = sb["q_tmp"].ap()
        V(lambda v, off=off, ts3=ts3: v.tensor_tensor(
            sb["tmp2"].ap()[0:64, 0:512], qt[64:128, off:off + 512],
            sb["rks"].ap()[64:128, 512 * ts3:512 * (ts3 + 1)], op=ALU.mult),
            waits=[("s", ev[("kev", ts3)])])
        V(lambda v, off=off, ts3=ts3: v.tensor_tensor(
            sb["tmp2"].ap()[64:128, 0:512], qt[0:64, off:off + 512],
            sb["rks"].ap()[0:64, 512 * ts3:512 * (ts3 + 1)], op=ALU.mult))
        V(lambda v, off=off, ts3=ts3: v.tensor_tensor(
            sb["KrT"].ap()[:, 512 * ts3:512 * (ts3 + 1)], qt[:, off:off + 512],
            sb["rkc"].ap()[:, 512 * ts3:512 * (ts3 + 1)], op=ALU.mult))
        ev[("krope", ts3)] = V(lambda v, ts3=ts3: v.tensor_tensor(
            sb["KrT"].ap()[:, 512 * ts3:512 * (ts3 + 1)],
            sb["KrT"].ap()[:, 512 * ts3:512 * (ts3 + 1)],
            sb["tmp2"].ap()[:, 0:512], op=ALU.add))
    for tg in range(3):
        pb = ps["b"] if tg % 2 == 0 else ps["c"]
        for fc in range(NFC):
            for tt4 in range(4):
                tt = 4 * tg + tt4
                T(lambda t, fc=fc, tt=tt, tt4=tt4, pb=pb: t.matmul(
                    pb.ap()[:, 128 * tt4:128 * (tt4 + 1)],
                    xnT[:, fc, 128 * tt:128 * (tt + 1)],
                    sb["wv_sb"].ap()[:, 128 * fc:128 * (fc + 1)],
                    start=(fc == 0 and tt4 == 0), stop=(fc == NFC - 1 and tt4 == 3)),
                    waits=([("v", ev.get(("vev", tg - 2), 0))] if (fc == 0 and tt4 == 0) else ()),
                    inc=("t" if (fc == NFC - 1 and tt4 == 3) else None))
        ev[("vt", tg)] = bld.cnt["t"]
        ev[("vev", tg)] = V(lambda v, tg=tg, pb=pb: v.tensor_copy(
            sb["Vn"].ap()[:, 512 * tg:512 * (tg + 1)], pb.ap()[:, :]),
            waits=[("t", ev[("vt", tg)])])

    bld.barrier()

    bld.phases.append(("KV", dict(bld.cnt)))

    wqhs = [sb["wqh0"], sb["wqh1"]]
    for h in range(NH):
        wq = wqhs[h % 2]
        qsem = "dg0" if h % 2 == 0 else "dg1"
        ld = G(lambda g, h=h, wq=wq: g.dma_start(out=wq.ap()[:, :], in_=dr["wqt"][h]),
               waits=[("t", ev.get(("qt", h - 2, 1), 0))], sem=qsem)
        for ts in range(2):
            slot = (2 * h + ts) % 4
            pq, off = [(ps["a"], 0), (ps["a"], 512), (ps["b"], 0), (ps["c"], 0)][slot]
            for fc in range(NFC):
                T(lambda t, wq=wq, fc=fc, ts=ts, pq=pq, off=off: t.matmul(
                    pq.ap()[:, off:off + 512],
                    wq.ap()[:, 128 * fc:128 * (fc + 1)],
                    xnT[:, fc, OWN0 + 512 * ts:OWN0 + 512 * (ts + 1)],
                    start=(fc == 0), stop=(fc == NFC - 1)),
                    waits=([(qsem, ld), ("s", ev.get(("qev", h - 2, ts), 0))] if fc == 0 else ()),
                    inc=("t" if fc == NFC - 1 else None))
            ev[("qt", h, ts)] = bld.cnt["t"]
            ev[("qev", h, ts)] = S(lambda s, ts=ts, pq=pq, off=off: s.copy(
                sb["q_tmp"].ap()[:, 512 * ts:512 * (ts + 1)], pq.ap()[:, off:off + 512]),
                waits=[("t", ev[("qt", h, ts)]), ("v", ev.get(("qrope", h - 1), 0))])
        qt = sb["q_tmp"].ap()
        dstq = sb["QrT"].ap()[:, NTOK * h:NTOK * (h + 1)]
        V(lambda v, qt=qt: v.tensor_tensor(
            sb["tmp2"].ap()[0:64, :], qt[64:128, :], sb["rqs"].ap()[64:128, :], op=ALU.mult),
            waits=[("s", ev[("qev", h, 1)])])
        V(lambda v, qt=qt: v.tensor_tensor(
            sb["tmp2"].ap()[64:128, :], qt[0:64, :], sb["rqs"].ap()[0:64, :], op=ALU.mult))
        V(lambda v, qt=qt, dstq=dstq: v.tensor_tensor(dstq, qt, sb["rqc"].ap()[:, :], op=ALU.mult))
        ev[("qrope", h)] = V(lambda v, dstq=dstq: v.tensor_tensor(
            dstq, dstq, sb["tmp2"].ap()[:, :], op=ALU.add))

    bld.barrier()

    bld.phases.append(("Q", dict(bld.cnt)))

    for tt in range(NOT):
        for bq in range(8):
            gi = tt * 8 + bq
            pb = ps["b"] if gi % 2 == 0 else ps["c"]
            for j in range(4):
                fc = 4 * bq + j
                T(lambda t, fc=fc, tt=tt, pb=pb, j=j: t.matmul(
                    pb.ap()[:, 128 * j:128 * (j + 1)],
                    xnT[:, fc, OWN0 + 128 * tt:OWN0 + 128 * (tt + 1)],
                    sb["hbd_bf"].ap()[:, :], start=True, stop=True),
                    waits=([("v", ev.get(("xsc", gi - 2), 0))] if j == 0 else ()),
                    inc=("t" if j == 3 else None))
            tmm = bld.cnt["t"]
            if bq == 0:
                V(lambda v, tt=tt, pb=pb: v.reduce_max(
                    sb["xmaxn"].ap()[:, tt:tt + 1],
                    pb.ap().rearrange("p (a b) -> p a b", a=4)[:, :, :],
                    axis=AX.XY, apply_absolute_value=True), waits=[("t", tmm)])
            else:
                V(lambda v, pb=pb: v.reduce_max(
                    sb["xtmp1"].ap()[:, 0:1],
                    pb.ap().rearrange("p (a b) -> p a b", a=4)[:, :, :],
                    axis=AX.XY, apply_absolute_value=True), waits=[("t", tmm)])
                V(lambda v, tt=tt: v.tensor_tensor(
                    sb["xmaxn"].ap()[:, tt:tt + 1], sb["xmaxn"].ap()[:, tt:tt + 1],
                    sb["xtmp1"].ap()[:, 0:1], op=ALU.max))
            ev[("xsc", gi)] = bld.cnt["v"]
        V(lambda v, tt=tt: v.tensor_scalar(
            sb["snrt"].ap()[:, 2 * tt:2 * tt + 1], sb["xmaxn"].ap()[:, tt:tt + 1],
            INV224, CLAMP, op0=ALU.mult, op1=ALU.max))
        ev[("xsn", tt)] = V(lambda v, tt=tt: v.reciprocal(
            sb["snrt"].ap()[:, 2 * tt + 1:2 * tt + 2], sb["snrt"].ap()[:, 2 * tt:2 * tt + 1]))
    for tt in range(NOT):
        T(lambda t, tt=tt: t.matmul(
            ps["a"].ap()[0:1, 128 * tt:128 * (tt + 1)],
            sb["snrt"].ap()[:, 2 * tt:2 * tt + 1],
            sb["ident_f"].ap()[:, :], start=True, stop=True),
            waits=([("v", ev[("xsn", NOT - 1)]), ("dyc", cy0),
                    ("v", ev[("xsc", NOT * 8 - 1)])] if tt == 0 else ()))
        prow = ps["b"] if tt < 4 else ps["c"]
        T(lambda t, tt=tt, prow=prow: t.matmul(
            prow.ap()[0:1, 128 * (tt % 4):128 * (tt % 4) + 128],
            sb["snrt"].ap()[:, 2 * tt + 1:2 * tt + 2],
            sb["ident_f"].ap()[:, :], start=True, stop=True),
            inc=("t" if tt == NOT - 1 else None))
    ev["xrt"] = bld.cnt["t"]
    ev["xrow_s_ev"] = V(lambda v: v.tensor_copy(sb["xrow_s"].ap()[0:1, :], ps["a"].ap()[0:1, 0:1024]),
                        waits=[("t", ev["xrt"])])
    V(lambda v: v.tensor_copy(sb["xrow_r"].ap()[0:1, 0:512], ps["b"].ap()[0:1, 0:512]))
    ev["xrow_r_ev"] = V(lambda v: v.tensor_copy(sb["xrow_r"].ap()[0:1, 512:1024], ps["c"].ap()[0:1, 0:512]))
    for c2 in range(2):
        pd = ps["d"] if c2 == 0 else ps["e"]
        T(lambda t, c2=c2, pd=pd: t.matmul(
            pd.ap()[:, :], sb["onesr_f"].ap()[0:1, :],
            sb["xrow_s"].ap()[0:1, 512 * c2:512 * (c2 + 1)], start=True, stop=True),
            waits=[("v", ev["xrow_s_ev"])])
        pb = ps["b"] if c2 == 0 else ps["c"]
        T(lambda t, c2=c2, pb=pb: t.matmul(
            pb.ap()[:, :], sb["onesr_f"].ap()[0:1, :],
            sb["xrow_r"].ap()[0:1, 512 * c2:512 * (c2 + 1)], start=True, stop=True),
            waits=[("v", ev["xrow_r_ev"])],
            inc=("t" if c2 == 1 else None))
    xbc_t = bld.cnt["t"]
    V(lambda v: v.tensor_copy(sb["sxp_bc"].ap()[:, 0:512], ps["d"].ap()[:, :]), waits=[("t", xbc_t)])
    V(lambda v: v.tensor_copy(sb["sxp_bc"].ap()[:, 512:1024], ps["e"].ap()[:, :]))
    V(lambda v: v.tensor_copy(sb["recipx_bc"].ap()[:, 0:512], ps["b"].ap()[:, :]))
    ev["xbc_ev"] = V(lambda v: v.tensor_copy(sb["recipx_bc"].ap()[:, 512:1024], ps["c"].ap()[:, :]))
    for fc in range(NFC):
        for ts in range(2):
            k = 2 * fc + ts
            pd = ps["d"] if k % 2 == 0 else ps["e"]
            tmm = T(lambda t, fc=fc, ts=ts, pd=pd: t.matmul(
                pd.ap()[:, :], sb["hbd_bf"].ap()[:, :],
                xnT[:, fc, OWN0 + 512 * ts:OWN0 + 512 * (ts + 1)], start=True, stop=True),
                waits=[("v", ev.get(("xq", k - 2), ev["xbc_ev"]))], inc="t")
            V(lambda v, fc=fc, ts=ts, pd=pd: v.tensor_tensor(
                xq8v[:, fc, 512 * ts:512 * (ts + 1)], pd.ap()[:, :],
                sb["recipx_bc"].ap()[:, 512 * ts:512 * (ts + 1)], op=ALU.mult),
                waits=[("t", tmm)])
            ev[("xq", k)] = bld.cnt["v"]

    latew = [("t", bld.cnt["t"]), ("v", bld.cnt["v"])]
    for dst, src in [("onesr_bf", "onesr"), ("rsmask_sb", "rsmask"), ("selw_bf", "selw")]:
        G(lambda g, dst=dst, src=src: g.dma_start(out=sb[dst].ap()[:, :], in_=dr[src]),
          waits=latew)
    for dst, src in [("alphat_sb", "alphat"), ("bias_sb", "bias_t")]:
        Y(lambda yq, dst=dst, src=src: yq.dma_start(out=sb[dst].ap()[:, :], in_=dr[src]),
          waits=latew)

    bld.barrier()

    bld.phases.append(("XQ", dict(bld.cnt)))

    sigTs = [sb["sigT0"], sb["sigT1"]]
    cabcs = [sb["cabc0"], sb["cabc1"]]
    cacolps = [sb[f"cacolp{_i}"] for _i in range(4)]
    av_sbs = [sb["av_sb0"], sb["av_sb1"], sb["av_sb2"], sb["av_sb3"]]

    def att_scores(hs):
        rr, h = hs // 32, hs % 32
        L0 = 256 * rr
        for j in range(6):
            T(lambda t, h=h, j=j, L0=L0, rr=rr: t.matmul(
                ps["a"].ap()[:, 256 * j:256 * (j + 1)],
                sb["KrT"].ap()[:, L0 + 128 * j:L0 + 128 * (j + 1)],
                sb["QrT"].ap()[:, NTOK * h + 256 * rr:NTOK * h + 256 * (rr + 1)],
                start=True, stop=True),
                waits=([("s", ev.get(("sig", hs - 1), 0))] if j == 0 else ()),
                inc=("t" if j == 5 else None))
        ev[("sct", hs)] = bld.cnt["t"]
        sigT = sigTs[hs % 2].ap()
        ev[("sig", hs)] = S(lambda s, sigT=sigT: s.activation(
            sigT[:, :], ps["a"].ap()[:, :], ACTF.Sigmoid),
            waits=[("t", ev[("sct", hs)]), ("t", ev.get(("avt", hs - 2), 0))])

    def att_cons1(hs):
        rr, h = hs // 32, hs % 32
        L0 = 256 * rr
        par = hs % 2
        sigT = sigTs[par].ap()
        b3, m3 = hs // 3, hs % 3
        prs = ps["d"] if b3 % 2 == 0 else ps["f"]
        base = 32 * m3
        for j in range(6):
            T(lambda t, sigT=sigT, j=j, rr=rr, prs=prs, base=base: t.matmul(
                prs.ap()[base:base + 2, 0:256],
                sb["rsmask_sb"].ap()[:, 12 * rr + 2 * j:12 * rr + 2 * j + 2],
                sigT[:, 256 * j:256 * (j + 1)],
                start=(j == 0), stop=(j == 5)),
                waits=([("s", ev[("sig", hs)]),
                        ("v", ev.get(("recb", b3 - 2), 0))] if j == 0 else ()),
                inc=("t" if j == 5 else None))
        ev[("rst", hs)] = bld.cnt["t"]
        pav = ps["b"] if par == 0 else ps["c"]
        for j in range(6):
            ops = []
            if j <= 3:
                ops.append((0, j == 0, False))
            if j >= 2:
                ops.append((1, False, j == 5))
            for (w2, st_, sp_) in ops:
                T(lambda t, j=j, w2=w2, st_=st_, sp_=sp_, sigT=sigT, pav=pav, L0=L0: t.matmul(
                    pav.ap()[:, 256 * w2:256 * (w2 + 1)],
                    sb["Vn"].ap()[:, L0 + 128 * j:L0 + 128 * (j + 1)],
                    sigT[:, 256 * j:256 * (j + 1)],
                    start=st_, stop=sp_),
                    waits=([("s", ev.get(("avec", hs - 2), 0))] if (j == 0 and w2 == 0) else ()),
                    inc=("t" if (w2 == 1 and sp_) else None))
        ev[("avt", hs)] = bld.cnt["t"]
        ev[("avec", hs)] = S(lambda s, pav=pav, hs4=hs % 4: s.copy(
            av_sbs[hs4].ap()[:, :], pav.ap()[:, :]),
            waits=[("t", ev[("avt", hs)]), ("v", ev.get(("bl", hs - 4), 0))])
        if m3 == 2 or hs == 127:
            ro = 256 * (b3 % 2)
            V(lambda v, prs=prs, ro=ro: v.reciprocal(
                sb["rsrp"].ap()[0:66, ro:ro + 256], prs.ap()[0:66, 0:256]),
                waits=[("t", ev[("rst", hs)])])
            ev[("recb", b3)] = bld.cnt["v"]
            for hh in range(3 * b3, min(3 * b3 + 3, 128)):
                bh = 32 * (hh % 3)
                rrh = hh // 32
                ev[("ca", hh)] = V(lambda v, rrh=rrh, h4=hh % 4, ro=ro, bh=bh:
                    v.tensor_tensor(
                        cacolps[h4].ap()[bh:bh + 2, :],
                        sb["rsrp"].ap()[bh:bh + 2, ro:ro + 256],
                        sb["alphat_sb"].ap()[bh:bh + 2, 256 * rrh:256 * (rrh + 1)],
                        op=ALU.mult),
                    waits=[("t", ev.get(("bctt", hh - 4), 0))])

    def att_cons2_pe(hs):
        cac = cacolps[hs % 4].ap()
        bh = 32 * (hs % 3)
        for w2 in range(2):
            T(lambda t, w2=w2, cac=cac, bh=bh: t.matmul(
                ps["e"].ap()[:, 256 * w2:256 * (w2 + 1)],
                sb["selw_bf"].ap()[bh:bh + 2, 128 * w2:128 * (w2 + 1)],
                cac[bh:bh + 2, 0:256],
                start=True, stop=True),
                waits=([("v", ev[("ca", hs)]), ("s", ev.get(("cab", hs - 1), 0))]
                       if w2 == 0 else ()),
                inc=("t" if w2 == 1 else None))
        ev[("bctt", hs)] = bld.cnt["t"]

    def att_cons2(hs):
        rr, h = hs // 32, hs % 32
        par = hs % 2
        cab = cabcs[par].ap()
        ev[("cab", hs)] = S(lambda s, cab=cab: s.copy(cab[:, :], ps["e"].ap()[:, :]),
                            waits=[("t", ev[("bctt", hs)]), ("v", ev.get(("bl", hs - 2), 0))])
        avs = av_sbs[hs % 4].ap()
        dsta = sb["attnT"].ap()[:, NTOK * h + 256 * rr:NTOK * h + 256 * (rr + 1)]
        V(lambda v, avs=avs, cab=cab: v.tensor_tensor(
            sb["tmpbl"].ap()[:, :], avs[:, 0:256], cab[:, 0:256], op=ALU.mult),
            waits=[("s", ev[("cab", hs)])])
        V(lambda v, avs=avs, cab=cab: v.tensor_tensor(
            sb["tmpbl2"].ap()[:, :], avs[:, 256:512], cab[:, 256:512], op=ALU.mult))
        V(lambda v, dsta=dsta: v.tensor_tensor(
            dsta, sb["tmpbl"].ap()[:, :], sb["tmpbl2"].ap()[:, :], op=ALU.add))
        ev[("bl", hs)] = bld.cnt["v"]

    for hs in range(132):
        if hs < 128:
            att_scores(hs)
        if 1 <= hs and hs - 1 < 128:
            att_cons1(hs - 1)
        if 4 <= hs:
            att_cons2_pe(hs - 4)
            att_cons2(hs - 4)

    bld.barrier()

    bld.phases.append(("ATT", dict(bld.cnt)))

    wos = [sb["wo0"], sb["wo1"]]
    for fo in range(NH):
        wo = wos[fo % 2]
        osem = "dg0" if fo % 2 == 0 else "dg1"
        ld = G(lambda g, fo=fo, wo=wo: g.dma_start(out=wo.ap()[:, :], in_=dr["wot"][fo]),
               waits=[("t", ev.get(("wot", fo - 2, 1), 0))], sem=osem)
        for ts in range(2):
            slot = (2 * fo + ts) % 4
            pq, off = [(ps["a"], 0), (ps["a"], 512), (ps["b"], 0), (ps["c"], 0)][slot]
            for fc in range(NFC):
                T(lambda t, wo=wo, fc=fc, ts=ts, pq=pq, off=off: t.matmul(
                    pq.ap()[:, off:off + 512],
                    wo.ap()[:, 128 * fc:128 * (fc + 1)],
                    sb["attnT"].ap()[:, NTOK * fc + 512 * ts:NTOK * fc + 512 * (ts + 1)],
                    start=(fc == 0), stop=(fc == NFC - 1)),
                    waits=([(osem, ld), ("s", ev.get(("woev", fo - 2, ts), 0))] if fc == 0 else ()),
                    inc=("t" if fc == NFC - 1 else None))
            ev[("wot", fo, ts)] = bld.cnt["t"]
            ev[("woev", fo, ts)] = S(lambda s, fo=fo, ts=ts, pq=pq, off=off: s.activation(
                sb["finalT"].ap()[:, NTOK * fo + 512 * ts:NTOK * fo + 512 * (ts + 1)],
                pq.ap()[:, off:off + 512], ACTF.Identity, bias=sb["bias_sb"].ap()[:, fo:fo + 1]),
                waits=[("t", ev[("wot", fo, ts)])])
    wfs = [sb["wf0"], sb["wf1"]]
    ev[("wfld", 0)] = G(lambda g: g.dma_start(out=sb["wf0"].ap()[:, :], in_=dr["wft"][0]),
                        waits=[("dg0", bld.cnt["dg0"])], sem="dg0")

    bld.barrier()

    bld.phases.append(("WO", dict(bld.cnt)))

    swnats = [sb["swnat"], sb["swnat1"]]
    swss = [sb["sws"], sb["sws1"]]
    recws = [sb["recw"], sb["recw1"]]
    wtmps = [sb["wtmp"], sb["wtmp1"]]
    wrows = [sb["wrow"], sb["wrow1"]]
    recipws = [sb["recipw_bc"], sb["recipw_bc1"]]
    main_ps = [ps["b"].ap()[:, :], ps["c"].ap()[:, :], ps["f"].ap()[:, :],
               ps["a"].ap()[:, 0:512]]

    def scales_batch(g_, k):
        p = g_ % 2
        wf = wfs[p]
        ldsem = "dg0" if p == 0 else "dg1"
        fo2, bq = k // 8, k % 8
        slot = 1 + k % 2
        for j in range(4):
            fc = 4 * bq + j
            T(lambda t, wf=wf, fc=fc, fo2=fo2, slot=slot, j=j: t.matmul(
                ps["a"].ap()[:, 512 * slot + 128 * j:512 * slot + 128 * (j + 1)],
                wf.ap()[:, 256 * fc + 128 * fo2:256 * fc + 128 * (fo2 + 1)],
                sb["hbd_bf"].ap()[:, :], start=True, stop=True),
                waits=([(ldsem, ev[("wfld", g_)]),
                        ("s", ev[("fstg", g_, k - 2)] if k >= 2 else
                              ev.get(("fstg", g_ - 1, 14 + k), 0))]
                       if j == 0 else ()),
                inc=("t" if j == 3 else None))
        tmm = bld.cnt["t"]
        stg = sb["wstage0"] if k % 2 == 0 else sb["wstage1"]
        rmx = sb["runmax0"] if fo2 == 0 else sb["runmax1"]
        ev[("fstg", g_, k)] = S(lambda s, stg=stg, slot=slot: s.activation(
            stg.ap()[:, :], ps["a"].ap()[:, 512 * slot:512 * (slot + 1)], ACTF.Abs),
            waits=[("t", tmm), ("v", ev.get(("fsc", g_, k - 2), 0))])
        if bq == 0:
            V(lambda v, stg=stg, rmx=rmx: v.tensor_copy(rmx.ap()[:, :], stg.ap()[:, :]),
                waits=[("s", ev[("fstg", g_, k)])])
        else:
            V(lambda v, stg=stg, rmx=rmx: v.tensor_tensor(
                rmx.ap()[:, :], rmx.ap()[:, :], stg.ap()[:, :], op=ALU.max),
                waits=[("s", ev[("fstg", g_, k)])])
        ev[("fsc", g_, k)] = bld.cnt["v"]
        if bq == 7:
            V(lambda v, fo2=fo2, p=p, rmx=rmx: v.reduce_max(
                swnats[p].ap()[:, fo2:fo2 + 1], rmx.ap()[:, :], axis=AX.X))
            V(lambda v, fo2=fo2, p=p: v.tensor_scalar(
                swss[p].ap()[:, fo2:fo2 + 1], swnats[p].ap()[:, fo2:fo2 + 1],
                INV224, CLAMP, op0=ALU.mult, op1=ALU.max))
            ev[("fsn", g_, fo2)] = V(lambda v, fo2=fo2, p=p: v.reciprocal(
                recws[p].ap()[:, fo2:fo2 + 1], swss[p].ap()[:, fo2:fo2 + 1]))
        if k == 15:
            ev[("fsct", g_)] = bld.cnt["t"]

    def emit_rows(g_):
        p = g_ % 2
        for fo2 in range(2):
            T(lambda t, fo2=fo2, p=p: t.matmul(
                ps["d"].ap()[0:1, 128 * fo2:128 * (fo2 + 1)],
                recws[p].ap()[:, fo2:fo2 + 1],
                sb["ident_f"].ap()[:, :], start=True, stop=True),
                waits=[("v", ev[("fsn", g_, fo2)]),
                       ("s", ev.get(("fqS", g_ - 1, NFC - 1), 0)),
                       ("v", ev.get(("fbce", g_ - 1), 0))],
                inc=("t" if fo2 == 1 else None))
        ev[("frt", g_)] = bld.cnt["t"]
        ev[("frow", g_)] = V(lambda v, p=p: v.tensor_copy(
            wrows[p].ap()[0:1, :], ps["d"].ap()[0:1, 0:256]),
            waits=[("t", ev[("frt", g_)])])
        ev[("fbc", g_)] = T(lambda t, p=p: t.matmul(
            ps["d"].ap()[:, 256:512], sb["onesr_f"].ap()[0:1, :],
            wrows[p].ap()[0:1, :], start=True, stop=True),
            waits=[("v", ev[("frow", g_)])], inc="t")
        ev[("fbce", g_)] = V(lambda v, p=p: v.tensor_copy(
            recipws[p].ap()[:, :], ps["d"].ap()[:, 256:512]),
            waits=[("t", ev[("fbc", g_)])])

    ubs = [sb["ub"], sb["ub1"]]
    def emit_epi(g_, ks):
        p = g_ % 2
        for k in ks:
            fo2, ts = k // 2, k % 2
            fo_idx = 2 * g_ + fo2
            ubp = ubs[k % 2]
            stt = V(lambda v, k=k, fo2=fo2, ts=ts, p=p, ubp=ubp: v.scalar_tensor_tensor(
                ubp.ap()[:, :], main_ps[k], swss[p].ap()[:, fo2:fo2 + 1],
                sb["sxp_bc"].ap()[:, 512 * ts:512 * (ts + 1)],
                op0=ALU.mult, op1=ALU.mult),
                waits=[("t", ev[("fmt01", g_)] if k < 2 else ev[("fmt", g_)]),
                       ("g", ev.get(("fepg", g_, k - 2), ev.get(("fepg", g_ - 1, k + 2), 0)))])
            ev[("fep", g_, k)] = stt
            ev[("fepg", g_, k)] = GC(lambda g, fo_idx=fo_idx, ts=ts, ubp=ubp: g.tensor_tensor(
                sb["finalT"].ap()[:, NTOK * fo_idx + 512 * ts:NTOK * fo_idx + 512 * (ts + 1)],
                sb["finalT"].ap()[:, NTOK * fo_idx + 512 * ts:NTOK * fo_idx + 512 * (ts + 1)],
                ubp.ap()[:, :], op=ALU.add),
                waits=[("v", stt)])

    ev[("wfld", 1)] = G(lambda g: g.dma_start(out=sb["wf1"].ap()[:, :], in_=dr["wft"][1]),
                        sem="dg1")
    for k in range(16):
        scales_batch(0, k)
    emit_rows(0)
    def emit_wfscale(g_):
        p = g_ % 2
        wf = wfs[p]
        ldsem = "dg0" if p == 0 else "dg1"
        for fc in range(NFC):
            V(lambda v, wf=wf, fc=fc, p=p: v.tensor_tensor(
                wf.ap()[:, 256 * fc:256 * (fc + 1)], wf.ap()[:, 256 * fc:256 * (fc + 1)],
                recipws[p].ap()[:, :], op=ALU.mult),
                waits=([("v", ev[("fbce", g_)]), ("t", ev[("fsct", g_)]),
                        (ldsem, ev[("wfld", g_)])] if fc == 0 else ()))
            ev[("wfs", g_, fc)] = bld.cnt["v"]

    emit_wfscale(0)
    for g_ in range(NG):
        p = g_ % 2
        wf = wfs[p]
        ldsem = "dg0" if p == 0 else "dg1"
        def qmm(fc):
            pd = ps["d"] if (fc // 2) % 2 == 0 else ps["e"]
            off = 256 * (fc % 2)
            ev[("fqt", g_, fc)] = T(lambda t, wf=wf, fc=fc, pd=pd, off=off: t.matmul(
                pd.ap()[:, off:off + 256], sb["hbd_bf"].ap()[:, :],
                wf.ap()[:, 256 * fc:256 * (fc + 1)], start=True, stop=True),
                waits=[("v", ev[("wfs", g_, fc)]),
                       ("s", ev[("fqS", g_, fc - 3)] if fc >= 3 else
                             ev.get(("fqS", g_ - 1, NFC - 1), 0))]
                      + ([("v", ev[("fbce", g_)])] if fc == 0 else []),
                inc="t")
            if fc % 2 == 1:
                ev[("fqS", g_, fc)] = S(lambda s, fc=fc, pd=pd: s.copy(
                    sb["wq8"].ap()[:, 256 * (fc - 1):256 * (fc + 1)], pd.ap()[:, :]),
                    waits=[("t", ev[("fqt", g_, fc)])])
            else:
                ev[("fqS", g_, fc)] = ev.get(("fqS", g_, fc - 1), 0)
        wq8r = sb["wq8"].ap().rearrange("p (f c) -> p f c", f=NFC)
        def main_mm(fcp, k, gate):
            fo2, ts = k // 2, k % 2
            T(lambda t, fcp=fcp, fo2=fo2, ts=ts, k=k: t.matmul(
                main_ps[k],
                wq8r[:, fcp:fcp + 2, 128 * fo2:128 * (fo2 + 1)],
                xq8v[:, fcp:fcp + 2, 512 * ts:512 * (ts + 1)],
                start=(fcp == 0), stop=(fcp == NFC - 2),
                perf_mode=mybir.MatmulPerfMode.DoubleRow),
                waits=(([("s", ev[("fqS", g_, fcp + 1)])] if gate else [])
                      + ([("v", ev.get(("fep", g_ - 1, k), 0))] if fcp == 0 else [])),
                inc=("t" if fcp == NFC - 2 else None))
        qmm(0)
        qmm(1)
        qmm(2)
        qmm(3)
        for fc in range(0, NFC, 2):
            if g_ + 1 < NG:
                scales_batch(g_ + 1, fc // 2)
            if fc + 4 < NFC:
                qmm(fc + 4)
                qmm(fc + 5)
            if fc >= 2:
                main_mm(fc - 2, 0, True)
                main_mm(fc - 2, 1, False)
            if fc == 4 and g_ >= 1:
                emit_epi(g_ - 1, (2, 3))
        main_mm(NFC - 2, 0, True)
        main_mm(NFC - 2, 1, False)
        ev[("fmt01", g_)] = bld.cnt["t"]
        emit_epi(g_, (0, 1))
        for fc in range(0, NFC, 2):
            main_mm(fc, 2, False)
            main_mm(fc, 3, False)
        ev[("fmt", g_)] = bld.cnt["t"]
        if g_ + 2 < NG:
            ev[("wfld", g_ + 2)] = G(lambda g, g_=g_: g.dma_start(
                out=wfs[g_ % 2].ap()[:, :], in_=dr["wft"][g_ + 2]),
                waits=[("t", ev[("fqt", g_, NFC - 1)]), ("v", ev[("wfs", g_, NFC - 1)])],
                sem=("dg0" if g_ % 2 == 0 else "dg1"))
        if g_ + 1 < NG:
            emit_rows(g_ + 1)
            emit_wfscale(g_ + 1)
        if g_ == NG - 1:
            emit_epi(g_ - 1, (2, 3)) if ("fepg", g_ - 1, 2) not in ev else None
            emit_epi(g_, (2, 3))

    bld.barrier()

    bld.phases.append(("FFN", dict(bld.cnt)))

    xins = [sb["xin0"], sb["xin1"]]
    outs = [sb["outsb0"], sb["outsb1"]]
    for tt in range(NOT):
        p = tt % 2
        xin, osb = xins[p], outs[p]
        dsem = "dy0" if p == 0 else "dy1"
        ssem = "ds0" if p == 0 else "ds1"
        ld = Y(lambda yq, xin=xin, tt=tt: yq.dma_start(
            out=xin.ap()[:, :], in_=dr["xs"][OWN0 + 128 * tt:OWN0 + 128 * (tt + 1), :]),
            waits=[("v", ev.get(("oev", (tt - 2) * 8 + 7), 0))], sem=dsem)
        for bq in range(8):
            gi = tt * 8 + bq
            pb = ps["b"] if gi % 2 == 0 else ps["c"]
            for j in range(4):
                fc = 4 * bq + j
                T(lambda t, fc=fc, tt=tt, pb=pb, j=j: t.matmul(
                    pb.ap()[:, 128 * j:128 * (j + 1)],
                    sb["finalT"].ap()[:, NTOK * fc + 128 * tt:NTOK * fc + 128 * (tt + 1)],
                    sb["ident_bf"].ap()[:, :], start=True, stop=True),
                    waits=([("v", ev.get(("oev", gi - 2), 0))] if j == 0 else ()),
                    inc=("t" if j == 3 else None))
            tmm = bld.cnt["t"]
            V(lambda v, osb=osb, xin=xin, bq=bq, pb=pb: v.tensor_tensor(
                osb.ap()[:, 512 * bq:512 * (bq + 1)], pb.ap()[:, :],
                xin.ap()[:, 512 * bq:512 * (bq + 1)], op=ALU.add),
                waits=[("t", tmm), (dsem, ld), (ssem, ev.get(("ost", tt - 2), 0))])
            ev[("oev", gi)] = bld.cnt["v"]
        ev[("ost", tt)] = SD(lambda s, osb=osb, tt=tt: s.dma_start(
            out=y[128 * tt:128 * (tt + 1), :], in_=osb.ap()[:, :]),
            waits=[("v", ev[("oev", tt * 8 + 7)])], sem=ssem)

    bld.phases.append(("OUT", dict(bld.cnt)))
    global PHASES
    PHASES = bld.phases

    bld.ops["s"].append(({"ds0": bld.cnt["ds0"], "ds1": bld.cnt["ds1"]}, None, None, None))

    import contextlib
    with contextlib.ExitStack() as stack:
        SEMS = {}
        for nm in ("t", "v", "s", "g") + DMA_SEMS:
            SEMS[nm] = stack.enter_context(nc.semaphore("sem_" + nm))
        block = stack.enter_context(nc.Block())

        def run(engname, eng):
            lastw = {}
            for (w, fn, inc, val) in bld.ops[engname]:
                for semn in sorted(w):
                    tgt = w[semn]
                    if tgt > lastw.get(semn, 0):
                        eng.wait_ge(SEMS[semn], tgt)
                        lastw[semn] = tgt
                if fn is None:
                    continue
                inst = fn(eng)
                if inc is not None:
                    inst.then_inc(SEMS[inc], 16 if inc in DMA_SEMS else 1)

        @block.tensor
        def _(eng):
            run("t", eng)

        @block.vector
        def _(eng):
            run("v", eng)

        @block.scalar
        def _(eng):
            run("s", eng)

        @block.gpsimd
        def _(eng):
            run("g", eng)

        @block.sync
        def _(eng):
            run("y", eng)

    return nc


def _host_prep(L0_attn_x, L0_attn_Wq, L0_attn_Wk, L0_attn_Wv, L0_attn_Wo,
               L0_ffn_weight, L0_ffn_bias, L0_ffn_hadamard_matrix):
    f32 = np.float32
    Wq = np.asarray(L0_attn_Wq, f32)
    Wk = np.asarray(L0_attn_Wk, f32)
    Wv = np.asarray(L0_attn_Wv, f32)
    Wo = np.asarray(L0_attn_Wo, f32)
    Wf = np.asarray(L0_ffn_weight, f32)
    H = np.asarray(L0_ffn_hadamard_matrix, f32)
    bias = np.asarray(L0_ffn_bias, f32)

    wqt = np.ascontiguousarray(Wq.reshape(32, 128, 32, 128).transpose(0, 3, 2, 1)).reshape(32, 128, 4096)
    wot = np.ascontiguousarray(Wo.reshape(32, 128, 32, 128).transpose(0, 3, 2, 1)).reshape(32, 128, 4096)
    wkt = np.ascontiguousarray(Wk.reshape(128, 32, 128).transpose(2, 1, 0)).reshape(128, 4096)
    wvt = np.ascontiguousarray(Wv.reshape(128, 32, 128).transpose(2, 1, 0)).reshape(128, 4096)
    wft = np.ascontiguousarray(
        Wf.reshape(16, 256, 32, 128).transpose(0, 3, 2, 1)).reshape(16, 128, 8192)
    bias_t = np.ascontiguousarray(bias.reshape(32, 128).T)
    hbd = np.zeros((128, 128), f32)
    hbd[0:64, 0:64] = H
    hbd[64:128, 64:128] = H

    S = 2048
    inv = 1.0 / 10000.0 ** (np.arange(0, 128, 2, dtype=f32) / 128.0)
    frq = np.outer(np.arange(S, dtype=f32), inv)
    cos = np.zeros((S, 128), f32); sin = np.zeros((S, 128), f32)
    cos[:, 0::2] = np.cos(frq); cos[:, 1::2] = np.cos(frq)
    sin[:, 0::2] = np.sin(frq); sin[:, 1::2] = np.sin(frq)
    cosT = np.ascontiguousarray(cos.T)
    sin_modT = np.ascontiguousarray(sin.T)
    sin_modT[0:64, :] = -sin_modT[0:64, :]
    sin_modT = np.ascontiguousarray(np.concatenate([sin_modT[64:128], sin_modT[0:64]], axis=0))

    alpha_lin = np.linspace(0.0, 1.0, 256, dtype=f32)

    def _mk_selw():
        s = np.zeros((66, 256), f32)
        for bh in (0, 32, 64):
            s[bh + 0, 0:128] = 1.0
            s[bh + 1, 128:256] = 1.0
        return s

    consts = dict(
        bias_t=bias_t, hbd=hbd,
        ident=np.eye(128, dtype=f32),
        onesr=np.ones((1, 128), f32),
        selw=_mk_selw(),
        wqt=wqt, wkt=wkt, wvt=wvt, wot=wot, wft=wft,
    )

    x = np.asarray(L0_attn_x, f32)
    per_core = []
    for core in range(8):
        b, half = core // 2, core % 2
        xs = np.zeros((SROWS, D), f32)
        kc = np.zeros((128, SROWS), f32); ks = np.zeros((128, SROWS), f32)
        if half == 0:
            xs[256:1536] = x[b, 0:1280]
            kc[:, 256:1536] = cosT[:, 0:1280]; ks[:, 256:1536] = sin_modT[:, 0:1280]
        else:
            xs[0:1280] = x[b, 768:2048]
            kc[:, 0:1280] = cosT[:, 768:2048]; ks[:, 0:1280] = sin_modT[:, 768:2048]
        q0 = 1024 * half
        qc = (cosT[:, q0:q0 + 1024] * f32(SCALE)).astype(f32)
        qs = (sin_modT[:, q0:q0 + 1024] * f32(SCALE)).astype(f32)
        alphat = np.zeros((66, 1024), f32)
        for rr in range(4):
            r_abs = rr + 4 * half
            if r_abs == 0:
                om, al = np.zeros(256, f32), np.ones(256, f32)
            else:
                om, al = (1.0 - alpha_lin).astype(f32), alpha_lin
            for bh in (0, 32, 64):
                alphat[bh + 0, 256 * rr:256 * (rr + 1)] = om
                alphat[bh + 1, 256 * rr:256 * (rr + 1)] = al
        rsmask = np.zeros((128, 48), f32)
        for rr in range(4):
            for j in range(6):
                if j <= 3:
                    rsmask[:, rr * 12 + 2 * j] = 1.0
                if j >= 2:
                    dead = half == 1 and rr == 3 and j >= 4
                    rsmask[:, rr * 12 + 2 * j + 1] = 0.0 if dead else 1.0
        m = dict(consts)
        m.update(xs=np.ascontiguousarray(xs), ropekc=np.ascontiguousarray(kc),
                 ropeks=np.ascontiguousarray(ks), ropeqc=np.ascontiguousarray(qc),
                 ropeqs=np.ascontiguousarray(qs), alphat=alphat, rsmask=rsmask)
        per_core.append(m)
    return per_core


def kernel(L0_attn_x, L0_attn_Wq, L0_attn_Wk, L0_attn_Wv, L0_attn_Wo,
           L0_ffn_weight, L0_ffn_bias, L0_ffn_hadamard_matrix):
    in_maps = _host_prep(L0_attn_x, L0_attn_Wq, L0_attn_Wk, L0_attn_Wv,
                         L0_attn_Wo, L0_ffn_weight, L0_ffn_bias,
                         L0_ffn_hadamard_matrix)
    nc = build_nc()
    res = run_bass_kernel_spmd(nc, in_maps, core_ids=list(range(8)))
    out = np.empty((4, 2048, 4096), np.float32)
    for core in range(8):
        b, half = core // 2, core % 2
        out[b, 1024 * half:1024 * (half + 1)] = res.results[core]["y"]
    return out
